# version 01
# baseline (speedup 1.0000x reference)
import numpy as np
import ml_dtypes
import concourse.bass as bass
import concourse.mybir as mybir
from concourse.bass_utils import run_bass_kernel_spmd

F32 = mybir.dt.float32
BF16 = mybir.dt.bfloat16
I16 = mybir.dt.int16
AF = mybir.ActivationFunctionType
ALU = mybir.AluOpType
AX = mybir.AxisListType
NPBF = ml_dtypes.bfloat16

BC, L, D = 2, 2048, 1024
H, DKH, DVH = 4, 192, 384
KD, VD, DI, DH = 768, 1536, 2048, 4096
C = 128
G1 = 1024
NG = BC * L // G1
PK = 2 * KD + 2 * VD + 8
BIG = 1.0e30

SC_KN, SC_QN, SC_QG, SC_RK, SC_KR, SC_RV, SC_GT = 0, KD, 2 * KD, 3 * KD, 4 * KD, 5 * KD, 5 * KD + VD
SCW = 5 * KD + 2 * VD

CL_D, CL_QB, CL_WQB, CL_SW1, CL_ISW1, CL_SW2, CL_ISW2, CL_ZERO, CL_T0, CL_T1 = range(10)


def pieces(c0, c1, width=512):
    out = []
    c = c0
    while c < c1:
        b = c // width
        hi = min(c1, (b + 1) * width)
        out.append((b, c - b * width, hi - c, c))
        c = hi
    return out


def head_slices(h, dk=DKH):
    out = []
    a, b = dk * h, dk * (h + 1)
    while a < b:
        t = a // 128
        hi = min(b, (t + 1) * 128)
        out.append((t, a - t * 128, hi - t * 128))
        a = hi
    return out


class P:
    ENGINES = ("sync", "gpsimd", "tensor", "vector", "scalar")

    def __init__(self, nc):
        self.nc = nc
        self.ev = {e: [] for e in self.ENGINES}
        self.cnt = {}
        self.lw = {e: {} for e in self.ENGINES}
        self.sems = {}
        for s in ("dS", "dG", "dA", "eT", "eV", "eS"):
            self.sems[s] = nc.alloc_semaphore(name=s)
            self.cnt[s] = 0

    AUTO_RAW = {"vector": "eV", "scalar": "eS"}

    def op(self, eng, fn, inc=None, amt=None):
        own = self.AUTO_RAW.get(eng)
        if inc in ("dS", "dG", "dA"):
            own = None
        if own is not None and eng != "gpsimd":
            prev = self.cnt[own]
            self.wait(eng, own, prev)
            if inc is None or inc == own:
                inc, amt = own, 1
            else:
                self.cnt[own] += 1
                ownv = self.cnt[own]
                sem_o = self.sems[own]
                self.cnt[inc] += amt if amt else (16 if inc in ("dS", "dG", "dA") else 1)
                sem_i = self.sems[inc]
                amt_i = amt if amt else (16 if inc in ("dS", "dG", "dA") else 1)

                def wrapped2(e, fn=fn):
                    ins = fn(e)
                    ins.then_inc(sem_i, amt_i)
                    ins.then_inc(sem_o, 1)
                self.ev[eng].append(wrapped2)
                return self.cnt[inc]
        if inc is None:
            self.ev[eng].append(fn)
            return None
        if amt is None:
            amt = 16 if inc in ("dS", "dG", "dA") else 1
        self.cnt[inc] += amt
        sem = self.sems[inc]

        def wrapped(e, fn=fn, sem=sem, amt=amt):
            fn(e).then_inc(sem, amt)
        self.ev[eng].append(wrapped)
        return self.cnt[inc]

    def wait(self, eng, semname, val):
        if not val:
            return
        if self.lw[eng].get(semname, 0) >= val:
            return
        self.lw[eng][semname] = val
        sem = self.sems[semname]
        self.ev[eng].append(lambda e, sem=sem, val=val: e.wait_ge(sem, val))

    def join_all(self):
        m = dict(self.cnt)
        for eng in self.ENGINES:
            for s, v in m.items():
                self.wait(eng, s, v)

    def replay(self, block):
        for name in self.ENGINES:
            evs = self.ev[name]

            def body(e, evs=evs):
                for fn in evs:
                    fn(e)
            getattr(block, name)(body)


def build_program(p0_only=False, p1_groups=NG, p2_groups=NG, p1_stage='e', d_chunks=8, dbg_out=False):
    nc = bass.Bass("TRN2", target_bir_lowering=False,
                   detect_race_conditions=False)

    ein = {}
    for name, shape in [
            ("x", (BC, L, D)), ("W_in", (D, DI)),
            ("conv_x_w", (D, 1, 4)), ("conv_x_b", (D,)),
            ("conv_z_w", (D, 1, 4)), ("conv_z_b", (D,)),
            ("W_q", (D, KD)), ("W_k", (D, KD)), ("W_v", (D, VD)),
            ("W_g", (D, VD)), ("W_gk", (D, H)), ("W_b", (D, H)),
            ("b_b", (H,)), ("A_log", (H,)), ("dt_bias", (H,)),
            ("g_norm_w", (DVH,)), ("W_so", (VD, D)), ("W_out", (DI, D)),
            ("w1", (DH, D)), ("conv_w", (DH, 1, 3)), ("conv_b", (DH,)),
            ("w2", (D, DH))]:
        ein[name] = nc.dram_tensor(name, list(shape), F32, kind="ExternalInput").ap()
    out_d = nc.dram_tensor("out", [BC, L, D], F32, kind="ExternalOutput").ap()

    def scr(name, shape, dt):
        kind = "ExternalOutput" if dbg_out else "Internal"
        return nc.dram_tensor(name, list(shape), dt, kind=kind).ap()

    win_sc = scr("win_sc", (D, DI), BF16)
    wpk_sc = scr("wpk_sc", (D, PK), BF16)
    wso_sc = scr("wso_sc", (VD, D), BF16)
    wout_sc = scr("wout_sc", (DI, D), BF16)
    t1_sc = scr("t1_sc", (DH, D), BF16)
    t2_sc = scr("t2_sc", (D, DH), BF16)
    t1T_sc = scr("t1T_sc", (D, DH), BF16)
    t2T_sc = scr("t2T_sc", (DH, D), BF16)
    dgf_sc = scr("dgf_sc", (96 * 128, 128), BF16)
    xn1_sc = scr("xn1_sc", (BC, L + 256, D), BF16)
    scC = scr("scC", (NG, G1, SCW), BF16)
    zs_sc = scr("zs_sc", (NG, 8, 128, G1), BF16)
    og_sc = scr("og_sc", (NG, G1, VD), BF16)
    x2_sc = scr("x2_sc", (BC, L, D), F32)
    xq1_sc = scr("xq1_sc", (BC, L + 128, D), BF16)
    hs_sc = scr("hs_sc", (BC, DH, L), BF16)
    hq_sc = scr("hq_sc", (BC, L, DH), BF16)

    eye = np.eye(128, dtype=np.float32)
    dm32 = np.zeros((128, 128), np.float32)
    for i in range(4):
        dm32[i * 32:(i + 1) * 32, i * 32:(i + 1) * 32] = 1.0
    consts_np = {
        "c_idbf": eye.astype(NPBF),
        "c_idf": eye,
        "c_triu": np.triu(np.ones((128, 128), np.float32)),
        "c_trils": np.tril(np.ones((128, 128), np.float32), -1),
        "c_ones": np.ones((128, 128), np.float32),
        "c_ltm": np.where(np.tril(np.ones((128, 128), np.float32), -1) > 0, 0.0, -BIG).astype(np.float32),
        "c_utm": np.where(np.triu(np.ones((128, 128), np.float32), 0) > 0, 0.0, -BIG).astype(np.float32),
        "c_dm": dm32,
        "c_nom": (dm32 - 1.0).astype(np.float32),
    }
    sel4 = np.zeros((4, 512), np.float32)
    for i in range(4):
        sel4[i, i * 128:(i + 1) * 128] = 1.0
    consts_np["c_sel"] = sel4
    const_t = {k: nc.inline_tensor(v, name=k) for k, v in consts_np.items()}

    p = P(nc)
    s = {}

    def alloc(name, shape, dt):
        s[name] = nc.alloc_sbuf_tensor(name, list(shape), dt)
        return s[name]

    for nm, sh, dt in [
            ("idbf", (128, 128), BF16), ("idf", (128, 128), F32),
            ("triu", (128, 128), F32), ("trils", (128, 128), F32),
            ("ones", (128, 128), F32), ("ltm", (128, 128), F32),
            ("utm", (128, 128), F32), ("dm", (128, 128), F32),
            ("nom", (128, 128), F32),
            ("cols", (128, 16), F32), ("i16c", (128, 2), I16),
            ("negA", (128, H), F32), ("dtb", (128, H), F32),
            ("bbt", (128, H), F32), ("gnw", (128, DVH), F32),
            ("cxb", (128, 16), F32), ("fcb", (128, 32), F32),
            ("cw", (128, 64), F32), ("fcw", (128, 96), F32),
            ("sel", (4, 512), F32),
            ("dga", (128, 64 * 128), BF16)]:
        alloc(nm, sh, dt)

    resident_end = nc.sbuf_base
    arena0 = (resident_end + 31) & ~31
    cur = [arena0]

    def place(name, shape, dt):
        nbytes = int(np.prod(shape[1:])) * mybir.dt.size(dt)
        off = (cur[0] + 31) & ~31
        cur[0] = off + nbytes
        s[name] = nc.alloc_sbuf_tensor_at(name, list(shape), dt, offset=off)
        return s[name]

    place("xw", (128, 2, D), F32)
    place("xn", (128, 2, D), BF16)
    place("xnT", (128, 8, 1152), BF16)
    place("xz", (128, 2, 1027), BF16)
    place("xs", (128, 8, G1), BF16)
    place("wp", (128, 16384), BF16)
    place("kqT", (128, 18, 128), BF16)
    place("kqtok", (128, 5 * KD), BF16)
    place("vt", (128, VD), BF16)
    place("gate", (128, 2, VD), BF16)
    place("rhsv", (128, H, DVH), BF16)
    place("rhsk", (128, H, DKH), BF16)
    place("kr", (128, H, DKH), BF16)
    place("q2", (128, 512), F32)
    place("qkc", (128, 24), F32)
    place("gcols", (128, 8, 28), F32)
    place("cT", (4, 8, 128), F32)
    place("R", (128, 512), F32)
    place("f1", (128, 512), F32)
    for i in range(11):
        place(f"db{i}", (128, 512), BF16)
    place("W1", (128, H, DVH), BF16)
    place("Kt", (128, H, DKH), BF16)
    place("P2T", (128, 2, DKH), BF16)
    place("QeT", (128, 2, 128), BF16)
    place("S1", (128, H, DVH), F32)
    place("S2", (64, H, DVH), F32)
    place("S1b", (128, H, DVH), BF16)
    place("S2b", (64, H, DVH), BF16)
    place("og", (128, VD), BF16)
    place("att", (128, D), F32)
    place("lnc", (128, 16), F32)
    p1_end = cur[0]
    cur[0] = nc.lookup_mloc(s["xnT"]).addr
    place("ogT", (128, 12, 512), BF16)
    place("ybf", (128, 8, 512), BF16)
    place("zsE", (128, 8, 512), BF16)
    assert cur[0] <= nc.lookup_mloc(s["wp"]).addr, "stage-E overlay overflow"

    cur[0] = arena0
    place("xqT", (128, 8, 1152), BF16)
    place("hh", (128, 2, 1027), BF16)
    place("hst", (128, 1, DH), BF16)
    place("hqi", (128, DH), BF16)
    place("i16t", (128, 1024), I16)
    place("qsc", (128, 2048), BF16)
    place("hqT", (128, 32, 256), BF16)
    place("dgf", (128, 6, 128), BF16)
    place("t1T", (128, 2, 8, 1024), BF16)
    place("t2T", (128, 32, 1024), BF16)
    place("p2x", (128, 2, D), F32)
    place("p2n", (128, D), F32)
    place("p2q", (128, D), BF16)
    place("i16q", (128, D), I16)
    place("y2", (128, D), F32)
    place("p2l", (128, 24), F32)
    p2_end = cur[0]

    cur[0] = arena0
    place("wprep", (128, PK), BF16)
    place("wf32", (128, 1024), F32)
    place("wi16", (128, 1024), I16)
    place("wbf", (128, 1024), BF16)
    place("wcol", (128, 4), F32)

    assert max(p1_end, p2_end) < nc.sbuf_top - 64, \
        f"SBUF overflow {max(p1_end, p2_end)} vs {nc.sbuf_top}"

    ps = [nc.alloc_psum_tensor(f"pb{i}", [128, 512], F32) for i in range(6)]
    pS = nc.alloc_psum_tensor("pS", [128, 512], F32)
    pT = nc.alloc_psum_tensor("pT", [128, 1024], BF16)

    def mm(out, lhsT, rhs, start, stop):
        return lambda e: e.matmul(out, lhsT, rhs, start=start, stop=stop)

    def b3(tile128, nrep=4, w=128):
        return tile128[:, :].unsqueeze(1).to_broadcast([128, nrep, w])

    def colb(col_ap, w=128):
        n = col_ap.shape[1]
        return col_ap.unsqueeze(2).to_broadcast([128, n, w])

    def v3(tile, n=4, w=128):
        return tile.rearrange("p (a b) -> p a b", a=n)

    for dst, key in [("idbf", "c_idbf"), ("idf", "c_idf"), ("triu", "c_triu"),
                     ("trils", "c_trils"), ("ones", "c_ones"), ("ltm", "c_ltm"),
                     ("utm", "c_utm"), ("dm", "c_dm"), ("nom", "c_nom"),
                     ("sel", "c_sel")]:
        p.op("sync", lambda e, d=s[dst], t=const_t[key]: e.dma_start(out=d[:, :], in_=t.ap()[:, :]), "dS")
    p.op("sync", lambda e: e.dma_start(out=s["negA"][:, :], in_=ein["A_log"].partition_broadcast(128)), "dS")
    p.op("sync", lambda e: e.dma_start(out=s["dtb"][:, :], in_=ein["dt_bias"].partition_broadcast(128)), "dS")
    p.op("sync", lambda e: e.dma_start(out=s["bbt"][:, :], in_=ein["b_b"].partition_broadcast(128)), "dS")
    p.op("sync", lambda e: e.dma_start(out=s["gnw"][:, :], in_=ein["g_norm_w"].partition_broadcast(128)), "dS")
    p.op("sync", lambda e: e.dma_start(out=s["cxb"][:, 0:8], in_=ein["conv_x_b"].rearrange("(t p) -> p t", p=128), allow_slow_non_contiguous=True), "dS")
    p.op("sync", lambda e: e.dma_start(out=s["cxb"][:, 8:16], in_=ein["conv_z_b"].rearrange("(t p) -> p t", p=128), allow_slow_non_contiguous=True), "dS")
    p.op("sync", lambda e: e.dma_start(out=s["fcb"][:, :], in_=ein["conv_b"].rearrange("(t p) -> p t", p=128), allow_slow_non_contiguous=True), "dS")
    p.op("sync", lambda e: e.dma_start(out=s["cw"][:, 0:32].rearrange("p (t k) -> p t k", k=4), in_=ein["conv_x_w"].rearrange("(t p) o k -> p t (o k)", p=128), allow_slow_non_contiguous=True), "dS")
    p.op("sync", lambda e: e.dma_start(out=s["cw"][:, 32:64].rearrange("p (t k) -> p t k", k=4), in_=ein["conv_z_w"].rearrange("(t p) o k -> p t (o k)", p=128), allow_slow_non_contiguous=True), "dS")
    p.op("sync", lambda e: e.dma_start(out=s["fcw"][:, :].rearrange("p (t k) -> p t k", k=3), in_=ein["conv_w"].rearrange("(t p) o k -> p t (o k)", p=128), allow_slow_non_contiguous=True), "dS")
    d0 = p.cnt["dS"]
    p.wait("vector", "dS", d0)
    p.wait("scalar", "dS", d0)

    p.op("vector", lambda e: e.memset(s["cols"][:, CL_T0:CL_T0 + 1], 0.7))
    p.op("vector", lambda e: e.tensor_copy(s["i16c"][:, 0:1], s["cols"][:, CL_T0:CL_T0 + 1]))
    p.op("vector", lambda e: e.tensor_copy(s["cols"][:, CL_D:CL_D + 1], s["i16c"][:, 0:1]))
    p.op("vector", lambda e: e.tensor_scalar(s["cols"][:, CL_QB:CL_QB + 1], s["cols"][:, CL_D:CL_D + 1], -0.5, 128.5, ALU.mult, ALU.add))
    p.op("vector", lambda e: e.tensor_scalar(s["cols"][:, CL_WQB:CL_WQB + 1], s["cols"][:, CL_D:CL_D + 1], -0.5, 4.5, ALU.mult, ALU.add))
    p.op("vector", lambda e: e.memset(s["cols"][:, CL_ZERO:CL_ZERO + 1], 0.0))
    p.op("scalar", lambda e: e.activation(s["negA"][:, :], s["negA"][:, :], AF.Exp), "eS")
    p.wait("vector", "eS", p.cnt["eS"])
    p.op("vector", lambda e: e.tensor_scalar_mul(s["negA"][:, :], s["negA"][:, :], -1.0))

    for idx in range(64):
        p.op("vector", lambda e, idx=idx: e.tensor_tensor(
            s["dga"][:, idx * 128:(idx + 1) * 128], s["idf"][:, :],
            s["cw"][:, idx:idx + 1].to_broadcast([128, 128]), ALU.mult))
    for idx in range(96):
        ev = p.op("vector", lambda e, idx=idx: e.tensor_tensor(
            s["dgf"][:, 0, :], s["idf"][:, :],
            s["fcw"][:, idx:idx + 1].to_broadcast([128, 128]), ALU.mult), "eV")
        p.wait("sync", "eV", ev)
        ds = p.op("sync", lambda e, idx=idx: e.dma_start(
            out=dgf_sc[idx * 128:(idx + 1) * 128, :], in_=s["dgf"][:, 0, :]), "dS")
        p.wait("vector", "dS", ds)

    ev = p.op("vector", lambda e: e.memset(s["xn"][:, :, :], 0.0), "eV")
    p.wait("sync", "eV", ev)
    for b in range(BC):
        for r0 in range(0, L + 256, 128):
            p.op("sync", lambda e, b=b, r0=r0: e.dma_start(out=xn1_sc[b, r0:r0 + 128, :], in_=s["xn"][:, 0, :]), "dS")
        for r0 in range(0, L + 128, 128):
            p.op("sync", lambda e, b=b, r0=r0: e.dma_start(out=xq1_sc[b, r0:r0 + 128, :], in_=s["xn"][:, 0, :]), "dS")

    d_zero = p.cnt["dS"]
    p.wait("gpsimd", "dS", d_zero)

    def cast_rows(dst_sc, src_ap, rows, cols):
        for r0 in range(0, rows, 128):
            dg = p.op("gpsimd", lambda e, r0=r0: e.dma_start(
                out=s["wprep"][:, 0:cols], in_=src_ap[r0:r0 + 128, 0:cols]), "dG")
            p.wait("gpsimd", "dG", dg)
            p.op("gpsimd", lambda e, r0=r0: e.dma_start(
                out=dst_sc[r0:r0 + 128, 0:cols], in_=s["wprep"][:, 0:cols]), "dG")
    cast_rows(win_sc, ein["W_in"], D, DI)
    segs = [("W_q", 0, KD), ("W_k", KD, KD), ("W_v", 2 * KD, VD),
            ("W_g", 2 * KD + VD, VD), ("W_gk", 2 * KD + 2 * VD, H),
            ("W_b", 2 * KD + 2 * VD + H, H)]
    for r0 in range(0, D, 128):
        dg = None
        for nm, c0, cn in segs:
            dg = p.op("gpsimd", lambda e, r0=r0, nm=nm, c0=c0, cn=cn: e.dma_start(
                out=s["wprep"][:, c0:c0 + cn], in_=ein[nm][r0:r0 + 128, 0:cn]), "dG")
        p.wait("gpsimd", "dG", dg)
        p.op("gpsimd", lambda e, r0=r0: e.dma_start(
            out=wpk_sc[r0:r0 + 128, :], in_=s["wprep"][:, :]), "dG")
    cast_rows(wso_sc, ein["W_so"], VD, D)
    cast_rows(wout_sc, ein["W_out"], DI, D)

    for wname, tdst, rows, cols, CLS, CLI in [
            ("w1", t1_sc, DH, D, CL_SW1, CL_ISW1),
            ("w2", t2_sc, D, DH, CL_SW2, CL_ISW2)]:
        src = ein[wname]
        ev = p.op("vector", lambda e: e.memset(s["wcol"][:, 0:1], 0.0), "eV")
        tiles = [(r0, c0) for r0 in range(0, rows, 128) for c0 in range(0, cols, 1024)]
        for (r0, c0) in tiles:
            p.wait("sync", "eV", p.cnt["eV"])
            ds = p.op("sync", lambda e, r0=r0, c0=c0, src=src: e.dma_start(
                out=s["wf32"][:, :], in_=src[r0:r0 + 128, c0:c0 + 1024]), "dS")
            p.wait("vector", "dS", ds)
            p.op("vector", lambda e: e.reduce_sum(
                s["wcol"][:, 1:2], s["wf32"][:, :], axis=AX.X, apply_absolute_value=True))
            ev = p.op("vector", lambda e: e.tensor_add(s["wcol"][:, 0:1], s["wcol"][:, 0:1], s["wcol"][:, 1:2]), "eV")
        p.wait("tensor", "eV", ev)
        et = p.op("tensor", mm(ps[0][:, 0:1], s["ones"][:, :], s["wcol"][:, 0:1], True, True), "eT")
        p.wait("vector", "eT", et)
        n = rows * cols
        p.op("vector", lambda e, n=n: e.tensor_scalar(
            s["cols"][:, CL_T1:CL_T1 + 1], ps[0][:, 0:1], 1.0 / n, 1e-5, ALU.mult, ALU.max))
        p.op("vector", lambda e, CLS=CLS: e.reciprocal(s["cols"][:, CLS:CLS + 1], s["cols"][:, CL_T1:CL_T1 + 1]))
        ev = p.op("vector", lambda e, CLI=CLI: e.tensor_copy(
            s["cols"][:, CLI:CLI + 1], s["cols"][:, CL_T1:CL_T1 + 1]), "eV")
        p.wait("scalar", "eV", ev)
        for (r0, c0) in tiles:
            p.wait("sync", "eV", p.cnt["eV"])
            ds = p.op("sync", lambda e, r0=r0, c0=c0, src=src: e.dma_start(
                out=s["wf32"][:, :], in_=src[r0:r0 + 128, c0:c0 + 1024]), "dS")
            p.wait("scalar", "dS", ds)
            es = p.op("scalar", lambda e, CLS=CLS: e.activation(
                s["wf32"][:, :], s["wf32"][:, :], AF.Identity,
                bias=s["cols"][:, CL_WQB:CL_WQB + 1], scale=s["cols"][:, CLS:CLS + 1]), "eS")
            p.wait("vector", "eS", es)
            p.op("vector", lambda e: e.tensor_copy(s["wi16"][:, :], s["wf32"][:, :]))
            p.op("vector", lambda e: e.tensor_scalar(
                s["wf32"][:, :], s["wi16"][:, :], -4.0, -1.0, ALU.add, ALU.max))
            ev = p.op("vector", lambda e: e.tensor_scalar(
                s["wbf"][:, :], s["wf32"][:, :], 1.0, None, ALU.min), "eV")
            p.wait("sync", "eV", ev)
            ds = p.op("sync", lambda e, r0=r0, c0=c0, tdst=tdst: e.dma_start(
                out=tdst[r0:r0 + 128, c0:c0 + 1024], in_=s["wbf"][:, :]), "dS")
            p.wait("vector", "dS", ds)

    for dt_i in range(8):
        ds = p.op("sync", lambda e, dt_i=dt_i: e.dma_start_transpose(
            out=s["wprep"][:, 0:DH], in_=t1_sc[:, dt_i * 128:(dt_i + 1) * 128]), "dS")
        p.wait("sync", "dS", ds)
        p.op("sync", lambda e, dt_i=dt_i: e.dma_start(
            out=t1T_sc[dt_i * 128:(dt_i + 1) * 128, :], in_=s["wprep"][:, 0:DH]), "dS")
    for ct in range(32):
        ds = p.op("sync", lambda e, ct=ct: e.dma_start_transpose(
            out=s["wprep"][:, 0:D], in_=t2_sc[:, ct * 128:(ct + 1) * 128]), "dS")
        p.wait("sync", "dS", ds)
        p.op("sync", lambda e, ct=ct: e.dma_start(
            out=t2T_sc[ct * 128:(ct + 1) * 128, :], in_=s["wprep"][:, 0:D]), "dS")

    p.join_all()

    if p0_only:
        p1_groups = 0
        p2_groups = 0
    def layernorm_tile(xap, rr, outap, lnc, extra_minmax=False, scratch=None):
        sq = scratch if scratch is not None else s["q2"]
        es = p.op("scalar", lambda e: e.activation(
            sq[0:rr, 0:512], xap[0:rr, 0:512], AF.Square, accum_out=lnc[0:rr, 0:1]), "eS")
        es = p.op("scalar", lambda e: e.activation(
            sq[0:rr, 0:512], xap[0:rr, 512:1024], AF.Square, accum_out=lnc[0:rr, 1:2]), "eS")
        p.wait("vector", "eS", es)
        p.op("vector", lambda e: e.reduce_sum(lnc[0:rr, 2:3], xap[0:rr, :], axis=AX.X))
        if extra_minmax:
            p.op("vector", lambda e: e.reduce_max(lnc[0:rr, 8:9], xap[0:rr, :], axis=AX.X))
            p.op("vector", lambda e: e.tensor_reduce(lnc[0:rr, 9:10], xap[0:rr, :], axis=AX.X, op=ALU.min))
        p.op("vector", lambda e: e.tensor_scalar_mul(lnc[0:rr, 3:4], lnc[0:rr, 2:3], 1.0 / D))
        p.op("vector", lambda e: e.tensor_add(lnc[0:rr, 4:5], lnc[0:rr, 0:1], lnc[0:rr, 1:2]))
        p.op("vector", lambda e: e.tensor_scalar_mul(lnc[0:rr, 4:5], lnc[0:rr, 4:5], 1.0 / D))
        p.op("vector", lambda e: e.tensor_mul(lnc[0:rr, 5:6], lnc[0:rr, 3:4], lnc[0:rr, 3:4]))
        p.op("vector", lambda e: e.tensor_sub(lnc[0:rr, 4:5], lnc[0:rr, 4:5], lnc[0:rr, 5:6]))
        ev = p.op("vector", lambda e: e.tensor_scalar(lnc[0:rr, 4:5], lnc[0:rr, 4:5], 1e-5, None, ALU.add), "eV")
        p.wait("scalar", "eV", ev)
        es = p.op("scalar", lambda e: e.activation(lnc[0:rr, 6:7], lnc[0:rr, 4:5], AF.Sqrt), "eS")
        p.wait("vector", "eS", es)
        p.op("vector", lambda e: e.reciprocal(lnc[0:rr, 7:8], lnc[0:rr, 6:7]))
        if outap is not None:
            ev = p.op("vector", lambda e: e.tensor_scalar(
                outap[0:rr, :], xap[0:rr, :], lnc[0:rr, 3:4], lnc[0:rr, 7:8],
                ALU.subtract, ALU.mult), "eV")
            return ev
        return p.cnt["eV"]

    for grp in range(p1_groups):
        p.join_all()
        b, gi = divmod(grp, 2)
        t0 = gi * G1
        nh = 0 if gi == 1 else 2
        ntok = G1 + nh

        last_w = None
        for tt in range((ntok + 127) // 128):
            r0 = t0 + tt * 128
            rr = min(128, ntok - tt * 128)
            buf = tt % 2
            p.wait("sync", "eV", p.lw["sync"].get("eV", 0))
            ds = p.op("sync", lambda e, b=b, r0=r0, rr=rr, buf=buf: e.dma_start(
                out=s["xw"][0:rr, buf, :], in_=ein["x"][b, r0:r0 + rr, :]), "dS")
            p.wait("scalar", "dS", ds)
            p.wait("vector", "dS", ds)
            ev = layernorm_tile(s["xw"][:, buf, :], rr, s["xn"][:, buf, :], s["lnc"])
            p.wait("sync", "eV", ev)
            last_w = p.op("sync", lambda e, b=b, r0=r0, rr=rr, buf=buf: e.dma_start(
                out=xn1_sc[b, 128 + r0:128 + r0 + rr, :], in_=s["xn"][0:rr, buf, :]), "dS")
            p.wait("vector", "dS", last_w)
        p.wait("sync", "dS", last_w)
        for dt_i in range(8):
            p.op("sync", lambda e, b=b, t0=t0, dt_i=dt_i: e.dma_start_transpose(
                out=s["xnT"][:, dt_i, :],
                in_=xn1_sc[b, 64 + t0:64 + t0 + 1152, dt_i * 128:(dt_i + 1) * 128]), "dS")
        d_xnT = p.cnt["dS"]

        if p1_stage < 'b':
            continue
        ev_zsw = []
        et_hist = {}
        for ct in range(16):
            par = ct % 2
            if ct >= 2:
                p.wait("gpsimd", "eT", et_hist[ct - 2])
            dgm = None
            for kt in range(8):
                dgm = p.op("gpsimd", lambda e, ct=ct, kt=kt, par=par: e.dma_start(
                    out=s["wp"][:, par * 1024 + kt * 128:par * 1024 + (kt + 1) * 128],
                    in_=win_sc[kt * 128:(kt + 1) * 128, ct * 128:(ct + 1) * 128]), "dG")
            p.wait("tensor", "dG", dgm)
            p.wait("tensor", "dS", d_xnT)
            for kt in range(8):
                p.op("tensor", mm(ps[0][:, :], s["wp"][:, par * 1024 + kt * 128:par * 1024 + (kt + 1) * 128],
                                  s["xnT"][:, kt, 63:575], kt == 0, kt == 7))
            for kt in range(8):
                p.op("tensor", mm(ps[1][:, :], s["wp"][:, par * 1024 + kt * 128:par * 1024 + (kt + 1) * 128],
                                  s["xnT"][:, kt, 575:1087], kt == 0, kt == 7))
            et = None
            for kt in range(8):
                et = p.op("tensor", mm(ps[2][:, 0:3], s["wp"][:, par * 1024 + kt * 128:par * 1024 + (kt + 1) * 128],
                                       s["xnT"][:, kt, 1087:1090], kt == 0, kt == 7),
                          "eT" if kt == 7 else None)
            p.wait("vector", "eT", et)
            p.op("vector", lambda e, par=par: e.tensor_copy(s["xz"][:, par, 0:512], ps[0][:, :]))
            p.op("vector", lambda e, par=par: e.tensor_copy(s["xz"][:, par, 512:1024], ps[1][:, :]))
            ev = p.op("vector", lambda e, par=par: e.tensor_copy(s["xz"][:, par, 1024:1027], ps[2][:, 0:3]), "eV")
            p.wait("tensor", "eV", ev)
            if p1_stage < 'b2':
                continue
            for half in range(2):
                et = None
                for j in range(4):
                    et = p.op("tensor", mm(
                        ps[3][:, :], s["dga"][:, (ct * 4 + j) * 128:(ct * 4 + j + 1) * 128],
                        s["xz"][:, par, half * 512 + j:half * 512 + j + 512], j == 0, j == 3),
                        "eT" if j == 3 else None)
                p.wait("scalar", "eT", et)
                if p1_stage < 'b3':
                    continue
                if ct < 8:
                    es = p.op("scalar", lambda e, ct=ct, half=half: e.activation(
                        s["xs"][:, ct, half * 512:(half + 1) * 512], ps[3][:, :], AF.Silu,
                        bias=s["cxb"][:, ct:ct + 1], scale=1.0), "eS")
                    p.wait("tensor", "eS", es)
                else:
                    es = p.op("scalar", lambda e, ct=ct, half=half: e.activation(
                        s["vt"][:, half * 512:(half + 1) * 512] if half == 0 else s["vt"][:, 512:1024],
                        ps[3][:, :], AF.Silu,
                        bias=s["cxb"][:, ct:ct + 1], scale=1.0), "eS")
                    p.wait("sync", "eS", es)
                    p.wait("tensor", "eS", es)
                    dw = p.op("sync", lambda e, grp=grp, ct=ct, half=half: e.dma_start(
                        out=zs_sc[grp, ct - 8, :, half * 512:(half + 1) * 512],
                        in_=s["vt"][:, half * 512:(half + 1) * 512]), "dS")
                    ev_zsw.append(dw)
                    p.wait("scalar", "dS", dw)
                    p.wait("vector", "dS", dw)
            et_hist[ct] = p.cnt["eT"]
        ev_xs = p.cnt["eS"]

        if p1_stage < 'c':
            continue
        dgm = None
        for kt in range(8):
            dgm = p.op("gpsimd", lambda e, kt=kt: e.dma_start(
                out=s["wp"][:, 2048 + kt * 8:2048 + (kt + 1) * 8],
                in_=wpk_sc[kt * 128:(kt + 1) * 128, 4608:4616]), "dG")
        p.wait("tensor", "dG", dgm)
        p.wait("tensor", "eS", ev_xs)
        for cc in range(8):
            et = None
            for kt in range(8):
                et = p.op("tensor", mm(ps[3][:, 0:8], s["xs"][:, kt, cc * 128:(cc + 1) * 128],
                                       s["wp"][:, 2048 + kt * 8:2048 + (kt + 1) * 8],
                                       kt == 0, kt == 7), "eT" if kt == 7 else None)
            p.wait("vector", "eT", et)
            if p1_stage == 'cA':
                continue
            gc = s["gcols"]
            qc = s["qkc"]
            p.op("vector", lambda e, cc=cc: e.tensor_add(qc[:, 0:4], ps[3][:, 0:4], s["dtb"][:, :]))
            ev = p.op("vector", lambda e, cc=cc: e.tensor_add(qc[:, 4:8], ps[3][:, 4:8], s["bbt"][:, :]), "eV")
            p.wait("scalar", "eV", ev)
            es = p.op("scalar", lambda e: e.activation(qc[:, 8:12], qc[:, 0:4], AF.Abs), "eS")
            es = p.op("scalar", lambda e: e.activation(qc[:, 8:12], qc[:, 8:12], AF.Exp, scale=-1.0), "eS")
            es = p.op("scalar", lambda e: e.activation(qc[:, 8:12], qc[:, 8:12], AF.Ln, bias=1.0), "eS")
            es = p.op("scalar", lambda e: e.activation(qc[:, 12:16], qc[:, 0:4], AF.Relu), "eS")
            es = p.op("scalar", lambda e: e.activation(qc[:, 16:20], qc[:, 4:8], AF.Sigmoid), "eS")
            p.wait("vector", "eS", es)
            p.op("vector", lambda e: e.tensor_add(qc[:, 8:12], qc[:, 8:12], qc[:, 12:16]))
            ev = p.op("vector", lambda e: e.tensor_mul(qc[:, 20:24], qc[:, 8:12], s["negA"][:, :]), "eV")
            p.op("vector", lambda e, cc=cc: e.tensor_copy(gc[:, cc, 0:4], qc[:, 16:20]))
            ev = p.op("vector", lambda e, cc=cc: e.tensor_scalar_mul(gc[:, cc, 24:28], qc[:, 16:20], -1.0), "eV")
            p.wait("tensor", "eV", ev)
            if p1_stage == 'cB':
                continue
            et = p.op("tensor", mm(ps[4][:, 0:4], s["triu"][:, :], qc[:, 20:24], True, True), "eT")
            et = p.op("tensor", mm(ps[4][:, 4:8], s["trils"][:, :], qc[:, 20:24], True, True), "eT")
            et = p.op("tensor", mm(ps[4][:, 8:12], s["ones"][:, :], qc[:, 20:24], True, True), "eT")
            p.wait("vector", "eT", et)
            ev = p.op("vector", lambda e, cc=cc: e.tensor_copy(gc[:, cc, 12:16], ps[4][:, 0:4]), "eV")
            p.wait("scalar", "eV", ev)
            es = p.op("scalar", lambda e, cc=cc: e.activation(gc[:, cc, 4:8], ps[4][:, 0:4], AF.Exp), "eS")
            es = p.op("scalar", lambda e, cc=cc: e.activation(gc[:, cc, 8:12], ps[4][:, 4:8], AF.Exp), "eS")
            es = p.op("scalar", lambda e, cc=cc: e.activation(gc[:, cc, 16:20], ps[4][:, 8:12], AF.Exp), "eS")
            p.wait("vector", "eS", es)
            ev = p.op("vector", lambda e, cc=cc: e.tensor_mul(gc[:, cc, 20:24], gc[:, cc, 0:4], gc[:, cc, 4:8]), "eV")
            p.wait("tensor", "eV", ev)
            if p1_stage == 'cC':
                continue
            et = p.op("tensor", lambda e, cc=cc: e.transpose(pS[0:4, 0:128], gc[:, cc, 12:16], s["idf"][:, :]), "eT")
            p.wait("vector", "eT", et)
            ev = p.op("vector", lambda e, cc=cc: e.tensor_copy(s["cT"][0:4, cc, :], pS[0:4, 0:128]), "eV")
            p.wait("tensor", "eV", ev)

        if p1_stage < 'c1':
            continue
        p.wait("gpsimd", "eT", p.cnt["eT"])
        dgm = None
        for kt in range(8):
            dgm = p.op("gpsimd", lambda e, kt=kt: e.dma_start(
                out=s["wp"][:, kt * 1536:(kt + 1) * 1536],
                in_=wpk_sc[kt * 128:(kt + 1) * 128, 0:1536]), "dG")
        p.wait("tensor", "dG", dgm)
        for cc in range(8):
            et = None
            for nb in range(3):
                for kt in range(8):
                    et = p.op("tensor", mm(ps[nb][:, :], s["xs"][:, kt, cc * 128:(cc + 1) * 128],
                                           s["wp"][:, kt * 1536 + nb * 512:kt * 1536 + (nb + 1) * 512],
                                           kt == 0, kt == 7), "eT" if kt == 7 else None)
            p.wait("scalar", "eT", et)
            es = None
            split_heads = []
            for hh in range(8):
                pl = pieces(hh * DKH, (hh + 1) * DKH)
                for i, (bk, lo, ln, ab) in enumerate(pl):
                    tgt = hh if i == 0 else 8 + hh
                    if i > 0:
                        split_heads.append(hh)
                    es = p.op("scalar", lambda e, bk=bk, lo=lo, ln=ln, tgt=tgt: e.activation(
                        s["q2"][:, 0:ln], ps[bk][:, lo:lo + ln], AF.Square,
                        accum_out=s["qkc"][:, tgt:tgt + 1]), "eS")
            p.wait("vector", "eS", es)
            for hh in split_heads:
                p.op("vector", lambda e, hh=hh: e.tensor_add(
                    s["qkc"][:, hh:hh + 1], s["qkc"][:, hh:hh + 1], s["qkc"][:, 8 + hh:9 + hh]))
            ev = p.op("vector", lambda e: e.tensor_scalar(
                s["qkc"][:, 0:8], s["qkc"][:, 0:8], 1e-6, None, ALU.add), "eV")
            p.wait("scalar", "eV", ev)
            es = p.op("scalar", lambda e: e.activation(s["qkc"][:, 16:24], s["qkc"][:, 0:8], AF.Sqrt), "eS")
            p.wait("vector", "eS", es)
            ev = p.op("vector", lambda e: e.reciprocal(s["qkc"][:, 8:16], s["qkc"][:, 16:24]), "eV")
            p.op("vector", lambda e, cc=cc: e.tensor_mul(s["qkc"][:, 16:20], s["qkc"][:, 8:12], s["gcols"][:, cc, 4:8]))
            ev = p.op("vector", lambda e, cc=cc: e.tensor_copy(s["qkc"][:, 20:24], s["qkc"][:, 12:16]), "eV")
            for hh in range(8):
                dst0 = (SC_KN + (hh - 4) * DKH) if hh >= 4 else (SC_QN + hh * DKH)
                rs_col = 12 + (hh - 4) if hh >= 4 else 8 + hh
                for (bk, lo, ln, ab) in pieces(hh * DKH, (hh + 1) * DKH):
                    off = ab - hh * DKH
                    ev = p.op("vector", lambda e, bk=bk, lo=lo, ln=ln, dst0=dst0, off=off, rs_col=rs_col: e.tensor_scalar(
                        s["kqtok"][:, dst0 + off:dst0 + off + ln], ps[bk][:, lo:lo + ln],
                        s["qkc"][:, rs_col:rs_col + 1], None, ALU.mult), "eV")
                    if hh < 4:
                        ev = p.op("vector", lambda e, bk=bk, lo=lo, ln=ln, hh=hh, off=off: e.tensor_scalar(
                            s["kqtok"][:, SC_QG + hh * DKH + off:SC_QG + hh * DKH + off + ln],
                            ps[bk][:, lo:lo + ln], s["qkc"][:, 16 + hh:17 + hh], None, ALU.mult), "eV")
            for hh in range(4):
                ev = p.op("vector", lambda e, hh=hh, cc=cc: e.tensor_scalar(
                    s["kqtok"][:, SC_RK + hh * DKH:SC_RK + (hh + 1) * DKH],
                    s["kqtok"][:, SC_KN + hh * DKH:SC_KN + (hh + 1) * DKH],
                    s["gcols"][:, cc, 20 + hh:21 + hh], None, ALU.mult), "eV")
                ev = p.op("vector", lambda e, hh=hh, cc=cc: e.tensor_scalar(
                    s["kqtok"][:, SC_KR + hh * DKH:SC_KR + (hh + 1) * DKH],
                    s["kqtok"][:, SC_KN + hh * DKH:SC_KN + (hh + 1) * DKH],
                    s["gcols"][:, cc, 8 + hh:9 + hh], None, ALU.mult), "eV")
            p.wait("sync", "eV", ev)
            ds = p.op("sync", lambda e, grp=grp, cc=cc: e.dma_start(
                out=scC[grp, cc * 128:(cc + 1) * 128, 0:5 * KD], in_=s["kqtok"][:, :]), "dS")
            p.wait("vector", "dS", ds)
            p.wait("tensor", "eV", ev)

        if p1_stage < 'c2':
            continue
        p.wait("gpsimd", "eT", p.cnt["eT"])
        dgm = None
        for kt in range(8):
            dgm = p.op("gpsimd", lambda e, kt=kt: e.dma_start(
                out=s["wp"][:, kt * 1536:(kt + 1) * 1536],
                in_=wpk_sc[kt * 128:(kt + 1) * 128, 1536:3072]), "dG")
        p.wait("tensor", "dG", dgm)
        for cc in range(8):
            et = None
            for nb in range(3):
                for kt in range(8):
                    et = p.op("tensor", mm(ps[nb][:, :], s["xs"][:, kt, cc * 128:(cc + 1) * 128],
                                           s["wp"][:, kt * 1536 + nb * 512:kt * 1536 + (nb + 1) * 512],
                                           kt == 0, kt == 7), "eT" if kt == 7 else None)
            p.wait("vector", "eT", et)
            ev = None
            for hh in range(4):
                for (bk, lo, ln, ab) in pieces(hh * DVH, (hh + 1) * DVH):
                    off = ab
                    ev = p.op("vector", lambda e, bk=bk, lo=lo, ln=ln, off=off, hh=hh, cc=cc: e.tensor_scalar(
                        s["vt"][:, off:off + ln], ps[bk][:, lo:lo + ln],
                        s["gcols"][:, cc, 0 + hh:1 + hh], None, ALU.mult), "eV")
            p.wait("sync", "eV", ev)
            ds = p.op("sync", lambda e, grp=grp, cc=cc: e.dma_start(
                out=scC[grp, cc * 128:(cc + 1) * 128, SC_RV:SC_RV + VD], in_=s["vt"][:, :]), "dS")
            p.wait("vector", "dS", ds)
            p.wait("scalar", "dS", ds)
            p.wait("tensor", "eV", ev)

        if p1_stage < 'c3':
            continue
        p.wait("gpsimd", "eT", p.cnt["eT"])
        dgm = None
        for kt in range(8):
            dgm = p.op("gpsimd", lambda e, kt=kt: e.dma_start(
                out=s["wp"][:, kt * 1536:(kt + 1) * 1536],
                in_=wpk_sc[kt * 128:(kt + 1) * 128, 3072:4608]), "dG")
        p.wait("tensor", "dG", dgm)
        for cc in range(8):
            et = None
            for nb in range(3):
                for kt in range(8):
                    et = p.op("tensor", mm(ps[nb][:, :], s["xs"][:, kt, cc * 128:(cc + 1) * 128],
                                           s["wp"][:, kt * 1536 + nb * 512:kt * 1536 + (nb + 1) * 512],
                                           kt == 0, kt == 7), "eT" if kt == 7 else None)
            p.wait("scalar", "eT", et)
            es = None
            for nb in range(3):
                es = p.op("scalar", lambda e, nb=nb: e.activation(
                    s["vt"][:, nb * 512:(nb + 1) * 512], ps[nb][:, :], AF.Silu), "eS")
            p.wait("vector", "eS", es)
            ev = p.op("vector", lambda e: e.tensor_tensor(
                v3(s["vt"][:, :], 4, DVH), v3(s["vt"][:, :], 4, DVH),
                s["gnw"][:, :].unsqueeze(1).to_broadcast([128, 4, DVH]), ALU.mult), "eV")
            p.wait("sync", "eV", ev)
            ds = p.op("sync", lambda e, grp=grp, cc=cc: e.dma_start(
                out=scC[grp, cc * 128:(cc + 1) * 128, SC_GT:SC_GT + VD], in_=s["vt"][:, :]), "dS")
            p.wait("vector", "dS", ds)
            p.wait("scalar", "dS", ds)
            p.wait("tensor", "eV", ev)
        d_stageC = p.cnt["dS"]

        if p1_stage < 'd':
            continue
        if gi == 0:
            ev = p.op("vector", lambda e: e.memset(s["S1"][:, :, :], 0.0), "eV")
            ev = p.op("vector", lambda e: e.memset(s["S2"][:, :, :], 0.0), "eV")
            ev = p.op("vector", lambda e: e.memset(s["S1b"][:, :, :], 0.0), "eV")
            ev = p.op("vector", lambda e: e.memset(s["S2b"][:, :, :], 0.0), "eV")
            p.wait("tensor", "eV", ev)

        for cc in range(d_chunks):
            p.wait("sync", "dS", d_stageC)
            p.wait("sync", "eT", p.cnt["eT"])
            p.wait("sync", "eV", p.cnt["eV"])
            for f in range(18):
                p.op("sync", lambda e, grp=grp, cc=cc, f=f: e.dma_start_transpose(
                    out=s["kqT"][:, f, :],
                    in_=scC[grp, cc * 128:(cc + 1) * 128, f * 128:(f + 1) * 128]), "dS")
            p.op("sync", lambda e, grp=grp, cc=cc: e.dma_start(
                out=s["rhsk"][:, :, :], in_=scC[grp, cc * 128:(cc + 1) * 128, SC_RK:SC_RK + KD]), "dS")
            p.op("sync", lambda e, grp=grp, cc=cc: e.dma_start(
                out=s["kr"][:, :, :], in_=scC[grp, cc * 128:(cc + 1) * 128, SC_KR:SC_KR + KD]), "dS")
            p.op("sync", lambda e, grp=grp, cc=cc: e.dma_start(
                out=s["rhsv"][:, :, :], in_=scC[grp, cc * 128:(cc + 1) * 128, SC_RV:SC_RV + VD]), "dS")
            dl = p.op("sync", lambda e, grp=grp, cc=cc: e.dma_start(
                out=s["gate"][:, cc % 2, :], in_=scC[grp, cc * 128:(cc + 1) * 128, SC_GT:SC_GT + VD]), "dS")
            p.wait("tensor", "dS", dl)

            for hh in range(4):
                sls = head_slices(hh)
                first = True
                for (tl, p0, p1) in sls:
                    p.op("tensor", mm(ps[0][:, hh * 128:(hh + 1) * 128],
                                      s["kqT"][p0:p1, tl, :], s["kqT"][p0:p1, tl, :],
                                      first, (tl, p0, p1) == sls[-1]))
                    first = False
                first = True
                for (tl, p0, p1) in sls:
                    p.op("tensor", mm(ps[1][:, hh * 128:(hh + 1) * 128],
                                      s["kqT"][p0:p1, tl, :], s["kqT"][p0:p1, 6 + tl, :],
                                      first, (tl, p0, p1) == sls[-1]))
                    first = False
            et = None
            for hh in range(4):
                et = p.op("tensor", mm(ps[2][:, hh * 128:(hh + 1) * 128],
                                       s["sel"][0:4, hh * 128:(hh + 1) * 128],
                                       s["cT"][0:4, cc, :], True, True),
                          "eT")
            p.wait("vector", "eT", et)
            ev = p.op("vector", lambda e, cc=cc: e.tensor_tensor(
                v3(s["R"][:, :]), v3(ps[2][:, :]),
                s["gcols"][:, cc, 12:16].unsqueeze(2).to_broadcast([128, 4, 128]),
                ALU.subtract), "eV")
            p.wait("scalar", "eV", ev)
            es = p.op("scalar", lambda e: e.activation(s["f1"][:, :], s["R"][:, :], AF.Identity, scale=-1.0), "eS")
            p.wait("vector", "eS", es)
            ev = p.op("vector", lambda e: e.tensor_tensor(
                v3(s["f1"][:, :]), v3(s["f1"][:, :]), b3(s["ltm"]), ALU.add), "eV")
            p.wait("scalar", "eV", ev)
            es = p.op("scalar", lambda e: e.activation(s["f1"][:, :], s["f1"][:, :], AF.Exp), "eS")
            p.wait("vector", "eS", es)
            p.op("vector", lambda e: e.tensor_mul(s["f1"][:, :], s["f1"][:, :], ps[0][:, :]))
            ev = p.op("vector", lambda e, cc=cc: e.tensor_tensor(
                v3(s["db0"][:, :]), v3(s["f1"][:, :]),
                s["gcols"][:, cc, 24:28].unsqueeze(2).to_broadcast([128, 4, 128]),
                ALU.mult), "eV")
            p.op("vector", lambda e: e.tensor_tensor(
                v3(s["f1"][:, :]), v3(s["R"][:, :]), b3(s["utm"]), ALU.add))
            ev2 = p.op("vector", lambda e: e.engine_nop(), "eV")
            p.wait("scalar", "eV", ev2)
            es = p.op("scalar", lambda e: e.activation(s["f1"][:, :], s["f1"][:, :], AF.Exp), "eS")
            p.wait("vector", "eS", es)
            ev = p.op("vector", lambda e: e.tensor_mul(s["f1"][:, :], s["f1"][:, :], ps[1][:, :]), "eV")
            ev = p.op("vector", lambda e: e.tensor_copy(s["db1"][:, :], s["f1"][:, :]), "eV")
            p.op("vector", lambda e: e.tensor_tensor(v3(s["db2"][:, :]), v3(s["db0"][:, :]), b3(s["dm"]), ALU.mult))
            ev = p.op("vector", lambda e: e.tensor_tensor(v3(s["db3"][:, :]), v3(s["db0"][:, :]), b3(s["nom"]), ALU.mult), "eV")
            p.wait("tensor", "eV", ev)
            et = None
            for hh in range(4):
                et = p.op("tensor", lambda e, hh=hh: e.transpose(
                    pT[:, hh * 128:(hh + 1) * 128], s["db2"][:, hh * 128:(hh + 1) * 128], s["idbf"][:, :]), "eT")
            p.wait("vector", "eT", et)
            ev = p.op("vector", lambda e: e.tensor_copy(s["db4"][:, :], pT[:, 0:512]), "eV")
            ev = p.op("vector", lambda e: e.tensor_tensor(v3(s["db5"][:, :]), v3(s["db4"][:, :]), b3(s["idbf"]), ALU.add), "eV")
            p.wait("tensor", "eV", ev)
            Ga, Ha, Pa, Gb, Hb, Pb = "db2", "db4", "db5", "db6", "db7", "db8"
            for lev in range(1, 5):
                et = None
                for hh in range(4):
                    et = p.op("tensor", mm(ps[3][:, hh * 128:(hh + 1) * 128],
                                           s[Ha][:, hh * 128:(hh + 1) * 128],
                                           s[Ga][:, hh * 128:(hh + 1) * 128], True, True), "eT")
                if lev < 4:
                    for hh in range(4):
                        et = p.op("tensor", mm(ps[4][:, hh * 128:(hh + 1) * 128],
                                               s[Ga][:, hh * 128:(hh + 1) * 128],
                                               s[Ha][:, hh * 128:(hh + 1) * 128], True, True), "eT")
                p.wait("vector", "eT", et)
                p.op("vector", lambda e, Gb=Gb: e.tensor_copy(s[Gb][:, :], ps[3][:, :]))
                if lev < 4:
                    p.op("vector", lambda e, Hb=Hb: e.tensor_copy(s[Hb][:, :], ps[4][:, :]))
                ev = p.op("vector", lambda e: e.engine_nop(), "eV")
                p.wait("tensor", "eV", ev)
                et = None
                for hh in range(4):
                    p.op("tensor", mm(ps[5][:, hh * 128:(hh + 1) * 128],
                                      s["idbf"][:, :], s[Pa][:, hh * 128:(hh + 1) * 128], True, False))
                    et = p.op("tensor", mm(ps[5][:, hh * 128:(hh + 1) * 128],
                                           s[Gb][:, hh * 128:(hh + 1) * 128],
                                           s[Pa][:, hh * 128:(hh + 1) * 128], False, True), "eT")
                p.wait("vector", "eT", et)
                ev = p.op("vector", lambda e, Pb=Pb: e.tensor_copy(s[Pb][:, :], ps[5][:, :]), "eV")
                p.wait("tensor", "eV", ev)
                Ga, Gb = Gb, Ga
                Ha, Hb = Hb, Ha
                Pa, Pb = Pb, Pa
            TdT = Pa
            et = None
            for hh in range(4):
                et = p.op("tensor", lambda e, hh=hh, TdT=TdT: e.transpose(
                    pT[:, hh * 128:(hh + 1) * 128], s[TdT][:, hh * 128:(hh + 1) * 128], s["idbf"][:, :]), "eT")
            p.wait("vector", "eT", et)
            ev = p.op("vector", lambda e: e.tensor_copy(s["db9"][:, :], pT[:, 0:512]), "eV")
            p.wait("tensor", "eV", ev)
            et = None
            for hh in range(4):
                et = p.op("tensor", mm(ps[3][:, hh * 128:(hh + 1) * 128],
                                       s["db3"][:, hh * 128:(hh + 1) * 128],
                                       s[TdT][:, hh * 128:(hh + 1) * 128], True, True), "eT")
            p.wait("vector", "eT", et)
            ev = p.op("vector", lambda e: e.tensor_copy(s["db10"][:, :], ps[3][:, :]), "eV")
            p.wait("tensor", "eV", ev)
            et = None
            for hh in range(4):
                et = p.op("tensor", lambda e, hh=hh: e.transpose(
                    pT[:, hh * 128:(hh + 1) * 128], s["db10"][:, hh * 128:(hh + 1) * 128], s["idbf"][:, :]), "eT")
            p.wait("vector", "eT", et)
            ev = p.op("vector", lambda e, Gb=Gb: e.tensor_copy(s[Gb][:, :], pT[:, 0:512]), "eV")
            p.wait("tensor", "eV", ev)
            et = None
            for hh in range(4):
                et = p.op("tensor", mm(ps[3][:, hh * 128:(hh + 1) * 128],
                                       s[Gb][:, hh * 128:(hh + 1) * 128],
                                       s["db10"][:, hh * 128:(hh + 1) * 128], True, True), "eT")
            p.wait("vector", "eT", et)
            ev = p.op("vector", lambda e, Hb=Hb: e.tensor_copy(s[Hb][:, :], ps[3][:, :]), "eV")
            p.wait("tensor", "eV", ev)
            et = None
            for hh in range(4):
                et = p.op("tensor", mm(ps[4][:, hh * 128:(hh + 1) * 128],
                                       s[Gb][:, hh * 128:(hh + 1) * 128],
                                       s[Hb][:, hh * 128:(hh + 1) * 128], True, True), "eT")
            p.wait("vector", "eT", et)
            p.op("vector", lambda e: e.tensor_tensor(v3(s["f1"][:, :]), b3(s["idbf"]), v3(s["db10"][:, :]), ALU.subtract))
            p.op("vector", lambda e, Hb=Hb: e.tensor_add(s["f1"][:, :], s["f1"][:, :], s[Hb][:, :]))
            ev = p.op("vector", lambda e, Pb=Pb: e.tensor_sub(s[Pb][:, :], s["f1"][:, :], ps[4][:, :]), "eV")
            p.wait("tensor", "eV", ev)
            et = None
            for hh in range(4):
                et = p.op("tensor", mm(ps[3][:, hh * 128:(hh + 1) * 128],
                                       s["db9"][:, hh * 128:(hh + 1) * 128],
                                       s[Pb][:, hh * 128:(hh + 1) * 128], True, True), "eT")
            p.wait("vector", "eT", et)
            ev = p.op("vector", lambda e: e.tensor_copy(s["db2"][:, :], ps[3][:, :]), "eV")
            p.wait("tensor", "eV", ev)
            ev_par = {}
            for hh in range(4):
                pa, pb = (0, 1) if hh % 2 == 0 else (4, 5)
                if hh >= 2:
                    p.wait("tensor", "eV", ev_par[hh - 2])
                et = p.op("tensor", mm(ps[pa][:, 0:DVH], s["db2"][:, hh * 128:(hh + 1) * 128],
                                       s["rhsv"][:, hh, :], True, True), "eT")
                et = p.op("tensor", mm(ps[pb][:, 0:DKH], s["db2"][:, hh * 128:(hh + 1) * 128],
                                       s["rhsk"][:, hh, :], True, True), "eT")
                p.wait("vector", "eT", et)
                p.op("vector", lambda e, hh=hh, pa=pa: e.tensor_copy(s["W1"][:, hh, :], ps[pa][:, 0:DVH]))
                ev = p.op("vector", lambda e, hh=hh, pb=pb: e.tensor_copy(s["Kt"][:, hh, :], ps[pb][:, 0:DKH]), "eV")
                ev_par[hh] = ev
            p.wait("tensor", "eV", ev)

            for hh in range(4):
                et = p.op("tensor", mm(ps[0][:, 0:128], s["Kt"][:, hh, 0:128],
                                       s["db1"][:, hh * 128:(hh + 1) * 128], True, True), "eT")
                et = p.op("tensor", mm(ps[1][0:64, 0:128], s["Kt"][:, hh, 128:192],
                                       s["db1"][:, hh * 128:(hh + 1) * 128], True, True), "eT")
                p.wait("vector", "eT", et)
                sls = head_slices(hh)
                (tl0, a0, a1) = sls[0]
                (tl1, b0, b1) = sls[1]
                n0 = a1 - a0
                p.op("vector", lambda e, tl0=tl0, a0=a0, n0=n0: e.tensor_sub(
                    s["QeT"][0:n0, 0, :], s["kqT"][a0:a0 + n0, 12 + tl0, :], ps[0][0:n0, 0:128]))
                if n0 < 128:
                    p.op("vector", lambda e, tl1=tl1, b0=b0, n0=n0: e.tensor_sub(
                        s["QeT"][n0:128, 0, :], s["kqT"][b0:b0 + 128 - n0, 12 + tl1, :], ps[0][n0:128, 0:128]))
                    ev = p.op("vector", lambda e, tl1=tl1, n0=n0: e.tensor_sub(
                        s["QeT"][0:64, 1, :], s["kqT"][128 - n0:64 + 128 - n0, 12 + tl1, :], ps[1][0:64, 0:128]), "eV")
                else:
                    ev = p.op("vector", lambda e, tl1=tl1, b0=b0: e.tensor_sub(
                        s["QeT"][0:64, 1, :], s["kqT"][b0:b0 + 64, 12 + tl1, :], ps[1][0:64, 0:128]), "eV")
                p.wait("tensor", "eV", ev)
                p.op("tensor", mm(ps[2][:, 0:DVH], s["QeT"][:, 0, :], s["S1b"][:, hh, :], True, False))
                p.op("tensor", mm(ps[2][:, 0:DVH], s["QeT"][0:64, 1, :], s["S2b"][0:64, hh, :], False, False))
                et = p.op("tensor", mm(ps[2][:, 0:DVH], s["db1"][:, hh * 128:(hh + 1) * 128],
                                       s["W1"][:, hh, :], False, True), "eT")
                p.wait("scalar", "eT", et)
                es = p.op("scalar", lambda e: e.activation(
                    s["q2"][:, 0:DVH], ps[2][:, 0:DVH], AF.Square, accum_out=s["lnc"][:, 10:11]), "eS")
                p.wait("vector", "eS", es)
                ev = p.op("vector", lambda e: e.tensor_scalar(
                    s["lnc"][:, 10:11], s["lnc"][:, 10:11], 1.0 / DVH, 1e-5, ALU.mult, ALU.add), "eV")
                p.wait("scalar", "eV", ev)
                es = p.op("scalar", lambda e: e.activation(
                    s["lnc"][:, 11:12], s["lnc"][:, 10:11], AF.Sqrt), "eS")
                p.wait("vector", "eS", es)
                p.op("vector", lambda e: e.reciprocal(s["lnc"][:, 12:13], s["lnc"][:, 11:12]))
                ev = p.op("vector", lambda e, hh=hh, cc=cc: e.scalar_tensor_tensor(
                    s["og"][:, hh * DVH:(hh + 1) * DVH], ps[2][:, 0:DVH],
                    s["lnc"][:, 12:13], s["gate"][:, cc % 2, hh * DVH:(hh + 1) * DVH],
                    ALU.mult, ALU.mult), "eV")
                p.wait("tensor", "eV", ev)
                et = p.op("tensor", mm(ps[0][:, 0:DKH], s["Kt"][:, hh, 0:128],
                                       s["kr"][:, hh, :], True, True), "eT")
                et = p.op("tensor", mm(ps[1][0:64, 0:DKH], s["Kt"][:, hh, 128:192],
                                       s["kr"][:, hh, :], True, True), "eT")
                p.wait("vector", "eT", et)
                p.op("vector", lambda e: e.tensor_scalar_mul(s["P2T"][:, 0, :], ps[0][:, 0:DKH], -1.0))
                ev = p.op("vector", lambda e: e.tensor_scalar_mul(s["P2T"][0:64, 1, :], ps[1][0:64, 0:DKH], -1.0), "eV")
                p.wait("tensor", "eV", ev)
                p.op("tensor", mm(pS[:, 0:DVH], s["kr"][:, hh, 0:128], s["W1"][:, hh, :], True, False))
                p.op("tensor", mm(pS[:, 0:DVH], s["P2T"][:, 0, 0:128], s["S1b"][:, hh, :], False, False))
                et = p.op("tensor", mm(pS[:, 0:DVH], s["P2T"][0:64, 1, 0:128], s["S2b"][0:64, hh, :], False, True), "eT")
                p.op("tensor", mm(ps[3][0:64, 0:DVH], s["kr"][:, hh, 128:192], s["W1"][:, hh, :], True, False))
                p.op("tensor", mm(ps[3][0:64, 0:DVH], s["P2T"][:, 0, 128:192], s["S1b"][:, hh, :], False, False))
                et = p.op("tensor", mm(ps[3][0:64, 0:DVH], s["P2T"][0:64, 1, 128:192], s["S2b"][0:64, hh, :], False, True), "eT")
                p.wait("vector", "eT", et)
                ev = p.op("vector", lambda e, hh=hh, cc=cc: e.scalar_tensor_tensor(
                    s["S1"][:, hh, :], s["S1"][:, hh, :], s["gcols"][:, cc, 16 + hh:17 + hh],
                    pS[:, 0:DVH], ALU.mult, ALU.add), "eV")
                ev = p.op("vector", lambda e, hh=hh, cc=cc: e.scalar_tensor_tensor(
                    s["S2"][0:64, hh, :], s["S2"][0:64, hh, :], s["gcols"][0:64, cc, 16 + hh:17 + hh],
                    ps[3][0:64, 0:DVH], ALU.mult, ALU.add), "eV")
                p.op("vector", lambda e, hh=hh: e.tensor_copy(s["S1b"][:, hh, :], s["S1"][:, hh, :]))
                ev = p.op("vector", lambda e, hh=hh: e.tensor_copy(s["S2b"][0:64, hh, :], s["S2"][0:64, hh, :]), "eV")
                p.wait("tensor", "eV", ev)
            p.wait("sync", "eV", ev)
            ds = p.op("sync", lambda e, grp=grp, cc=cc: e.dma_start(
                out=og_sc[grp, cc * 128:(cc + 1) * 128, :], in_=s["og"][:, :]), "dS")
            p.wait("vector", "dS", ds)
        d_og = p.cnt["dS"]

        if p1_stage < 'e':
            continue
        for half in range(2):
            p.wait("sync", "dS", d_og)
            p.wait("gpsimd", "eT", p.cnt["eT"])
            p.wait("sync", "eT", p.cnt["eT"])
            for f in range(12):
                p.op("sync", lambda e, grp=grp, half=half, f=f: e.dma_start_transpose(
                    out=s["ogT"][:, f, :],
                    in_=og_sc[grp, half * 512:(half + 1) * 512, f * 128:(f + 1) * 128]), "dS")
            d_ogT = p.cnt["dS"]
            for ct in range(8):
                p.op("sync", lambda e, grp=grp, ct=ct, half=half: e.dma_start(
                    out=s["zsE"][:, ct, :], in_=zs_sc[grp, ct, :, half * 512:(half + 1) * 512]), "dS")
            d_zsE = p.cnt["dS"]
            for dt_i in range(8):
                dgm = None
                for f in range(12):
                    dgm = p.op("gpsimd", lambda e, dt_i=dt_i, f=f: e.dma_start(
                        out=s["wp"][:, f * 128:(f + 1) * 128],
                        in_=wso_sc[f * 128:(f + 1) * 128, dt_i * 128:(dt_i + 1) * 128]), "dG")
                p.wait("tensor", "dG", dgm)
                p.wait("tensor", "dS", d_ogT)
                et = None
                for f in range(12):
                    et = p.op("tensor", mm(ps[0][:, :], s["wp"][:, f * 128:(f + 1) * 128],
                                           s["ogT"][:, f, :], f == 0, f == 11), "eT" if f == 11 else None)
                p.wait("vector", "eT", et)
                ev = p.op("vector", lambda e, dt_i=dt_i: e.tensor_copy(s["ybf"][:, dt_i, :], ps[0][:, :]), "eV")
                p.wait("gpsimd", "eV", ev)
                p.wait("tensor", "eV", ev)
            dgm = None
            for kt in range(16):
                dgm = p.op("gpsimd", lambda e, kt=kt: e.dma_start(
                    out=s["wp"][:, kt * 1024:(kt + 1) * 1024],
                    in_=wout_sc[kt * 128:(kt + 1) * 128, :]), "dG")
            p.wait("tensor", "dG", dgm)
            p.wait("tensor", "dS", d_zsE)
            for tc in range(4):
                et = None
                for nhf in range(2):
                    for kt in range(16):
                        lhs = s["ybf"][:, kt, tc * 128:(tc + 1) * 128] if kt < 8 \
                            else s["zsE"][:, kt - 8, tc * 128:(tc + 1) * 128]
                        et = p.op("tensor", mm(
                            ps[2 + nhf][:, :], lhs,
                            s["wp"][:, kt * 1024 + nhf * 512:kt * 1024 + (nhf + 1) * 512],
                            kt == 0, kt == 15), "eT" if kt == 15 else None)
                p.wait("scalar", "eT", et)
                es = p.op("scalar", lambda e: e.copy(s["att"][:, 0:512], ps[2][:, :]), "eS")
                es = p.op("scalar", lambda e: e.copy(s["att"][:, 512:1024], ps[3][:, :]), "eS")
                rbase = t0 + half * 512 + tc * 128
                ds = p.op("sync", lambda e, b=b, rbase=rbase: e.dma_start(
                    out=s["xw"][:, 0, :], in_=ein["x"][b, rbase:rbase + 128, :]), "dS")
                p.wait("vector", "eS", es)
                p.wait("scalar", "eS", es)
                layernorm_tile(s["att"][:, :], 128, None, s["lnc"])
                p.wait("vector", "dS", ds)
                p.op("vector", lambda e: e.tensor_scalar(
                    s["att"][:, :], s["att"][:, :],
                    s["lnc"][:, 3:4], s["lnc"][:, 7:8], ALU.subtract, ALU.mult))
                ev = p.op("vector", lambda e: e.tensor_add(
                    s["att"][:, :], s["att"][:, :], s["xw"][:, 0, :]), "eV")
                p.wait("sync", "eV", ev)
                ds = p.op("sync", lambda e, b=b, rbase=rbase: e.dma_start(
                    out=x2_sc[b, rbase:rbase + 128, :], in_=s["att"][:, :]), "dS")
                p.wait("scalar", "dS", ds)
                p.wait("tensor", "dS", ds)


    p.join_all()

    if p2_groups > 0:
        dg = None
        for cb in range(32):
            dg = p.op("gpsimd", lambda e, cb=cb: e.dma_start(
                out=s["t2T"][:, cb, :], in_=t2T_sc[cb * 128:(cb + 1) * 128, :]), "dG")
        p.wait("tensor", "dG", dg)

    for grp in range(p2_groups):
        p.join_all()
        b, gi = divmod(grp, 2)
        t0 = gi * G1
        nh = 0 if gi == 1 else 2
        ntok = G1 + nh

        for tt in range((ntok + 127) // 128):
            r0 = t0 + tt * 128
            rr = min(128, ntok - tt * 128)
            buf = tt % 2
            ds = p.op("sync", lambda e, b=b, r0=r0, rr=rr, buf=buf: e.dma_start(
                out=s["p2x"][0:rr, buf, :], in_=x2_sc[b, r0:r0 + rr, :]), "dS")
            p.wait("scalar", "dS", ds)
            p.wait("vector", "dS", ds)
            layernorm_tile(s["p2x"][:, buf, :], rr, None, s["p2l"], extra_minmax=True, scratch=s["p2n"])
            lc = s["p2l"]
            p.op("vector", lambda e, rr=rr: e.tensor_sub(lc[0:rr, 12:13], lc[0:rr, 8:9], lc[0:rr, 3:4]))
            p.op("vector", lambda e, rr=rr: e.tensor_sub(lc[0:rr, 13:14], lc[0:rr, 3:4], lc[0:rr, 9:10]))
            p.op("vector", lambda e, rr=rr: e.tensor_max(lc[0:rr, 12:13], lc[0:rr, 12:13], lc[0:rr, 13:14]))
            p.op("vector", lambda e, rr=rr: e.tensor_mul(lc[0:rr, 12:13], lc[0:rr, 12:13], lc[0:rr, 7:8]))
            p.op("vector", lambda e, rr=rr: e.tensor_scalar(lc[0:rr, 12:13], lc[0:rr, 12:13], 1e-5, None, ALU.max))
            p.op("vector", lambda e, rr=rr: e.reciprocal(lc[0:rr, 14:15], lc[0:rr, 12:13]))
            p.op("vector", lambda e, rr=rr: e.tensor_scalar_mul(lc[0:rr, 14:15], lc[0:rr, 14:15], 127.0))
            p.op("vector", lambda e, rr=rr: e.tensor_scalar_mul(lc[0:rr, 15:16], lc[0:rr, 12:13], 1.0 / 127.0))
            p.op("vector", lambda e, rr=rr: e.tensor_mul(lc[0:rr, 12:13], lc[0:rr, 7:8], lc[0:rr, 14:15]))
            p.op("vector", lambda e, rr=rr: e.tensor_mul(lc[0:rr, 13:14], lc[0:rr, 3:4], lc[0:rr, 12:13]))
            ev = p.op("vector", lambda e, rr=rr: e.tensor_tensor(
                lc[0:rr, 13:14], s["cols"][0:rr, CL_QB:CL_QB + 1], lc[0:rr, 13:14], ALU.subtract), "eV")
            p.wait("scalar", "eV", ev)
            es = p.op("scalar", lambda e, rr=rr, buf=buf: e.activation(
                s["p2n"][0:rr, :], s["p2x"][0:rr, buf, :], AF.Identity,
                bias=lc[0:rr, 13:14], scale=lc[0:rr, 12:13]), "eS")
            p.wait("vector", "eS", es)
            p.op("vector", lambda e, rr=rr: e.tensor_copy(s["i16q"][0:rr, :], s["p2n"][0:rr, :]))
            ev = p.op("vector", lambda e, rr=rr: e.tensor_scalar(
                s["p2q"][0:rr, :], s["i16q"][0:rr, :], -128.0, lc[0:rr, 15:16],
                ALU.add, ALU.mult), "eV")
            p.wait("sync", "eV", ev)
            ds = p.op("sync", lambda e, b=b, r0=r0, rr=rr: e.dma_start(
                out=xq1_sc[b, 64 + r0:64 + r0 + rr, :], in_=s["p2q"][0:rr, :]), "dS")
            p.wait("vector", "dS", ds)
            last_q = ds
        p.wait("sync", "dS", last_q)
        for dt_i in range(8):
            p.op("sync", lambda e, b=b, t0=t0, dt_i=dt_i: e.dma_start_transpose(
                out=s["xqT"][:, dt_i, :],
                in_=xq1_sc[b, t0:t0 + 1152, dt_i * 128:(dt_i + 1) * 128]), "dS")
        d_xqT = p.cnt["dS"]

        for oct in range(4):
            op_ = oct % 2
            if oct >= 2:
                p.wait("gpsimd", "eT", p.cnt["eT"])
            dgm = None
            for dt_i in range(8):
                dgm = p.op("gpsimd", lambda e, oct=oct, dt_i=dt_i, op_=op_: e.dma_start(
                    out=s["t1T"][:, op_, dt_i, :],
                    in_=t1T_sc[dt_i * 128:(dt_i + 1) * 128, oct * 1024:(oct + 1) * 1024]), "dG")
            p.wait("tensor", "dG", dgm)
            p.wait("tensor", "dS", d_xqT)
            for ctl in range(8):
                ct = oct * 8 + ctl
                cp = ct % 2
                if ctl >= 2 or oct > 0:
                    p.wait("sync", "eT", p.cnt["eT"])
                dd = None
                for j in range(3):
                    dd = p.op("sync", lambda e, ct=ct, j=j, cp=cp: e.dma_start(
                        out=s["dgf"][:, cp * 3 + j, :],
                        in_=dgf_sc[(ct * 3 + j) * 128:(ct * 3 + j + 1) * 128, :]), "dS")
                for dt_i in range(8):
                    p.op("tensor", mm(ps[0][:, :], s["t1T"][:, op_, dt_i, ct * 128 - oct * 1024:ct * 128 - oct * 1024 + 128],
                                      s["xqT"][:, dt_i, 63:575], dt_i == 0, dt_i == 7))
                for dt_i in range(8):
                    p.op("tensor", mm(ps[1][:, :], s["t1T"][:, op_, dt_i, ct * 128 - oct * 1024:ct * 128 - oct * 1024 + 128],
                                      s["xqT"][:, dt_i, 575:1087], dt_i == 0, dt_i == 7))
                et = None
                for dt_i in range(8):
                    et = p.op("tensor", mm(ps[2][:, 0:3], s["t1T"][:, op_, dt_i, ct * 128 - oct * 1024:ct * 128 - oct * 1024 + 128],
                                           s["xqT"][:, dt_i, 1087:1090], dt_i == 0, dt_i == 7),
                              "eT" if dt_i == 7 else None)
                p.wait("vector", "eT", et)
                p.op("vector", lambda e, cp=cp: e.tensor_copy(s["hh"][:, cp, 0:512], ps[0][:, :]))
                p.op("vector", lambda e, cp=cp: e.tensor_copy(s["hh"][:, cp, 512:1024], ps[1][:, :]))
                ev = p.op("vector", lambda e, cp=cp: e.tensor_copy(s["hh"][:, cp, 1024:1027], ps[2][:, 0:3]), "eV")
                p.wait("tensor", "eV", ev)
                p.wait("tensor", "dS", dd)
                for half in range(2):
                    et = None
                    for j in range(3):
                        et = p.op("tensor", mm(
                            ps[3][:, :], s["dgf"][:, cp * 3 + j, :],
                            s["hh"][:, cp, half * 512 + j:half * 512 + j + 512], j == 0, j == 2),
                            "eT" if j == 2 else None)
                    p.wait("scalar", "eT", et)
                    es = p.op("scalar", lambda e, ct=ct: e.activation(
                        s["p2q"][:, 0:512], ps[3][:, :], AF.Silu,
                        bias=s["fcb"][:, ct:ct + 1], scale=s["cols"][:, CL_ISW1:CL_ISW1 + 1]), "eS")
                    p.wait("sync", "eS", es)
                    p.wait("tensor", "eS", es)
                    ds = p.op("sync", lambda e, b=b, ct=ct, t0=t0, half=half: e.dma_start(
                        out=hs_sc[b, ct * 128:(ct + 1) * 128, t0 + half * 512:t0 + (half + 1) * 512],
                        in_=s["p2q"][:, 0:512]), "dS")
                    p.wait("scalar", "dS", ds)
        d_hs = p.cnt["dS"]

        for tt in range(8):
            bufh = 0
            p.wait("sync", "dS", d_hs)
            p.wait("sync", "eV", p.cnt["eV"])
            p.wait("sync", "eS", p.cnt["eS"])
            ds = p.op("sync", lambda e, b=b, t0=t0, tt=tt, bufh=bufh: e.dma_start_transpose(
                out=s["hst"][:, bufh, :], in_=hs_sc[b, :, t0 + tt * 128:t0 + (tt + 1) * 128]), "dS")
            p.wait("vector", "dS", ds)
            p.op("vector", lambda e, bufh=bufh: e.scalar_tensor_tensor(
                s["qsc"][:, 0:2048], s["hst"][:, bufh, 0:2048], -1.0,
                s["hst"][:, bufh, 0:2048], ALU.mult, ALU.max))
            p.op("vector", lambda e, bufh=bufh: e.scalar_tensor_tensor(
                s["hqi"][:, 0:2048], s["hst"][:, bufh, 2048:4096], -1.0,
                s["hst"][:, bufh, 2048:4096], ALU.mult, ALU.max))
            p.op("vector", lambda e: e.tensor_max(
                s["qsc"][:, 0:2048], s["qsc"][:, 0:2048], s["hqi"][:, 0:2048]))
            p.op("vector", lambda e: e.tensor_max(
                s["qsc"][:, 0:1024], s["qsc"][:, 0:1024], s["qsc"][:, 1024:2048]))
            p.op("vector", lambda e: e.tensor_max(
                s["qsc"][:, 0:512], s["qsc"][:, 0:512], s["qsc"][:, 512:1024]))
            lc = s["p2l"]
            p.op("vector", lambda e: e.reduce_max(lc[:, 0:1], s["qsc"][:, 0:512], axis=AX.X, apply_absolute_value=True))
            p.op("vector", lambda e: e.tensor_scalar(lc[:, 0:1], lc[:, 0:1], 1e-5, None, ALU.max))
            p.op("vector", lambda e: e.reciprocal(lc[:, 1:2], lc[:, 0:1]))
            p.op("vector", lambda e: e.tensor_scalar_mul(lc[:, 2:3], lc[:, 1:2], 127.0))
            p.op("vector", lambda e: e.tensor_scalar_mul(lc[:, 3:4], lc[:, 0:1], 1.0 / 127.0))
            ev = p.op("vector", lambda e, tt=tt: e.tensor_tensor(
                lc[:, 16 + tt:17 + tt], lc[:, 3:4], s["cols"][:, CL_ISW2:CL_ISW2 + 1], ALU.mult), "eV")
            p.wait("scalar", "eV", ev)
            for qq in range(4):
                if qq > 0:
                    p.wait("scalar", "eV", ev)
                es = p.op("scalar", lambda e, bufh=bufh, qq=qq: e.activation(
                    s["p2n"][:, :], s["hst"][:, bufh, qq * 1024:(qq + 1) * 1024], AF.Identity,
                    bias=s["cols"][:, CL_QB:CL_QB + 1], scale=lc[:, 2:3]), "eS")
                p.wait("vector", "eS", es)
                p.op("vector", lambda e: e.tensor_copy(s["i16t"][:, :], s["p2n"][:, :]))
                ev = p.op("vector", lambda e, qq=qq: e.tensor_scalar(
                    s["hqi"][:, qq * 1024:(qq + 1) * 1024], s["i16t"][:, :],
                    -128.0, None, ALU.add), "eV")
            p.wait("sync", "eV", ev)
            ds = p.op("sync", lambda e, b=b, t0=t0, tt=tt: e.dma_start(
                out=hq_sc[b, t0 + tt * 128:t0 + (tt + 1) * 128, :], in_=s["hqi"][:, :]), "dS")
            p.wait("vector", "dS", ds)
            p.wait("scalar", "dS", ds)
        d_hq = p.cnt["dS"]

        for tp in range(4):
            p.wait("scalar", "dS", d_hq)
            p.wait("scalar", "eT", p.cnt["eT"])
            da = None
            for cb in range(32):
                da = p.op("scalar", lambda e, b=b, t0=t0, tp=tp, cb=cb: e.dma_start_transpose(
                    out=s["hqT"][:, cb, :],
                    in_=hq_sc[b, t0 + tp * 256:t0 + (tp + 1) * 256, cb * 128:(cb + 1) * 128]), "dA")
            p.wait("tensor", "dA", da)
            et = None
            for two in range(2):
                for nhf in range(2):
                    for cb in range(32):
                        et = p.op("tensor", mm(
                            ps[two * 2 + nhf][:, :],
                            s["hqT"][:, cb, two * 128:(two + 1) * 128],
                            s["t2T"][:, cb, nhf * 512:(nhf + 1) * 512],
                            cb == 0, cb == 31), "eT" if cb == 31 else None)
            p.wait("vector", "eT", et)
            for two in range(2):
                tt = tp * 2 + two
                rbase = t0 + tt * 128
                p.op("vector", lambda e, two=two, tt=tt: e.tensor_scalar(
                    s["y2"][:, 0:512], ps[two * 2][:, :], s["p2l"][:, 16 + tt:17 + tt], None, ALU.mult))
                ev = p.op("vector", lambda e, two=two, tt=tt: e.tensor_scalar(
                    s["y2"][:, 512:1024], ps[two * 2 + 1][:, :], s["p2l"][:, 16 + tt:17 + tt], None, ALU.mult), "eV")
                ds = p.op("sync", lambda e, b=b, rbase=rbase: e.dma_start(
                    out=s["p2x"][:, 0, :], in_=x2_sc[b, rbase:rbase + 128, :]), "dS")
                p.wait("scalar", "eV", ev)
                p.wait("vector", "eV", ev)
                layernorm_tile(s["y2"][:, :], 128, None, s["p2l"], scratch=s["p2n"])
                p.wait("vector", "dS", ds)
                p.op("vector", lambda e: e.tensor_scalar(
                    s["y2"][:, :], s["y2"][:, :], s["p2l"][:, 3:4], s["p2l"][:, 7:8],
                    ALU.subtract, ALU.mult))
                ev = p.op("vector", lambda e: e.tensor_add(s["y2"][:, :], s["y2"][:, :], s["p2x"][:, 0, :]), "eV")
                p.wait("sync", "eV", ev)
                ds = p.op("sync", lambda e, b=b, rbase=rbase: e.dma_start(
                    out=out_d[b, rbase:rbase + 128, :], in_=s["y2"][:, :]), "dS")
                p.wait("vector", "dS", ds)
                p.wait("scalar", "dS", ds)
                p.wait("tensor", "dS", ds)

    p.wait("sync", "dS", p.cnt["dS"])
    p.wait("gpsimd", "dG", p.cnt["dG"])
    p.wait("scalar", "dA", p.cnt["dA"])

    with nc.Block() as block:
        p.replay(block)
    return nc


_NC_CACHE = [None]


def kernel(**inputs):
    if _NC_CACHE[0] is None:
        _NC_CACHE[0] = build_program()
    nc = _NC_CACHE[0]
    x = np.ascontiguousarray(np.asarray(inputs["L0_attn_x"], dtype=np.float32))
    wmap = {
        "W_in": "L0_attn_W_in", "conv_x_w": "L0_attn_conv_x_w",
        "conv_x_b": "L0_attn_conv_x_b", "conv_z_w": "L0_attn_conv_z_w",
        "conv_z_b": "L0_attn_conv_z_b", "W_q": "L0_attn_W_q",
        "W_k": "L0_attn_W_k", "W_v": "L0_attn_W_v", "W_g": "L0_attn_W_g",
        "W_gk": "L0_attn_W_gk", "W_b": "L0_attn_W_b", "b_b": "L0_attn_b_b",
        "A_log": "L0_attn_A_log", "dt_bias": "L0_attn_dt_bias",
        "g_norm_w": "L0_attn_g_norm_w", "W_so": "L0_attn_W_so",
        "W_out": "L0_attn_W_out", "w1": "L0_ffn_w1", "conv_w": "L0_ffn_conv_w",
        "conv_b": "L0_ffn_conv_b", "w2": "L0_ffn_w2",
    }
    weights = {k: np.ascontiguousarray(np.asarray(inputs[v], dtype=np.float32))
               for k, v in wmap.items()}
    in_maps = []
    for i in range(8):
        m = {"x": x[2 * i:2 * i + 2]}
        m.update(weights)
        in_maps.append(m)
    res = run_bass_kernel_spmd(nc, in_maps, core_ids=list(range(8)))
    out = np.concatenate([r["out"] for r in res.results], axis=0)
    return out.reshape(16, L, D).astype(np.float32)


if __name__ == "__main__":
    nc = build_program()
    print("built:", nc.n_instructions(), "instructions")


# version 02
# speedup vs baseline: 1.0304x; 1.0304x over previous
import numpy as np
import ml_dtypes
import concourse.bass as bass
import concourse.mybir as mybir
from concourse.bass_utils import run_bass_kernel_spmd

F32 = mybir.dt.float32
BF16 = mybir.dt.bfloat16
I16 = mybir.dt.int16
AF = mybir.ActivationFunctionType
ALU = mybir.AluOpType
AX = mybir.AxisListType
NPBF = ml_dtypes.bfloat16

BC, L, D = 2, 2048, 1024
H, DKH, DVH = 4, 192, 384
KD, VD, DI, DH = 768, 1536, 2048, 4096
C = 128
G1 = 1024
NG = BC * L // G1
PK = 2 * KD + 2 * VD + 8
BIG = 1.0e30

SC_KN, SC_QN, SC_QG, SC_RK, SC_KR, SC_RV, SC_GT = 0, KD, 2 * KD, 3 * KD, 4 * KD, 5 * KD, 5 * KD + VD
SCW = 5 * KD + 2 * VD

CL_D, CL_QB, CL_WQB, CL_SW1, CL_ISW1, CL_SW2, CL_ISW2, CL_ZERO, CL_T0, CL_T1 = range(10)


def pieces(c0, c1, width=512):
    out = []
    c = c0
    while c < c1:
        b = c // width
        hi = min(c1, (b + 1) * width)
        out.append((b, c - b * width, hi - c, c))
        c = hi
    return out


def head_slices(h, dk=DKH):
    out = []
    a, b = dk * h, dk * (h + 1)
    while a < b:
        t = a // 128
        hi = min(b, (t + 1) * 128)
        out.append((t, a - t * 128, hi - t * 128))
        a = hi
    return out


class P:
    ENGINES = ("sync", "gpsimd", "tensor", "vector", "scalar")

    def __init__(self, nc):
        self.nc = nc
        self.ev = {e: [] for e in self.ENGINES}
        self.cnt = {}
        self.lw = {e: {} for e in self.ENGINES}
        self.sems = {}
        for s in ("dS", "dG", "dA", "eT", "eV", "eS"):
            self.sems[s] = nc.alloc_semaphore(name=s)
            self.cnt[s] = 0

    AUTO_RAW = {"vector": "eV", "scalar": "eS"}

    def op(self, eng, fn, inc=None, amt=None):
        own = self.AUTO_RAW.get(eng)
        if inc in ("dS", "dG", "dA"):
            own = None
        if own is not None and eng != "gpsimd":
            prev = self.cnt[own]
            self.wait(eng, own, prev)
            if inc is None or inc == own:
                inc, amt = own, 1
            else:
                self.cnt[own] += 1
                ownv = self.cnt[own]
                sem_o = self.sems[own]
                self.cnt[inc] += amt if amt else (16 if inc in ("dS", "dG", "dA") else 1)
                sem_i = self.sems[inc]
                amt_i = amt if amt else (16 if inc in ("dS", "dG", "dA") else 1)

                def wrapped2(e, fn=fn):
                    ins = fn(e)
                    ins.then_inc(sem_i, amt_i)
                    ins.then_inc(sem_o, 1)
                self.ev[eng].append(wrapped2)
                return self.cnt[inc]
        if inc is None:
            self.ev[eng].append(fn)
            return None
        if amt is None:
            amt = 16 if inc in ("dS", "dG", "dA") else 1
        self.cnt[inc] += amt
        sem = self.sems[inc]

        def wrapped(e, fn=fn, sem=sem, amt=amt):
            fn(e).then_inc(sem, amt)
        self.ev[eng].append(wrapped)
        return self.cnt[inc]

    def wait(self, eng, semname, val):
        if not val:
            return
        if self.lw[eng].get(semname, 0) >= val:
            return
        self.lw[eng][semname] = val
        sem = self.sems[semname]
        self.ev[eng].append(lambda e, sem=sem, val=val: e.wait_ge(sem, val))

    def join_all(self):
        m = dict(self.cnt)
        for eng in self.ENGINES:
            for s, v in m.items():
                self.wait(eng, s, v)

    def replay(self, block):
        for name in self.ENGINES:
            evs = self.ev[name]

            def body(e, evs=evs):
                for fn in evs:
                    fn(e)
            getattr(block, name)(body)


def build_program(p0_only=False, p1_groups=NG, p2_groups=NG, p1_stage='e', d_chunks=8, dbg_out=False):
    nc = bass.Bass("TRN2", target_bir_lowering=False,
                   detect_race_conditions=False)

    ein = {}
    for name, shape in [
            ("x", (BC, L, D)), ("W_in", (D, DI)),
            ("conv_x_w", (D, 1, 4)), ("conv_x_b", (D,)),
            ("conv_z_w", (D, 1, 4)), ("conv_z_b", (D,)),
            ("W_q", (D, KD)), ("W_k", (D, KD)), ("W_v", (D, VD)),
            ("W_g", (D, VD)), ("W_gk", (D, H)), ("W_b", (D, H)),
            ("b_b", (H,)), ("A_log", (H,)), ("dt_bias", (H,)),
            ("g_norm_w", (DVH,)), ("W_so", (VD, D)), ("W_out", (DI, D)),
            ("w1", (DH, D)), ("conv_w", (DH, 1, 3)), ("conv_b", (DH,)),
            ("w2", (D, DH))]:
        ein[name] = nc.dram_tensor(name, list(shape), F32, kind="ExternalInput").ap()
    out_d = nc.dram_tensor("out", [BC, L, D], F32, kind="ExternalOutput").ap()

    def scr(name, shape, dt):
        kind = "ExternalOutput" if dbg_out else "Internal"
        return nc.dram_tensor(name, list(shape), dt, kind=kind).ap()

    win_sc = scr("win_sc", (D, DI), BF16)
    wpk_sc = scr("wpk_sc", (D, PK), BF16)
    wso_sc = scr("wso_sc", (VD, D), BF16)
    wout_sc = scr("wout_sc", (DI, D), BF16)
    t1_sc = scr("t1_sc", (DH, D), BF16)
    t2_sc = scr("t2_sc", (D, DH), BF16)
    t1T_sc = scr("t1T_sc", (D, DH), BF16)
    t2T_sc = scr("t2T_sc", (DH, D), BF16)
    dgf_sc = scr("dgf_sc", (96 * 128, 128), BF16)
    xn1_sc = scr("xn1_sc", (BC, L + 256, D), BF16)
    scC = scr("scC", (NG, G1, SCW), BF16)
    zs_sc = scr("zs_sc", (NG, 8, 128, G1), BF16)
    og_sc = scr("og_sc", (NG, G1, VD), BF16)
    x2_sc = scr("x2_sc", (BC, L, D), F32)
    xq1_sc = scr("xq1_sc", (BC, L + 128, D), BF16)
    hs_sc = scr("hs_sc", (BC, DH, L), BF16)
    hq_sc = scr("hq_sc", (BC, L, DH), BF16)

    eye = np.eye(128, dtype=np.float32)
    dm32 = np.zeros((128, 128), np.float32)
    for i in range(4):
        dm32[i * 32:(i + 1) * 32, i * 32:(i + 1) * 32] = 1.0
    consts_np = {
        "c_idbf": eye.astype(NPBF),
        "c_idf": eye,
        "c_triu": np.triu(np.ones((128, 128), np.float32)),
        "c_trils": np.tril(np.ones((128, 128), np.float32), -1),
        "c_ones": np.ones((128, 128), np.float32),
        "c_ltm": np.where(np.tril(np.ones((128, 128), np.float32), -1) > 0, 0.0, -BIG).astype(np.float32),
        "c_utm": np.where(np.triu(np.ones((128, 128), np.float32), 0) > 0, 0.0, -BIG).astype(np.float32),
        "c_dm": dm32,
        "c_nom": (dm32 - 1.0).astype(np.float32),
    }
    sel4 = np.zeros((4, 512), np.float32)
    for i in range(4):
        sel4[i, i * 128:(i + 1) * 128] = 1.0
    consts_np["c_sel"] = sel4
    const_t = {k: nc.inline_tensor(v, name=k) for k, v in consts_np.items()}

    p = P(nc)
    s = {}

    def alloc(name, shape, dt):
        s[name] = nc.alloc_sbuf_tensor(name, list(shape), dt)
        return s[name]

    for nm, sh, dt in [
            ("idbf", (128, 128), BF16), ("idf", (128, 128), F32),
            ("triu", (128, 128), F32), ("trils", (128, 128), F32),
            ("ones", (128, 128), F32), ("ltm", (128, 128), F32),
            ("utm", (128, 128), F32), ("dm", (128, 128), F32),
            ("nom", (128, 128), F32),
            ("cols", (128, 16), F32), ("i16c", (128, 2), I16),
            ("negA", (128, H), F32), ("dtb", (128, H), F32),
            ("bbt", (128, H), F32), ("gnw", (128, DVH), F32),
            ("cxb", (128, 16), F32), ("fcb", (128, 32), F32),
            ("cw", (128, 64), F32), ("fcw", (128, 96), F32),
            ("sel", (4, 512), F32),
            ("dga", (128, 64 * 128), BF16)]:
        alloc(nm, sh, dt)

    resident_end = nc.sbuf_base
    arena0 = (resident_end + 31) & ~31
    cur = [arena0]

    def place(name, shape, dt):
        nbytes = int(np.prod(shape[1:])) * mybir.dt.size(dt)
        off = (cur[0] + 31) & ~31
        cur[0] = off + nbytes
        s[name] = nc.alloc_sbuf_tensor_at(name, list(shape), dt, offset=off)
        return s[name]

    place("xw", (128, 2, D), F32)
    place("xn", (128, 2, D), BF16)
    place("xnT", (128, 8, 1152), BF16)
    place("xz", (128, 2, 1027), BF16)
    place("xs", (128, 8, G1), BF16)
    place("wp", (128, 16384), BF16)
    place("kqT", (128, 18, 128), BF16)
    place("kqtok", (128, 5 * KD), BF16)
    place("vt", (128, VD), BF16)
    place("gate", (128, 2, VD), BF16)
    place("rhsv", (128, H, DVH), BF16)
    place("rhsk", (128, H, DKH), BF16)
    place("kr", (128, H, DKH), BF16)
    place("q2", (128, 512), F32)
    place("qkc", (128, 24), F32)
    place("gcols", (128, 8, 28), F32)
    place("cT", (4, 8, 128), F32)
    place("R", (128, 512), F32)
    place("f1", (128, 512), F32)
    for i in range(11):
        place(f"db{i}", (128, 512), BF16)
    place("W1", (128, H, DVH), BF16)
    place("Kt", (128, H, DKH), BF16)
    place("P2T", (128, 2, DKH), BF16)
    place("QeT", (128, 2, 128), BF16)
    place("S1", (128, H, DVH), F32)
    place("S2", (64, H, DVH), F32)
    place("S1b", (128, H, DVH), BF16)
    place("S2b", (64, H, DVH), BF16)
    place("og", (128, VD), BF16)
    place("att", (128, D), F32)
    place("lnc", (128, 16), F32)
    p1_end = cur[0]
    cur[0] = nc.lookup_mloc(s["xnT"]).addr
    place("ogT", (128, 12, 512), BF16)
    place("ybf", (128, 8, 512), BF16)
    place("zsE", (128, 8, 512), BF16)
    assert cur[0] <= nc.lookup_mloc(s["wp"]).addr, "stage-E overlay overflow"

    cur[0] = arena0
    place("xqT", (128, 8, 1152), BF16)
    place("hh", (128, 2, 1027), BF16)
    place("hst", (128, 1, DH), BF16)
    place("hqi", (128, DH), BF16)
    place("i16t", (128, 1024), I16)
    place("qsc", (128, 2048), BF16)
    place("hqT", (128, 32, 256), BF16)
    place("dgf", (128, 6, 128), BF16)
    place("t1T", (128, 2, 8, 1024), BF16)
    place("t2T", (128, 32, 1024), BF16)
    place("p2x", (128, 2, D), F32)
    place("p2n", (128, D), F32)
    place("p2q", (128, D), BF16)
    place("i16q", (128, D), I16)
    place("y2", (128, D), F32)
    place("p2l", (128, 24), F32)
    p2_end = cur[0]

    cur[0] = p1_end
    place("qprep", (128, PK), BF16)
    q_end = cur[0]
    cur[0] = p1_end
    place("wf32", (128, 1024), F32)
    place("wi16", (128, 1024), I16)
    place("wbf", (128, 1024), BF16)
    place("wcol", (128, 4), F32)
    cur[0] = max(cur[0], q_end)
    place("qtp", (128, 1024), BF16)
    place("kqT2", (128, 18, 128), BF16)
    place("rhsk2", (128, H, DKH), BF16)
    place("kr2", (128, H, DKH), BF16)
    assert cur[0] < nc.sbuf_top - 32, f"Q/D2 overflow {cur[0]} vs {nc.sbuf_top}"

    assert max(p1_end, p2_end) < nc.sbuf_top - 64, \
        f"SBUF overflow {max(p1_end, p2_end)} vs {nc.sbuf_top}"

    ps = [nc.alloc_psum_tensor(f"pb{i}", [128, 512], F32) for i in range(6)]
    pS = nc.alloc_psum_tensor("pS", [128, 512], F32)
    pT = nc.alloc_psum_tensor("pT", [128, 1024], BF16)

    def mm(out, lhsT, rhs, start, stop):
        return lambda e: e.matmul(out, lhsT, rhs, start=start, stop=stop)

    def b3(tile128, nrep=4, w=128):
        return tile128[:, :].unsqueeze(1).to_broadcast([128, nrep, w])

    def colb(col_ap, w=128):
        n = col_ap.shape[1]
        return col_ap.unsqueeze(2).to_broadcast([128, n, w])

    def v3(tile, n=4, w=128):
        return tile.rearrange("p (a b) -> p a b", a=n)

    for dst, key in [("idbf", "c_idbf"), ("idf", "c_idf"), ("triu", "c_triu"),
                     ("trils", "c_trils"), ("ones", "c_ones"), ("ltm", "c_ltm"),
                     ("utm", "c_utm"), ("dm", "c_dm"), ("nom", "c_nom"),
                     ("sel", "c_sel")]:
        p.op("sync", lambda e, d=s[dst], t=const_t[key]: e.dma_start(out=d[:, :], in_=t.ap()[:, :]), "dS")
    p.op("sync", lambda e: e.dma_start(out=s["negA"][:, :], in_=ein["A_log"].partition_broadcast(128)), "dS")
    p.op("sync", lambda e: e.dma_start(out=s["dtb"][:, :], in_=ein["dt_bias"].partition_broadcast(128)), "dS")
    p.op("sync", lambda e: e.dma_start(out=s["bbt"][:, :], in_=ein["b_b"].partition_broadcast(128)), "dS")
    p.op("sync", lambda e: e.dma_start(out=s["gnw"][:, :], in_=ein["g_norm_w"].partition_broadcast(128)), "dS")
    p.op("sync", lambda e: e.dma_start(out=s["cxb"][:, 0:8], in_=ein["conv_x_b"].rearrange("(t p) -> p t", p=128), allow_slow_non_contiguous=True), "dS")
    p.op("sync", lambda e: e.dma_start(out=s["cxb"][:, 8:16], in_=ein["conv_z_b"].rearrange("(t p) -> p t", p=128), allow_slow_non_contiguous=True), "dS")
    p.op("sync", lambda e: e.dma_start(out=s["fcb"][:, :], in_=ein["conv_b"].rearrange("(t p) -> p t", p=128), allow_slow_non_contiguous=True), "dS")
    p.op("sync", lambda e: e.dma_start(out=s["cw"][:, 0:32].rearrange("p (t k) -> p t k", k=4), in_=ein["conv_x_w"].rearrange("(t p) o k -> p t (o k)", p=128), allow_slow_non_contiguous=True), "dS")
    p.op("sync", lambda e: e.dma_start(out=s["cw"][:, 32:64].rearrange("p (t k) -> p t k", k=4), in_=ein["conv_z_w"].rearrange("(t p) o k -> p t (o k)", p=128), allow_slow_non_contiguous=True), "dS")
    p.op("sync", lambda e: e.dma_start(out=s["fcw"][:, :].rearrange("p (t k) -> p t k", k=3), in_=ein["conv_w"].rearrange("(t p) o k -> p t (o k)", p=128), allow_slow_non_contiguous=True), "dS")
    d0 = p.cnt["dS"]
    p.wait("vector", "dS", d0)
    p.wait("scalar", "dS", d0)

    p.op("vector", lambda e: e.memset(s["cols"][:, CL_T0:CL_T0 + 1], 0.7))
    p.op("vector", lambda e: e.tensor_copy(s["i16c"][:, 0:1], s["cols"][:, CL_T0:CL_T0 + 1]))
    p.op("vector", lambda e: e.tensor_copy(s["cols"][:, CL_D:CL_D + 1], s["i16c"][:, 0:1]))
    p.op("vector", lambda e: e.tensor_scalar(s["cols"][:, CL_QB:CL_QB + 1], s["cols"][:, CL_D:CL_D + 1], -0.5, 128.5, ALU.mult, ALU.add))
    p.op("vector", lambda e: e.tensor_scalar(s["cols"][:, CL_WQB:CL_WQB + 1], s["cols"][:, CL_D:CL_D + 1], -0.5, 4.5, ALU.mult, ALU.add))
    p.op("vector", lambda e: e.memset(s["cols"][:, CL_ZERO:CL_ZERO + 1], 0.0))
    p.op("scalar", lambda e: e.activation(s["negA"][:, :], s["negA"][:, :], AF.Exp), "eS")
    p.wait("vector", "eS", p.cnt["eS"])
    p.op("vector", lambda e: e.tensor_scalar_mul(s["negA"][:, :], s["negA"][:, :], -1.0))

    for idx in range(64):
        p.op("vector", lambda e, idx=idx: e.tensor_tensor(
            s["dga"][:, idx * 128:(idx + 1) * 128], s["idf"][:, :],
            s["cw"][:, idx:idx + 1].to_broadcast([128, 128]), ALU.mult))
    ev_dga = p.cnt["eV"]

    ev = p.op("vector", lambda e: e.memset(s["xn"][:, :, :], 0.0), "eV")
    p.wait("sync", "eV", ev)
    for b in range(BC):
        for r0 in range(0, L + 256, 128):
            p.op("sync", lambda e, b=b, r0=r0: e.dma_start(out=xn1_sc[b, r0:r0 + 128, :], in_=s["xn"][:, 0, :]), "dS")
        for r0 in range(0, L + 128, 128):
            p.op("sync", lambda e, b=b, r0=r0: e.dma_start(out=xq1_sc[b, r0:r0 + 128, :], in_=s["xn"][:, 0, :]), "dS")

    d_zero = p.cnt["dS"]

    def cast_rows(dst_sc, src_ap, rows, cols):
        for r0 in range(0, rows, 128):
            dg = p.op("gpsimd", lambda e, r0=r0, cols=cols, src_ap=src_ap: e.dma_start(
                out=s["qprep"][:, 0:cols], in_=src_ap[r0:r0 + 128, 0:cols]), "dG")
            p.wait("gpsimd", "dG", dg)
            p.op("gpsimd", lambda e, r0=r0, cols=cols, dst_sc=dst_sc: e.dma_start(
                out=dst_sc[r0:r0 + 128, 0:cols], in_=s["qprep"][:, 0:cols]), "dG")
    cast_rows(win_sc, ein["W_in"], D, DI)
    c_win = p.cnt["dG"]
    segs = [("W_q", 0, KD), ("W_k", KD, KD), ("W_v", 2 * KD, VD),
            ("W_g", 2 * KD + VD, VD), ("W_gk", 2 * KD + 2 * VD, H),
            ("W_b", 2 * KD + 2 * VD + H, H)]
    for r0 in range(0, D, 128):
        dg = None
        for nm, c0, cn in segs:
            dg = p.op("gpsimd", lambda e, r0=r0, nm=nm, c0=c0, cn=cn: e.dma_start(
                out=s["qprep"][:, c0:c0 + cn], in_=ein[nm][r0:r0 + 128, 0:cn]), "dG")
        p.wait("gpsimd", "dG", dg)
        p.op("gpsimd", lambda e, r0=r0: e.dma_start(
            out=wpk_sc[r0:r0 + 128, :], in_=s["qprep"][:, :]), "dG")
    c_wpk = p.cnt["dG"]
    cast_rows(wso_sc, ein["W_so"], VD, D)
    c_wso = p.cnt["dG"]
    cast_rows(wout_sc, ein["W_out"], DI, D)
    c_cast = c_wout = p.cnt["dG"]

    def q_dgf():
        p.wait("vector", "dG", c_cast)
        for idx in range(96):
            ev = p.op("vector", lambda e, idx=idx: e.tensor_tensor(
                s["wbf"][:, 0:128], s["idf"][:, :],
                s["fcw"][:, idx:idx + 1].to_broadcast([128, 128]), ALU.mult), "eV")
            p.wait("sync", "eV", ev)
            ds = p.op("sync", lambda e, idx=idx: e.dma_start(
                out=dgf_sc[idx * 128:(idx + 1) * 128, :], in_=s["wbf"][:, 0:128]), "dS")
            p.wait("vector", "dS", ds)

    def q_quant(wname, tdst, rows, cols, CLS, CLI):
        src = ein[wname]
        p.wait("sync", "dG", c_cast)
        p.wait("vector", "dG", c_cast)
        p.wait("scalar", "dG", c_cast)
        p.wait("tensor", "eV", p.cnt["eV"])
        ev = p.op("vector", lambda e: e.memset(s["wcol"][:, 0:1], 0.0), "eV")
        tiles = [(r0, c0) for r0 in range(0, rows, 128) for c0 in range(0, cols, 1024)]
        for (r0, c0) in tiles:
            p.wait("sync", "eV", p.cnt["eV"])
            ds = p.op("sync", lambda e, r0=r0, c0=c0, src=src: e.dma_start(
                out=s["wf32"][:, :], in_=src[r0:r0 + 128, c0:c0 + 1024]), "dS")
            p.wait("vector", "dS", ds)
            p.op("vector", lambda e: e.reduce_sum(
                s["wcol"][:, 1:2], s["wf32"][:, :], axis=AX.X, apply_absolute_value=True))
            ev = p.op("vector", lambda e: e.tensor_add(s["wcol"][:, 0:1], s["wcol"][:, 0:1], s["wcol"][:, 1:2]), "eV")
        p.wait("tensor", "eV", ev)
        et = p.op("tensor", mm(ps[0][:, 0:1], s["ones"][:, :], s["wcol"][:, 0:1], True, True), "eT")
        p.wait("vector", "eT", et)
        n = rows * cols
        p.op("vector", lambda e, n=n: e.tensor_scalar(
            s["cols"][:, CL_T1:CL_T1 + 1], ps[0][:, 0:1], 1.0 / n, 1e-5, ALU.mult, ALU.max))
        p.op("vector", lambda e, CLS=CLS: e.reciprocal(s["cols"][:, CLS:CLS + 1], s["cols"][:, CL_T1:CL_T1 + 1]))
        ev = p.op("vector", lambda e, CLI=CLI: e.tensor_copy(
            s["cols"][:, CLI:CLI + 1], s["cols"][:, CL_T1:CL_T1 + 1]), "eV")
        p.wait("scalar", "eV", ev)
        p.wait("tensor", "eV", ev)
        for (r0, c0) in tiles:
            p.wait("sync", "eV", p.cnt["eV"])
            ds = p.op("sync", lambda e, r0=r0, c0=c0, src=src: e.dma_start(
                out=s["wf32"][:, :], in_=src[r0:r0 + 128, c0:c0 + 1024]), "dS")
            p.wait("scalar", "dS", ds)
            es = p.op("scalar", lambda e, CLS=CLS: e.activation(
                s["wf32"][:, :], s["wf32"][:, :], AF.Identity,
                bias=s["cols"][:, CL_WQB:CL_WQB + 1], scale=s["cols"][:, CLS:CLS + 1]), "eS")
            p.wait("vector", "eS", es)
            p.op("vector", lambda e: e.tensor_copy(s["wi16"][:, :], s["wf32"][:, :]))
            p.op("vector", lambda e: e.tensor_scalar(
                s["wf32"][:, :], s["wi16"][:, :], -4.0, -1.0, ALU.add, ALU.max))
            ev = p.op("vector", lambda e: e.tensor_scalar(
                s["wbf"][:, :], s["wf32"][:, :], 1.0, None, ALU.min), "eV")
            p.wait("sync", "eV", ev)
            ds = p.op("sync", lambda e, r0=r0, c0=c0, tdst=tdst: e.dma_start(
                out=tdst[r0:r0 + 128, c0:c0 + 1024], in_=s["wbf"][:, :]), "dS")
            p.wait("vector", "dS", ds)

    def q_t1T():
        p.wait("sync", "dS", p.cnt["dS"])
        for dt_i in range(8):
            for oct in range(4):
                ds = p.op("sync", lambda e, dt_i=dt_i, oct=oct: e.dma_start_transpose(
                    out=s["qtp"][:, :], in_=t1_sc[oct * 1024:(oct + 1) * 1024,
                                                  dt_i * 128:(dt_i + 1) * 128]), "dS")
                p.wait("sync", "dS", ds)
                p.op("sync", lambda e, dt_i=dt_i, oct=oct: e.dma_start(
                    out=t1T_sc[dt_i * 128:(dt_i + 1) * 128, oct * 1024:(oct + 1) * 1024],
                    in_=s["qtp"][:, :]), "dS")

    def q_t2T():
        p.wait("sync", "dS", p.cnt["dS"])
        for ct in range(32):
            ds = p.op("sync", lambda e, ct=ct: e.dma_start_transpose(
                out=s["qtp"][:, :], in_=t2_sc[:, ct * 128:(ct + 1) * 128]), "dS")
            p.wait("sync", "dS", ds)
            p.op("sync", lambda e, ct=ct: e.dma_start(
                out=t2T_sc[ct * 128:(ct + 1) * 128, :], in_=s["qtp"][:, :]), "dS")

    if p0_only:
        p1_groups = 0
        p2_groups = 0
    def layernorm_tile(xap, rr, outap, lnc, extra_minmax=False, scratch=None):
        sq = scratch if scratch is not None else s["q2"]
        es = p.op("scalar", lambda e: e.activation(
            sq[0:rr, 0:512], xap[0:rr, 0:512], AF.Square, accum_out=lnc[0:rr, 0:1]), "eS")
        es = p.op("scalar", lambda e: e.activation(
            sq[0:rr, 0:512], xap[0:rr, 512:1024], AF.Square, accum_out=lnc[0:rr, 1:2]), "eS")
        p.wait("vector", "eS", es)
        p.op("vector", lambda e: e.reduce_sum(lnc[0:rr, 2:3], xap[0:rr, :], axis=AX.X))
        if extra_minmax:
            p.op("vector", lambda e: e.reduce_max(lnc[0:rr, 8:9], xap[0:rr, :], axis=AX.X))
            p.op("vector", lambda e: e.tensor_reduce(lnc[0:rr, 9:10], xap[0:rr, :], axis=AX.X, op=ALU.min))
        p.op("vector", lambda e: e.tensor_scalar_mul(lnc[0:rr, 3:4], lnc[0:rr, 2:3], 1.0 / D))
        p.op("vector", lambda e: e.tensor_add(lnc[0:rr, 4:5], lnc[0:rr, 0:1], lnc[0:rr, 1:2]))
        p.op("vector", lambda e: e.tensor_scalar_mul(lnc[0:rr, 4:5], lnc[0:rr, 4:5], 1.0 / D))
        p.op("vector", lambda e: e.tensor_mul(lnc[0:rr, 5:6], lnc[0:rr, 3:4], lnc[0:rr, 3:4]))
        p.op("vector", lambda e: e.tensor_sub(lnc[0:rr, 4:5], lnc[0:rr, 4:5], lnc[0:rr, 5:6]))
        ev = p.op("vector", lambda e: e.tensor_scalar(lnc[0:rr, 4:5], lnc[0:rr, 4:5], 1e-5, None, ALU.add), "eV")
        p.wait("scalar", "eV", ev)
        es = p.op("scalar", lambda e: e.activation(lnc[0:rr, 6:7], lnc[0:rr, 4:5], AF.Sqrt), "eS")
        p.wait("vector", "eS", es)
        p.op("vector", lambda e: e.reciprocal(lnc[0:rr, 7:8], lnc[0:rr, 6:7]))
        if outap is not None:
            ev = p.op("vector", lambda e: e.tensor_scalar(
                outap[0:rr, :], xap[0:rr, :], lnc[0:rr, 3:4], lnc[0:rr, 7:8],
                ALU.subtract, ALU.mult), "eV")
            return ev
        return p.cnt["eV"]

    d_lastuse = {}
    rv_used = [None]
    for grp in range(p1_groups):
        p.wait("sync", "eV", p.cnt["eV"])
        p.wait("sync", "eT", p.cnt["eT"])
        p.wait("gpsimd", "eT", p.cnt["eT"])
        p.wait("tensor", "eS", p.cnt["eS"])
        p.wait("vector", "eT", p.cnt["eT"])
        b, gi = divmod(grp, 2)
        t0 = gi * G1
        nh = 0 if gi == 1 else 2
        ntok = G1 + nh

        last_w = None
        for tt in range((ntok + 127) // 128):
            r0 = t0 + tt * 128
            rr = min(128, ntok - tt * 128)
            buf = tt % 2
            p.wait("sync", "eV", p.lw["sync"].get("eV", 0))
            ds = p.op("sync", lambda e, b=b, r0=r0, rr=rr, buf=buf: e.dma_start(
                out=s["xw"][0:rr, buf, :], in_=ein["x"][b, r0:r0 + rr, :]), "dS")
            p.wait("scalar", "dS", ds)
            p.wait("vector", "dS", ds)
            ev = layernorm_tile(s["xw"][:, buf, :], rr, s["xn"][:, buf, :], s["lnc"])
            p.wait("sync", "eV", ev)
            last_w = p.op("sync", lambda e, b=b, r0=r0, rr=rr, buf=buf: e.dma_start(
                out=xn1_sc[b, 128 + r0:128 + r0 + rr, :], in_=s["xn"][0:rr, buf, :]), "dS")
            p.wait("vector", "dS", last_w)
        p.wait("sync", "dS", last_w)
        for dt_i in range(8):
            p.op("sync", lambda e, b=b, t0=t0, dt_i=dt_i: e.dma_start_transpose(
                out=s["xnT"][:, dt_i, :],
                in_=xn1_sc[b, 64 + t0:64 + t0 + 1152, dt_i * 128:(dt_i + 1) * 128]), "dS")
        d_xnT = p.cnt["dS"]

        if p1_stage < 'b':
            continue
        ev_zsw = []
        et_hist = {}
        if grp == 0:
            p.wait("gpsimd", "dG", c_win)
        for ct in range(16):
            par = ct % 2
            if ct >= 2:
                p.wait("gpsimd", "eT", et_hist[ct - 2])
            dgm = None
            for kt in range(8):
                dgm = p.op("gpsimd", lambda e, ct=ct, kt=kt, par=par: e.dma_start(
                    out=s["wp"][:, par * 1024 + kt * 128:par * 1024 + (kt + 1) * 128],
                    in_=win_sc[kt * 128:(kt + 1) * 128, ct * 128:(ct + 1) * 128]), "dG")
            p.wait("tensor", "dG", dgm)
            p.wait("tensor", "dS", d_xnT)
            if grp == 0 and ct == 0:
                p.wait("tensor", "dS", d0)
                p.wait("tensor", "eV", ev_dga)
            for kt in range(8):
                p.op("tensor", mm(ps[0][:, :], s["wp"][:, par * 1024 + kt * 128:par * 1024 + (kt + 1) * 128],
                                  s["xnT"][:, kt, 63:575], kt == 0, kt == 7))
            for kt in range(8):
                p.op("tensor", mm(ps[1][:, :], s["wp"][:, par * 1024 + kt * 128:par * 1024 + (kt + 1) * 128],
                                  s["xnT"][:, kt, 575:1087], kt == 0, kt == 7))
            et = None
            for kt in range(8):
                et = p.op("tensor", mm(ps[2][:, 0:3], s["wp"][:, par * 1024 + kt * 128:par * 1024 + (kt + 1) * 128],
                                       s["xnT"][:, kt, 1087:1090], kt == 0, kt == 7),
                          "eT" if kt == 7 else None)
            p.wait("vector", "eT", et)
            p.op("vector", lambda e, par=par: e.tensor_copy(s["xz"][:, par, 0:512], ps[0][:, :]))
            p.op("vector", lambda e, par=par: e.tensor_copy(s["xz"][:, par, 512:1024], ps[1][:, :]))
            ev = p.op("vector", lambda e, par=par: e.tensor_copy(s["xz"][:, par, 1024:1027], ps[2][:, 0:3]), "eV")
            p.wait("tensor", "eV", ev)
            if p1_stage < 'b2':
                continue
            for half in range(2):
                et = None
                for j in range(4):
                    et = p.op("tensor", mm(
                        ps[3][:, :], s["dga"][:, (ct * 4 + j) * 128:(ct * 4 + j + 1) * 128],
                        s["xz"][:, par, half * 512 + j:half * 512 + j + 512], j == 0, j == 3),
                        "eT" if j == 3 else None)
                p.wait("scalar", "eT", et)
                if p1_stage < 'b3':
                    continue
                if ct < 8:
                    es = p.op("scalar", lambda e, ct=ct, half=half: e.activation(
                        s["xs"][:, ct, half * 512:(half + 1) * 512], ps[3][:, :], AF.Silu,
                        bias=s["cxb"][:, ct:ct + 1], scale=1.0), "eS")
                    p.wait("tensor", "eS", es)
                else:
                    es = p.op("scalar", lambda e, ct=ct, half=half: e.activation(
                        s["vt"][:, half * 512:(half + 1) * 512] if half == 0 else s["vt"][:, 512:1024],
                        ps[3][:, :], AF.Silu,
                        bias=s["cxb"][:, ct:ct + 1], scale=1.0), "eS")
                    p.wait("sync", "eS", es)
                    p.wait("tensor", "eS", es)
                    dw = p.op("sync", lambda e, grp=grp, ct=ct, half=half: e.dma_start(
                        out=zs_sc[grp, ct - 8, :, half * 512:(half + 1) * 512],
                        in_=s["vt"][:, half * 512:(half + 1) * 512]), "dS")
                    ev_zsw.append(dw)
                    p.wait("scalar", "dS", dw)
                    p.wait("vector", "dS", dw)
            et_hist[ct] = p.cnt["eT"]
        ev_xs = p.cnt["eS"]

        if p1_stage < 'c':
            continue
        if grp == 0:
            p.wait("gpsimd", "dG", c_wpk)
        dgm = None
        for kt in range(8):
            dgm = p.op("gpsimd", lambda e, kt=kt: e.dma_start(
                out=s["wp"][:, 2048 + kt * 8:2048 + (kt + 1) * 8],
                in_=wpk_sc[kt * 128:(kt + 1) * 128, 4608:4616]), "dG")
        p.wait("tensor", "dG", dgm)
        p.wait("tensor", "eS", ev_xs)
        for cc in range(8):
            et = None
            for kt in range(8):
                et = p.op("tensor", mm(ps[3][:, 0:8], s["xs"][:, kt, cc * 128:(cc + 1) * 128],
                                       s["wp"][:, 2048 + kt * 8:2048 + (kt + 1) * 8],
                                       kt == 0, kt == 7), "eT" if kt == 7 else None)
            p.wait("vector", "eT", et)
            if p1_stage == 'cA':
                continue
            gc = s["gcols"]
            qc = s["qkc"]
            p.op("vector", lambda e, cc=cc: e.tensor_add(qc[:, 0:4], ps[3][:, 0:4], s["dtb"][:, :]))
            ev = p.op("vector", lambda e, cc=cc: e.tensor_add(qc[:, 4:8], ps[3][:, 4:8], s["bbt"][:, :]), "eV")
            p.wait("scalar", "eV", ev)
            es = p.op("scalar", lambda e: e.activation(qc[:, 8:12], qc[:, 0:4], AF.Abs), "eS")
            es = p.op("scalar", lambda e: e.activation(qc[:, 8:12], qc[:, 8:12], AF.Exp, scale=-1.0), "eS")
            es = p.op("scalar", lambda e: e.activation(qc[:, 8:12], qc[:, 8:12], AF.Ln, bias=1.0), "eS")
            es = p.op("scalar", lambda e: e.activation(qc[:, 12:16], qc[:, 0:4], AF.Relu), "eS")
            es = p.op("scalar", lambda e: e.activation(qc[:, 16:20], qc[:, 4:8], AF.Sigmoid), "eS")
            p.wait("vector", "eS", es)
            p.op("vector", lambda e: e.tensor_add(qc[:, 8:12], qc[:, 8:12], qc[:, 12:16]))
            ev = p.op("vector", lambda e: e.tensor_mul(qc[:, 20:24], qc[:, 8:12], s["negA"][:, :]), "eV")
            p.op("vector", lambda e, cc=cc: e.tensor_copy(gc[:, cc, 0:4], qc[:, 16:20]))
            ev = p.op("vector", lambda e, cc=cc: e.tensor_scalar_mul(gc[:, cc, 24:28], qc[:, 16:20], -1.0), "eV")
            p.wait("tensor", "eV", ev)
            if p1_stage == 'cB':
                continue
            et = p.op("tensor", mm(ps[4][:, 0:4], s["triu"][:, :], qc[:, 20:24], True, True), "eT")
            et = p.op("tensor", mm(ps[4][:, 4:8], s["trils"][:, :], qc[:, 20:24], True, True), "eT")
            et = p.op("tensor", mm(ps[4][:, 8:12], s["ones"][:, :], qc[:, 20:24], True, True), "eT")
            p.wait("vector", "eT", et)
            ev = p.op("vector", lambda e, cc=cc: e.tensor_copy(gc[:, cc, 12:16], ps[4][:, 0:4]), "eV")
            p.wait("scalar", "eV", ev)
            es = p.op("scalar", lambda e, cc=cc: e.activation(gc[:, cc, 4:8], ps[4][:, 0:4], AF.Exp), "eS")
            es = p.op("scalar", lambda e, cc=cc: e.activation(gc[:, cc, 8:12], ps[4][:, 4:8], AF.Exp), "eS")
            es = p.op("scalar", lambda e, cc=cc: e.activation(gc[:, cc, 16:20], ps[4][:, 8:12], AF.Exp), "eS")
            p.wait("vector", "eS", es)
            ev = p.op("vector", lambda e, cc=cc: e.tensor_mul(gc[:, cc, 20:24], gc[:, cc, 0:4], gc[:, cc, 4:8]), "eV")
            p.wait("tensor", "eV", ev)
            if p1_stage == 'cC':
                continue
            et = p.op("tensor", lambda e, cc=cc: e.transpose(pS[0:4, 0:128], gc[:, cc, 12:16], s["idf"][:, :]), "eT")
            p.wait("vector", "eT", et)
            ev = p.op("vector", lambda e, cc=cc: e.tensor_copy(s["cT"][0:4, cc, :], pS[0:4, 0:128]), "eV")
            p.wait("tensor", "eV", ev)

        if p1_stage < 'c1':
            continue
        p.wait("gpsimd", "eT", p.cnt["eT"])
        dgm = None
        for kt in range(8):
            dgm = p.op("gpsimd", lambda e, kt=kt: e.dma_start(
                out=s["wp"][:, kt * 1536:(kt + 1) * 1536],
                in_=wpk_sc[kt * 128:(kt + 1) * 128, 0:1536]), "dG")
        p.wait("tensor", "dG", dgm)
        for cc in range(8):
            et = None
            for nb in range(3):
                for kt in range(8):
                    et = p.op("tensor", mm(ps[nb][:, :], s["xs"][:, kt, cc * 128:(cc + 1) * 128],
                                           s["wp"][:, kt * 1536 + nb * 512:kt * 1536 + (nb + 1) * 512],
                                           kt == 0, kt == 7), "eT" if kt == 7 else None)
            p.wait("scalar", "eT", et)
            es = None
            split_heads = []
            for hh in range(8):
                pl = pieces(hh * DKH, (hh + 1) * DKH)
                for i, (bk, lo, ln, ab) in enumerate(pl):
                    tgt = hh if i == 0 else 8 + hh
                    if i > 0:
                        split_heads.append(hh)
                    es = p.op("scalar", lambda e, bk=bk, lo=lo, ln=ln, tgt=tgt: e.activation(
                        s["q2"][:, 0:ln], ps[bk][:, lo:lo + ln], AF.Square,
                        accum_out=s["qkc"][:, tgt:tgt + 1]), "eS")
            p.wait("vector", "eS", es)
            for hh in split_heads:
                p.op("vector", lambda e, hh=hh: e.tensor_add(
                    s["qkc"][:, hh:hh + 1], s["qkc"][:, hh:hh + 1], s["qkc"][:, 8 + hh:9 + hh]))
            ev = p.op("vector", lambda e: e.tensor_scalar(
                s["qkc"][:, 0:8], s["qkc"][:, 0:8], 1e-6, None, ALU.add), "eV")
            p.wait("scalar", "eV", ev)
            es = p.op("scalar", lambda e: e.activation(s["qkc"][:, 16:24], s["qkc"][:, 0:8], AF.Sqrt), "eS")
            p.wait("vector", "eS", es)
            ev = p.op("vector", lambda e: e.reciprocal(s["qkc"][:, 8:16], s["qkc"][:, 16:24]), "eV")
            p.op("vector", lambda e, cc=cc: e.tensor_mul(s["qkc"][:, 16:20], s["qkc"][:, 8:12], s["gcols"][:, cc, 4:8]))
            ev = p.op("vector", lambda e, cc=cc: e.tensor_copy(s["qkc"][:, 20:24], s["qkc"][:, 12:16]), "eV")
            for hh in range(8):
                dst0 = (SC_KN + (hh - 4) * DKH) if hh >= 4 else (SC_QN + hh * DKH)
                rs_col = 12 + (hh - 4) if hh >= 4 else 8 + hh
                for (bk, lo, ln, ab) in pieces(hh * DKH, (hh + 1) * DKH):
                    off = ab - hh * DKH
                    ev = p.op("vector", lambda e, bk=bk, lo=lo, ln=ln, dst0=dst0, off=off, rs_col=rs_col: e.tensor_scalar(
                        s["kqtok"][:, dst0 + off:dst0 + off + ln], ps[bk][:, lo:lo + ln],
                        s["qkc"][:, rs_col:rs_col + 1], None, ALU.mult), "eV")
                    if hh < 4:
                        ev = p.op("vector", lambda e, bk=bk, lo=lo, ln=ln, hh=hh, off=off: e.tensor_scalar(
                            s["kqtok"][:, SC_QG + hh * DKH + off:SC_QG + hh * DKH + off + ln],
                            ps[bk][:, lo:lo + ln], s["qkc"][:, 16 + hh:17 + hh], None, ALU.mult), "eV")
            for hh in range(4):
                ev = p.op("vector", lambda e, hh=hh, cc=cc: e.tensor_scalar(
                    s["kqtok"][:, SC_RK + hh * DKH:SC_RK + (hh + 1) * DKH],
                    s["kqtok"][:, SC_KN + hh * DKH:SC_KN + (hh + 1) * DKH],
                    s["gcols"][:, cc, 20 + hh:21 + hh], None, ALU.mult), "eV")
                ev = p.op("vector", lambda e, hh=hh, cc=cc: e.tensor_scalar(
                    s["kqtok"][:, SC_KR + hh * DKH:SC_KR + (hh + 1) * DKH],
                    s["kqtok"][:, SC_KN + hh * DKH:SC_KN + (hh + 1) * DKH],
                    s["gcols"][:, cc, 8 + hh:9 + hh], None, ALU.mult), "eV")
            p.wait("sync", "eV", ev)
            ds = p.op("sync", lambda e, grp=grp, cc=cc: e.dma_start(
                out=scC[grp, cc * 128:(cc + 1) * 128, 0:5 * KD], in_=s["kqtok"][:, :]), "dS")
            p.wait("vector", "dS", ds)
            p.wait("tensor", "eV", ev)

        if p1_stage < 'c2':
            continue
        p.wait("gpsimd", "eT", p.cnt["eT"])
        dgm = None
        for kt in range(8):
            dgm = p.op("gpsimd", lambda e, kt=kt: e.dma_start(
                out=s["wp"][:, kt * 1536:(kt + 1) * 1536],
                in_=wpk_sc[kt * 128:(kt + 1) * 128, 1536:3072]), "dG")
        p.wait("tensor", "dG", dgm)
        for cc in range(8):
            et = None
            for nb in range(3):
                for kt in range(8):
                    et = p.op("tensor", mm(ps[nb][:, :], s["xs"][:, kt, cc * 128:(cc + 1) * 128],
                                           s["wp"][:, kt * 1536 + nb * 512:kt * 1536 + (nb + 1) * 512],
                                           kt == 0, kt == 7), "eT" if kt == 7 else None)
            p.wait("vector", "eT", et)
            ev = None
            for hh in range(4):
                for (bk, lo, ln, ab) in pieces(hh * DVH, (hh + 1) * DVH):
                    off = ab
                    ev = p.op("vector", lambda e, bk=bk, lo=lo, ln=ln, off=off, hh=hh, cc=cc: e.tensor_scalar(
                        s["vt"][:, off:off + ln], ps[bk][:, lo:lo + ln],
                        s["gcols"][:, cc, 0 + hh:1 + hh], None, ALU.mult), "eV")
            p.wait("sync", "eV", ev)
            ds = p.op("sync", lambda e, grp=grp, cc=cc: e.dma_start(
                out=scC[grp, cc * 128:(cc + 1) * 128, SC_RV:SC_RV + VD], in_=s["vt"][:, :]), "dS")
            p.wait("vector", "dS", ds)
            p.wait("scalar", "dS", ds)
            p.wait("tensor", "eV", ev)

        if p1_stage < 'c3':
            continue
        p.wait("gpsimd", "eT", p.cnt["eT"])
        dgm = None
        for kt in range(8):
            dgm = p.op("gpsimd", lambda e, kt=kt: e.dma_start(
                out=s["wp"][:, kt * 1536:(kt + 1) * 1536],
                in_=wpk_sc[kt * 128:(kt + 1) * 128, 3072:4608]), "dG")
        p.wait("tensor", "dG", dgm)
        for cc in range(8):
            et = None
            for nb in range(3):
                for kt in range(8):
                    et = p.op("tensor", mm(ps[nb][:, :], s["xs"][:, kt, cc * 128:(cc + 1) * 128],
                                           s["wp"][:, kt * 1536 + nb * 512:kt * 1536 + (nb + 1) * 512],
                                           kt == 0, kt == 7), "eT" if kt == 7 else None)
            p.wait("scalar", "eT", et)
            es = None
            for nb in range(3):
                es = p.op("scalar", lambda e, nb=nb: e.activation(
                    s["vt"][:, nb * 512:(nb + 1) * 512], ps[nb][:, :], AF.Silu), "eS")
            p.wait("vector", "eS", es)
            ev = p.op("vector", lambda e: e.tensor_tensor(
                v3(s["vt"][:, :], 4, DVH), v3(s["vt"][:, :], 4, DVH),
                s["gnw"][:, :].unsqueeze(1).to_broadcast([128, 4, DVH]), ALU.mult), "eV")
            p.wait("sync", "eV", ev)
            ds = p.op("sync", lambda e, grp=grp, cc=cc: e.dma_start(
                out=scC[grp, cc * 128:(cc + 1) * 128, SC_GT:SC_GT + VD], in_=s["vt"][:, :]), "dS")
            p.wait("vector", "dS", ds)
            p.wait("scalar", "dS", ds)
            p.wait("tensor", "eV", ev)
        d_stageC = p.cnt["dS"]

        if p1_stage < 'd':
            continue

        d_zsw = ev_zsw[-1] if ev_zsw else 0
        e_state = {}

        def e_pre(half, dwait, grp=grp):
            p.wait("sync", "dG", dwait)
            p.wait("sync", "dS", d_zsw)
            p.wait("sync", "eT", p.cnt["eT"])
            for f in range(12):
                p.op("sync", lambda e, grp=grp, half=half, f=f: e.dma_start_transpose(
                    out=s["ogT"][:, f, :],
                    in_=og_sc[grp, half * 512:(half + 1) * 512, f * 128:(f + 1) * 128]), "dS")
            e_state["ogT"] = p.cnt["dS"]
            for ct in range(8):
                p.op("sync", lambda e, grp=grp, ct=ct, half=half: e.dma_start(
                    out=s["zsE"][:, ct, :], in_=zs_sc[grp, ct, :, half * 512:(half + 1) * 512]), "dS")
            e_state["zsE"] = p.cnt["dS"]

        def e_wso(half, dts):
            d_ogT = e_state["ogT"]
            if dts[0] == 0:
                p.wait("gpsimd", "dG", c_wso)
                p.wait("gpsimd", "eT", p.cnt["eT"])
            for dt_i in dts:
                dgm = None
                for f in range(12):
                    dgm = p.op("gpsimd", lambda e, dt_i=dt_i, f=f: e.dma_start(
                        out=s["wp"][:, f * 128:(f + 1) * 128],
                        in_=wso_sc[f * 128:(f + 1) * 128, dt_i * 128:(dt_i + 1) * 128]), "dG")
                p.wait("tensor", "dG", dgm)
                p.wait("tensor", "dS", d_ogT)
                et = None
                for f in range(12):
                    et = p.op("tensor", mm(ps[0][:, :], s["wp"][:, f * 128:(f + 1) * 128],
                                           s["ogT"][:, f, :], f == 0, f == 11), "eT" if f == 11 else None)
                p.wait("vector", "eT", et)
                ev = p.op("vector", lambda e, dt_i=dt_i: e.tensor_copy(s["ybf"][:, dt_i, :], ps[0][:, :]), "eV")
                p.wait("gpsimd", "eV", ev)
                p.wait("tensor", "eV", ev)

        def e_wout(half, b=b, t0=t0):
            d_zsE = e_state["zsE"]
            p.wait("gpsimd", "dG", c_wout)
            p.wait("gpsimd", "eT", p.cnt["eT"])
            dgm = None
            for kt in range(16):
                dgm = p.op("gpsimd", lambda e, kt=kt: e.dma_start(
                    out=s["wp"][:, kt * 1024:(kt + 1) * 1024],
                    in_=wout_sc[kt * 128:(kt + 1) * 128, :]), "dG")
            p.wait("tensor", "dG", dgm)
            p.wait("tensor", "dS", d_zsE)
            for tc in range(4):
                et = None
                for nhf in range(2):
                    for kt in range(16):
                        lhs = s["ybf"][:, kt, tc * 128:(tc + 1) * 128] if kt < 8 \
                            else s["zsE"][:, kt - 8, tc * 128:(tc + 1) * 128]
                        et = p.op("tensor", mm(
                            ps[2 + nhf][:, :], lhs,
                            s["wp"][:, kt * 1024 + nhf * 512:kt * 1024 + (nhf + 1) * 512],
                            kt == 0, kt == 15), "eT" if kt == 15 else None)
                p.wait("scalar", "eT", et)
                es = p.op("scalar", lambda e: e.copy(s["att"][:, 0:512], ps[2][:, :]), "eS")
                es = p.op("scalar", lambda e: e.copy(s["att"][:, 512:1024], ps[3][:, :]), "eS")
                rbase = t0 + half * 512 + tc * 128
                ds = p.op("sync", lambda e, b=b, rbase=rbase: e.dma_start(
                    out=s["xw"][:, 0, :], in_=ein["x"][b, rbase:rbase + 128, :]), "dS")
                p.wait("vector", "eS", es)
                p.wait("scalar", "eS", es)
                layernorm_tile(s["att"][:, :], 128, None, s["lnc"])
                p.wait("vector", "dS", ds)
                p.op("vector", lambda e: e.tensor_scalar(
                    s["att"][:, :], s["att"][:, :],
                    s["lnc"][:, 3:4], s["lnc"][:, 7:8], ALU.subtract, ALU.mult))
                ev = p.op("vector", lambda e: e.tensor_add(
                    s["att"][:, :], s["att"][:, :], s["xw"][:, 0, :]), "eV")
                p.wait("sync", "eV", ev)
                ds = p.op("sync", lambda e, b=b, rbase=rbase: e.dma_start(
                    out=x2_sc[b, rbase:rbase + 128, :], in_=s["att"][:, :]), "dS")
                p.wait("scalar", "dS", ds)
                p.wait("tensor", "dS", ds)

        if gi == 0:
            ev = p.op("vector", lambda e: e.memset(s["S1"][:, :, :], 0.0), "eV")
            ev = p.op("vector", lambda e: e.memset(s["S2"][:, :, :], 0.0), "eV")
            ev = p.op("vector", lambda e: e.memset(s["S1b"][:, :, :], 0.0), "eV")
            ev = p.op("vector", lambda e: e.memset(s["S2b"][:, :, :], 0.0), "eV")
            p.wait("tensor", "eV", ev)

        for cc in range(d_chunks):
            par = cc % 2
            bkq = s["kqT"] if par == 0 else s["kqT2"]
            brk = s["rhsk"] if par == 0 else s["rhsk2"]
            bkr = s["kr"] if par == 0 else s["kr2"]
            p.wait("sync", "dS", d_stageC)
            lu = d_lastuse.get(par)
            if lu is not None:
                p.wait("sync", "eT", lu[0])
                p.wait("sync", "eV", lu[1])
            for f in range(18):
                p.op("sync", lambda e, grp=grp, cc=cc, f=f, bkq=bkq: e.dma_start_transpose(
                    out=bkq[:, f, :],
                    in_=scC[grp, cc * 128:(cc + 1) * 128, f * 128:(f + 1) * 128]), "dS")
            p.op("sync", lambda e, grp=grp, cc=cc, brk=brk: e.dma_start(
                out=brk[:, :, :], in_=scC[grp, cc * 128:(cc + 1) * 128, SC_RK:SC_RK + KD]), "dS")
            p.op("sync", lambda e, grp=grp, cc=cc, bkr=bkr: e.dma_start(
                out=bkr[:, :, :], in_=scC[grp, cc * 128:(cc + 1) * 128, SC_KR:SC_KR + KD]), "dS")
            if rv_used[0] is not None:
                p.wait("sync", "eT", rv_used[0])
            p.op("sync", lambda e, grp=grp, cc=cc: e.dma_start(
                out=s["rhsv"][:, :, :], in_=scC[grp, cc * 128:(cc + 1) * 128, SC_RV:SC_RV + VD]), "dS")
            dl = p.op("sync", lambda e, grp=grp, cc=cc: e.dma_start(
                out=s["gate"][:, cc % 2, :], in_=scC[grp, cc * 128:(cc + 1) * 128, SC_GT:SC_GT + VD]), "dS")
            p.wait("tensor", "dS", dl)

            for hh in range(4):
                sls = head_slices(hh)
                first = True
                for (tl, p0, p1) in sls:
                    p.op("tensor", mm(ps[0][:, hh * 128:(hh + 1) * 128],
                                      bkq[p0:p1, tl, :], bkq[p0:p1, tl, :],
                                      first, (tl, p0, p1) == sls[-1]))
                    first = False
                first = True
                for (tl, p0, p1) in sls:
                    p.op("tensor", mm(ps[1][:, hh * 128:(hh + 1) * 128],
                                      bkq[p0:p1, tl, :], bkq[p0:p1, 6 + tl, :],
                                      first, (tl, p0, p1) == sls[-1]))
                    first = False
            et = None
            for hh in range(4):
                et = p.op("tensor", mm(ps[2][:, hh * 128:(hh + 1) * 128],
                                       s["sel"][0:4, hh * 128:(hh + 1) * 128],
                                       s["cT"][0:4, cc, :], True, True),
                          "eT")
            p.wait("vector", "eT", et)
            ev = p.op("vector", lambda e, cc=cc: e.tensor_tensor(
                v3(s["R"][:, :]), v3(ps[2][:, :]),
                s["gcols"][:, cc, 12:16].unsqueeze(2).to_broadcast([128, 4, 128]),
                ALU.subtract), "eV")
            p.wait("scalar", "eV", ev)
            es = p.op("scalar", lambda e: e.activation(s["f1"][:, :], s["R"][:, :], AF.Identity, scale=-1.0), "eS")
            p.wait("vector", "eS", es)
            ev = p.op("vector", lambda e: e.tensor_tensor(
                v3(s["f1"][:, :]), v3(s["f1"][:, :]), b3(s["ltm"]), ALU.add), "eV")
            p.wait("scalar", "eV", ev)
            es = p.op("scalar", lambda e: e.activation(s["f1"][:, :], s["f1"][:, :], AF.Exp), "eS")
            p.wait("vector", "eS", es)
            p.op("vector", lambda e: e.tensor_mul(s["f1"][:, :], s["f1"][:, :], ps[0][:, :]))
            ev = p.op("vector", lambda e, cc=cc: e.tensor_tensor(
                v3(s["db0"][:, :]), v3(s["f1"][:, :]),
                s["gcols"][:, cc, 24:28].unsqueeze(2).to_broadcast([128, 4, 128]),
                ALU.mult), "eV")
            p.op("vector", lambda e: e.tensor_tensor(
                v3(s["f1"][:, :]), v3(s["R"][:, :]), b3(s["utm"]), ALU.add))
            ev2 = p.op("vector", lambda e: e.engine_nop(), "eV")
            p.wait("scalar", "eV", ev2)
            es = p.op("scalar", lambda e: e.activation(s["f1"][:, :], s["f1"][:, :], AF.Exp), "eS")
            p.wait("vector", "eS", es)
            ev = p.op("vector", lambda e: e.tensor_mul(s["f1"][:, :], s["f1"][:, :], ps[1][:, :]), "eV")
            ev = p.op("vector", lambda e: e.tensor_copy(s["db1"][:, :], s["f1"][:, :]), "eV")
            p.op("vector", lambda e: e.tensor_tensor(v3(s["db2"][:, :]), v3(s["db0"][:, :]), b3(s["dm"]), ALU.mult))
            ev = p.op("vector", lambda e: e.tensor_tensor(v3(s["db3"][:, :]), v3(s["db0"][:, :]), b3(s["nom"]), ALU.mult), "eV")
            p.wait("tensor", "eV", ev)
            et = None
            for hh in range(4):
                et = p.op("tensor", lambda e, hh=hh: e.transpose(
                    pT[:, hh * 128:(hh + 1) * 128], s["db2"][:, hh * 128:(hh + 1) * 128], s["idbf"][:, :]), "eT")
            p.wait("vector", "eT", et)
            ev = p.op("vector", lambda e: e.tensor_copy(s["db4"][:, :], pT[:, 0:512]), "eV")
            ev = p.op("vector", lambda e: e.tensor_tensor(v3(s["db5"][:, :]), v3(s["db4"][:, :]), b3(s["idbf"]), ALU.add), "eV")
            p.wait("tensor", "eV", ev)
            Ga, Ha, Pa, Gb, Hb, Pb = "db2", "db4", "db5", "db6", "db7", "db8"
            for lev in range(1, 5):
                et = None
                for hh in range(4):
                    et = p.op("tensor", mm(ps[3][:, hh * 128:(hh + 1) * 128],
                                           s[Ha][:, hh * 128:(hh + 1) * 128],
                                           s[Ga][:, hh * 128:(hh + 1) * 128], True, True), "eT")
                if lev < 4:
                    for hh in range(4):
                        et = p.op("tensor", mm(ps[4][:, hh * 128:(hh + 1) * 128],
                                               s[Ga][:, hh * 128:(hh + 1) * 128],
                                               s[Ha][:, hh * 128:(hh + 1) * 128], True, True), "eT")
                p.wait("vector", "eT", et)
                p.op("vector", lambda e, Gb=Gb: e.tensor_copy(s[Gb][:, :], ps[3][:, :]))
                if lev < 4:
                    p.op("vector", lambda e, Hb=Hb: e.tensor_copy(s[Hb][:, :], ps[4][:, :]))
                ev = p.op("vector", lambda e: e.engine_nop(), "eV")
                p.wait("tensor", "eV", ev)
                et = None
                for hh in range(4):
                    p.op("tensor", mm(ps[5][:, hh * 128:(hh + 1) * 128],
                                      s["idbf"][:, :], s[Pa][:, hh * 128:(hh + 1) * 128], True, False))
                    et = p.op("tensor", mm(ps[5][:, hh * 128:(hh + 1) * 128],
                                           s[Gb][:, hh * 128:(hh + 1) * 128],
                                           s[Pa][:, hh * 128:(hh + 1) * 128], False, True), "eT")
                p.wait("vector", "eT", et)
                ev = p.op("vector", lambda e, Pb=Pb: e.tensor_copy(s[Pb][:, :], ps[5][:, :]), "eV")
                p.wait("tensor", "eV", ev)
                Ga, Gb = Gb, Ga
                Ha, Hb = Hb, Ha
                Pa, Pb = Pb, Pa
            TdT = Pa
            et = None
            for hh in range(4):
                et = p.op("tensor", lambda e, hh=hh, TdT=TdT: e.transpose(
                    pT[:, hh * 128:(hh + 1) * 128], s[TdT][:, hh * 128:(hh + 1) * 128], s["idbf"][:, :]), "eT")
            p.wait("vector", "eT", et)
            ev = p.op("vector", lambda e: e.tensor_copy(s["db9"][:, :], pT[:, 0:512]), "eV")
            p.wait("tensor", "eV", ev)
            et = None
            for hh in range(4):
                et = p.op("tensor", mm(ps[3][:, hh * 128:(hh + 1) * 128],
                                       s["db3"][:, hh * 128:(hh + 1) * 128],
                                       s[TdT][:, hh * 128:(hh + 1) * 128], True, True), "eT")
            p.wait("vector", "eT", et)
            ev = p.op("vector", lambda e: e.tensor_copy(s["db10"][:, :], ps[3][:, :]), "eV")
            p.wait("tensor", "eV", ev)
            et = None
            for hh in range(4):
                et = p.op("tensor", lambda e, hh=hh: e.transpose(
                    pT[:, hh * 128:(hh + 1) * 128], s["db10"][:, hh * 128:(hh + 1) * 128], s["idbf"][:, :]), "eT")
            p.wait("vector", "eT", et)
            ev = p.op("vector", lambda e, Gb=Gb: e.tensor_copy(s[Gb][:, :], pT[:, 0:512]), "eV")
            p.wait("tensor", "eV", ev)
            et = None
            for hh in range(4):
                et = p.op("tensor", mm(ps[3][:, hh * 128:(hh + 1) * 128],
                                       s[Gb][:, hh * 128:(hh + 1) * 128],
                                       s["db10"][:, hh * 128:(hh + 1) * 128], True, True), "eT")
            p.wait("vector", "eT", et)
            ev = p.op("vector", lambda e, Hb=Hb: e.tensor_copy(s[Hb][:, :], ps[3][:, :]), "eV")
            p.wait("tensor", "eV", ev)
            et = None
            for hh in range(4):
                et = p.op("tensor", mm(ps[4][:, hh * 128:(hh + 1) * 128],
                                       s[Gb][:, hh * 128:(hh + 1) * 128],
                                       s[Hb][:, hh * 128:(hh + 1) * 128], True, True), "eT")
            p.wait("vector", "eT", et)
            p.op("vector", lambda e: e.tensor_tensor(v3(s["f1"][:, :]), b3(s["idbf"]), v3(s["db10"][:, :]), ALU.subtract))
            p.op("vector", lambda e, Hb=Hb: e.tensor_add(s["f1"][:, :], s["f1"][:, :], s[Hb][:, :]))
            ev = p.op("vector", lambda e, Pb=Pb: e.tensor_sub(s[Pb][:, :], s["f1"][:, :], ps[4][:, :]), "eV")
            p.wait("tensor", "eV", ev)
            et = None
            for hh in range(4):
                et = p.op("tensor", mm(ps[3][:, hh * 128:(hh + 1) * 128],
                                       s["db9"][:, hh * 128:(hh + 1) * 128],
                                       s[Pb][:, hh * 128:(hh + 1) * 128], True, True), "eT")
            p.wait("vector", "eT", et)
            ev = p.op("vector", lambda e: e.tensor_copy(s["db2"][:, :], ps[3][:, :]), "eV")
            p.wait("tensor", "eV", ev)
            ev_par = {}
            for hh in range(4):
                pa, pb = (0, 1) if hh % 2 == 0 else (4, 5)
                if hh >= 2:
                    p.wait("tensor", "eV", ev_par[hh - 2])
                et = p.op("tensor", mm(ps[pa][:, 0:DVH], s["db2"][:, hh * 128:(hh + 1) * 128],
                                       s["rhsv"][:, hh, :], True, True), "eT")
                et = p.op("tensor", mm(ps[pb][:, 0:DKH], s["db2"][:, hh * 128:(hh + 1) * 128],
                                       brk[:, hh, :], True, True), "eT")
                p.wait("vector", "eT", et)
                p.op("vector", lambda e, hh=hh, pa=pa: e.tensor_copy(s["W1"][:, hh, :], ps[pa][:, 0:DVH]))
                ev = p.op("vector", lambda e, hh=hh, pb=pb: e.tensor_copy(s["Kt"][:, hh, :], ps[pb][:, 0:DKH]), "eV")
                ev_par[hh] = ev
            p.wait("tensor", "eV", ev)
            rv_used[0] = p.cnt["eT"]

            for hh in range(4):
                et = p.op("tensor", mm(ps[0][:, 0:128], s["Kt"][:, hh, 0:128],
                                       s["db1"][:, hh * 128:(hh + 1) * 128], True, True), "eT")
                et = p.op("tensor", mm(ps[1][0:64, 0:128], s["Kt"][:, hh, 128:192],
                                       s["db1"][:, hh * 128:(hh + 1) * 128], True, True), "eT")
                p.wait("vector", "eT", et)
                sls = head_slices(hh)
                (tl0, a0, a1) = sls[0]
                (tl1, b0, b1) = sls[1]
                n0 = a1 - a0
                p.op("vector", lambda e, tl0=tl0, a0=a0, n0=n0, bkq=bkq: e.tensor_sub(
                    s["QeT"][0:n0, 0, :], bkq[a0:a0 + n0, 12 + tl0, :], ps[0][0:n0, 0:128]))
                if n0 < 128:
                    p.op("vector", lambda e, tl1=tl1, b0=b0, n0=n0, bkq=bkq: e.tensor_sub(
                        s["QeT"][n0:128, 0, :], bkq[b0:b0 + 128 - n0, 12 + tl1, :], ps[0][n0:128, 0:128]))
                    ev = p.op("vector", lambda e, tl1=tl1, n0=n0, bkq=bkq: e.tensor_sub(
                        s["QeT"][0:64, 1, :], bkq[128 - n0:64 + 128 - n0, 12 + tl1, :], ps[1][0:64, 0:128]), "eV")
                else:
                    ev = p.op("vector", lambda e, tl1=tl1, b0=b0, bkq=bkq: e.tensor_sub(
                        s["QeT"][0:64, 1, :], bkq[b0:b0 + 64, 12 + tl1, :], ps[1][0:64, 0:128]), "eV")
                p.wait("tensor", "eV", ev)
                p.op("tensor", mm(ps[2][:, 0:DVH], s["QeT"][:, 0, :], s["S1b"][:, hh, :], True, False))
                p.op("tensor", mm(ps[2][:, 0:DVH], s["QeT"][0:64, 1, :], s["S2b"][0:64, hh, :], False, False))
                et = p.op("tensor", mm(ps[2][:, 0:DVH], s["db1"][:, hh * 128:(hh + 1) * 128],
                                       s["W1"][:, hh, :], False, True), "eT")
                p.wait("scalar", "eT", et)
                es = p.op("scalar", lambda e: e.activation(
                    s["q2"][:, 0:DVH], ps[2][:, 0:DVH], AF.Square, accum_out=s["lnc"][:, 10:11]), "eS")
                p.wait("vector", "eS", es)
                ev = p.op("vector", lambda e: e.tensor_scalar(
                    s["lnc"][:, 10:11], s["lnc"][:, 10:11], 1.0 / DVH, 1e-5, ALU.mult, ALU.add), "eV")
                p.wait("scalar", "eV", ev)
                es = p.op("scalar", lambda e: e.activation(
                    s["lnc"][:, 11:12], s["lnc"][:, 10:11], AF.Sqrt), "eS")
                p.wait("vector", "eS", es)
                p.op("vector", lambda e: e.reciprocal(s["lnc"][:, 12:13], s["lnc"][:, 11:12]))
                ev = p.op("vector", lambda e, hh=hh, cc=cc: e.scalar_tensor_tensor(
                    s["og"][:, hh * DVH:(hh + 1) * DVH], ps[2][:, 0:DVH],
                    s["lnc"][:, 12:13], s["gate"][:, cc % 2, hh * DVH:(hh + 1) * DVH],
                    ALU.mult, ALU.mult), "eV")
                p.wait("tensor", "eV", ev)
                et = p.op("tensor", mm(ps[0][:, 0:DKH], s["Kt"][:, hh, 0:128],
                                       bkr[:, hh, :], True, True), "eT")
                et = p.op("tensor", mm(ps[1][0:64, 0:DKH], s["Kt"][:, hh, 128:192],
                                       bkr[:, hh, :], True, True), "eT")
                p.wait("vector", "eT", et)
                p.op("vector", lambda e: e.tensor_scalar_mul(s["P2T"][:, 0, :], ps[0][:, 0:DKH], -1.0))
                ev = p.op("vector", lambda e: e.tensor_scalar_mul(s["P2T"][0:64, 1, :], ps[1][0:64, 0:DKH], -1.0), "eV")
                p.wait("tensor", "eV", ev)
                p.op("tensor", mm(pS[:, 0:DVH], bkr[:, hh, 0:128], s["W1"][:, hh, :], True, False))
                p.op("tensor", mm(pS[:, 0:DVH], s["P2T"][:, 0, 0:128], s["S1b"][:, hh, :], False, False))
                et = p.op("tensor", mm(pS[:, 0:DVH], s["P2T"][0:64, 1, 0:128], s["S2b"][0:64, hh, :], False, True), "eT")
                p.op("tensor", mm(ps[3][0:64, 0:DVH], bkr[:, hh, 128:192], s["W1"][:, hh, :], True, False))
                p.op("tensor", mm(ps[3][0:64, 0:DVH], s["P2T"][:, 0, 128:192], s["S1b"][:, hh, :], False, False))
                et = p.op("tensor", mm(ps[3][0:64, 0:DVH], s["P2T"][0:64, 1, 128:192], s["S2b"][0:64, hh, :], False, True), "eT")
                p.wait("vector", "eT", et)
                ev = p.op("vector", lambda e, hh=hh, cc=cc: e.scalar_tensor_tensor(
                    s["S1"][:, hh, :], s["S1"][:, hh, :], s["gcols"][:, cc, 16 + hh:17 + hh],
                    pS[:, 0:DVH], ALU.mult, ALU.add), "eV")
                ev = p.op("vector", lambda e, hh=hh, cc=cc: e.scalar_tensor_tensor(
                    s["S2"][0:64, hh, :], s["S2"][0:64, hh, :], s["gcols"][0:64, cc, 16 + hh:17 + hh],
                    ps[3][0:64, 0:DVH], ALU.mult, ALU.add), "eV")
                p.op("vector", lambda e, hh=hh: e.tensor_copy(s["S1b"][:, hh, :], s["S1"][:, hh, :]))
                ev = p.op("vector", lambda e, hh=hh: e.tensor_copy(s["S2b"][0:64, hh, :], s["S2"][0:64, hh, :]), "eV")
                p.wait("tensor", "eV", ev)
            p.wait("gpsimd", "eV", ev)
            dgo = p.op("gpsimd", lambda e, grp=grp, cc=cc: e.dma_start(
                out=og_sc[grp, cc * 128:(cc + 1) * 128, :], in_=s["og"][:, :]), "dG")
            p.wait("vector", "dG", dgo)
            d_lastuse[par] = (p.cnt["eT"], p.cnt["eV"])
            if cc == 3:
                d_og0 = p.cnt["dG"]
            if p1_stage >= 'e' and d_chunks == 8:
                if cc == 3:
                    e_pre(0, d_og0)
                elif cc == 4:
                    e_wso(0, [0, 1])
                elif cc == 5:
                    e_wso(0, [2, 3, 4])
                elif cc == 6:
                    e_wso(0, [5, 6, 7])
        d_og = p.cnt["dG"]

        if p1_stage < 'e':
            continue
        e_wout(0)
        e_pre(1, d_og)
        e_wso(1, list(range(8)))
        e_wout(1)

        if grp == 0:
            q_dgf()
            q_quant("w1", t1_sc, DH, D, CL_SW1, CL_ISW1)
        elif grp == 1:
            q_t1T()
            q_quant("w2", t2_sc, D, DH, CL_SW2, CL_ISW2)
        elif grp == 2:
            q_t2T()


    p.join_all()

    if p2_groups > 0:
        dg = None
        for cb in range(32):
            dg = p.op("gpsimd", lambda e, cb=cb: e.dma_start(
                out=s["t2T"][:, cb, :], in_=t2T_sc[cb * 128:(cb + 1) * 128, :]), "dG")
        p.wait("tensor", "dG", dg)

    for grp in range(p2_groups):
        p.join_all()
        b, gi = divmod(grp, 2)
        t0 = gi * G1
        nh = 0 if gi == 1 else 2
        ntok = G1 + nh

        for tt in range((ntok + 127) // 128):
            r0 = t0 + tt * 128
            rr = min(128, ntok - tt * 128)
            buf = tt % 2
            ds = p.op("sync", lambda e, b=b, r0=r0, rr=rr, buf=buf: e.dma_start(
                out=s["p2x"][0:rr, buf, :], in_=x2_sc[b, r0:r0 + rr, :]), "dS")
            p.wait("scalar", "dS", ds)
            p.wait("vector", "dS", ds)
            layernorm_tile(s["p2x"][:, buf, :], rr, None, s["p2l"], extra_minmax=True, scratch=s["p2n"])
            lc = s["p2l"]
            p.op("vector", lambda e, rr=rr: e.tensor_sub(lc[0:rr, 12:13], lc[0:rr, 8:9], lc[0:rr, 3:4]))
            p.op("vector", lambda e, rr=rr: e.tensor_sub(lc[0:rr, 13:14], lc[0:rr, 3:4], lc[0:rr, 9:10]))
            p.op("vector", lambda e, rr=rr: e.tensor_max(lc[0:rr, 12:13], lc[0:rr, 12:13], lc[0:rr, 13:14]))
            p.op("vector", lambda e, rr=rr: e.tensor_mul(lc[0:rr, 12:13], lc[0:rr, 12:13], lc[0:rr, 7:8]))
            p.op("vector", lambda e, rr=rr: e.tensor_scalar(lc[0:rr, 12:13], lc[0:rr, 12:13], 1e-5, None, ALU.max))
            p.op("vector", lambda e, rr=rr: e.reciprocal(lc[0:rr, 14:15], lc[0:rr, 12:13]))
            p.op("vector", lambda e, rr=rr: e.tensor_scalar_mul(lc[0:rr, 14:15], lc[0:rr, 14:15], 127.0))
            p.op("vector", lambda e, rr=rr: e.tensor_scalar_mul(lc[0:rr, 15:16], lc[0:rr, 12:13], 1.0 / 127.0))
            p.op("vector", lambda e, rr=rr: e.tensor_mul(lc[0:rr, 12:13], lc[0:rr, 7:8], lc[0:rr, 14:15]))
            p.op("vector", lambda e, rr=rr: e.tensor_mul(lc[0:rr, 13:14], lc[0:rr, 3:4], lc[0:rr, 12:13]))
            ev = p.op("vector", lambda e, rr=rr: e.tensor_tensor(
                lc[0:rr, 13:14], s["cols"][0:rr, CL_QB:CL_QB + 1], lc[0:rr, 13:14], ALU.subtract), "eV")
            p.wait("scalar", "eV", ev)
            es = p.op("scalar", lambda e, rr=rr, buf=buf: e.activation(
                s["p2n"][0:rr, :], s["p2x"][0:rr, buf, :], AF.Identity,
                bias=lc[0:rr, 13:14], scale=lc[0:rr, 12:13]), "eS")
            p.wait("vector", "eS", es)
            p.op("vector", lambda e, rr=rr: e.tensor_copy(s["i16q"][0:rr, :], s["p2n"][0:rr, :]))
            ev = p.op("vector", lambda e, rr=rr: e.tensor_scalar(
                s["p2q"][0:rr, :], s["i16q"][0:rr, :], -128.0, lc[0:rr, 15:16],
                ALU.add, ALU.mult), "eV")
            p.wait("sync", "eV", ev)
            ds = p.op("sync", lambda e, b=b, r0=r0, rr=rr: e.dma_start(
                out=xq1_sc[b, 64 + r0:64 + r0 + rr, :], in_=s["p2q"][0:rr, :]), "dS")
            p.wait("vector", "dS", ds)
            last_q = ds
        p.wait("sync", "dS", last_q)
        for dt_i in range(8):
            p.op("sync", lambda e, b=b, t0=t0, dt_i=dt_i: e.dma_start_transpose(
                out=s["xqT"][:, dt_i, :],
                in_=xq1_sc[b, t0:t0 + 1152, dt_i * 128:(dt_i + 1) * 128]), "dS")
        d_xqT = p.cnt["dS"]

        for oct in range(4):
            op_ = oct % 2
            if oct >= 2:
                p.wait("gpsimd", "eT", p.cnt["eT"])
            dgm = None
            for dt_i in range(8):
                dgm = p.op("gpsimd", lambda e, oct=oct, dt_i=dt_i, op_=op_: e.dma_start(
                    out=s["t1T"][:, op_, dt_i, :],
                    in_=t1T_sc[dt_i * 128:(dt_i + 1) * 128, oct * 1024:(oct + 1) * 1024]), "dG")
            p.wait("tensor", "dG", dgm)
            p.wait("tensor", "dS", d_xqT)
            for ctl in range(8):
                ct = oct * 8 + ctl
                cp = ct % 2
                if ctl >= 2 or oct > 0:
                    p.wait("sync", "eT", p.cnt["eT"])
                dd = None
                for j in range(3):
                    dd = p.op("sync", lambda e, ct=ct, j=j, cp=cp: e.dma_start(
                        out=s["dgf"][:, cp * 3 + j, :],
                        in_=dgf_sc[(ct * 3 + j) * 128:(ct * 3 + j + 1) * 128, :]), "dS")
                for dt_i in range(8):
                    p.op("tensor", mm(ps[0][:, :], s["t1T"][:, op_, dt_i, ct * 128 - oct * 1024:ct * 128 - oct * 1024 + 128],
                                      s["xqT"][:, dt_i, 63:575], dt_i == 0, dt_i == 7))
                for dt_i in range(8):
                    p.op("tensor", mm(ps[1][:, :], s["t1T"][:, op_, dt_i, ct * 128 - oct * 1024:ct * 128 - oct * 1024 + 128],
                                      s["xqT"][:, dt_i, 575:1087], dt_i == 0, dt_i == 7))
                et = None
                for dt_i in range(8):
                    et = p.op("tensor", mm(ps[2][:, 0:3], s["t1T"][:, op_, dt_i, ct * 128 - oct * 1024:ct * 128 - oct * 1024 + 128],
                                           s["xqT"][:, dt_i, 1087:1090], dt_i == 0, dt_i == 7),
                              "eT" if dt_i == 7 else None)
                p.wait("vector", "eT", et)
                p.op("vector", lambda e, cp=cp: e.tensor_copy(s["hh"][:, cp, 0:512], ps[0][:, :]))
                p.op("vector", lambda e, cp=cp: e.tensor_copy(s["hh"][:, cp, 512:1024], ps[1][:, :]))
                ev = p.op("vector", lambda e, cp=cp: e.tensor_copy(s["hh"][:, cp, 1024:1027], ps[2][:, 0:3]), "eV")
                p.wait("tensor", "eV", ev)
                p.wait("tensor", "dS", dd)
                for half in range(2):
                    et = None
                    for j in range(3):
                        et = p.op("tensor", mm(
                            ps[3][:, :], s["dgf"][:, cp * 3 + j, :],
                            s["hh"][:, cp, half * 512 + j:half * 512 + j + 512], j == 0, j == 2),
                            "eT" if j == 2 else None)
                    p.wait("scalar", "eT", et)
                    es = p.op("scalar", lambda e, ct=ct: e.activation(
                        s["p2q"][:, 0:512], ps[3][:, :], AF.Silu,
                        bias=s["fcb"][:, ct:ct + 1], scale=s["cols"][:, CL_ISW1:CL_ISW1 + 1]), "eS")
                    p.wait("sync", "eS", es)
                    p.wait("tensor", "eS", es)
                    ds = p.op("sync", lambda e, b=b, ct=ct, t0=t0, half=half: e.dma_start(
                        out=hs_sc[b, ct * 128:(ct + 1) * 128, t0 + half * 512:t0 + (half + 1) * 512],
                        in_=s["p2q"][:, 0:512]), "dS")
                    p.wait("scalar", "dS", ds)
        d_hs = p.cnt["dS"]

        for tt in range(8):
            bufh = 0
            p.wait("sync", "dS", d_hs)
            p.wait("sync", "eV", p.cnt["eV"])
            p.wait("sync", "eS", p.cnt["eS"])
            ds = p.op("sync", lambda e, b=b, t0=t0, tt=tt, bufh=bufh: e.dma_start_transpose(
                out=s["hst"][:, bufh, :], in_=hs_sc[b, :, t0 + tt * 128:t0 + (tt + 1) * 128]), "dS")
            p.wait("vector", "dS", ds)
            p.op("vector", lambda e, bufh=bufh: e.scalar_tensor_tensor(
                s["qsc"][:, 0:2048], s["hst"][:, bufh, 0:2048], -1.0,
                s["hst"][:, bufh, 0:2048], ALU.mult, ALU.max))
            p.op("vector", lambda e, bufh=bufh: e.scalar_tensor_tensor(
                s["hqi"][:, 0:2048], s["hst"][:, bufh, 2048:4096], -1.0,
                s["hst"][:, bufh, 2048:4096], ALU.mult, ALU.max))
            p.op("vector", lambda e: e.tensor_max(
                s["qsc"][:, 0:2048], s["qsc"][:, 0:2048], s["hqi"][:, 0:2048]))
            p.op("vector", lambda e: e.tensor_max(
                s["qsc"][:, 0:1024], s["qsc"][:, 0:1024], s["qsc"][:, 1024:2048]))
            p.op("vector", lambda e: e.tensor_max(
                s["qsc"][:, 0:512], s["qsc"][:, 0:512], s["qsc"][:, 512:1024]))
            lc = s["p2l"]
            p.op("vector", lambda e: e.reduce_max(lc[:, 0:1], s["qsc"][:, 0:512], axis=AX.X, apply_absolute_value=True))
            p.op("vector", lambda e: e.tensor_scalar(lc[:, 0:1], lc[:, 0:1], 1e-5, None, ALU.max))
            p.op("vector", lambda e: e.reciprocal(lc[:, 1:2], lc[:, 0:1]))
            p.op("vector", lambda e: e.tensor_scalar_mul(lc[:, 2:3], lc[:, 1:2], 127.0))
            p.op("vector", lambda e: e.tensor_scalar_mul(lc[:, 3:4], lc[:, 0:1], 1.0 / 127.0))
            ev = p.op("vector", lambda e, tt=tt: e.tensor_tensor(
                lc[:, 16 + tt:17 + tt], lc[:, 3:4], s["cols"][:, CL_ISW2:CL_ISW2 + 1], ALU.mult), "eV")
            p.wait("scalar", "eV", ev)
            for qq in range(4):
                if qq > 0:
                    p.wait("scalar", "eV", ev)
                es = p.op("scalar", lambda e, bufh=bufh, qq=qq: e.activation(
                    s["p2n"][:, :], s["hst"][:, bufh, qq * 1024:(qq + 1) * 1024], AF.Identity,
                    bias=s["cols"][:, CL_QB:CL_QB + 1], scale=lc[:, 2:3]), "eS")
                p.wait("vector", "eS", es)
                p.op("vector", lambda e: e.tensor_copy(s["i16t"][:, :], s["p2n"][:, :]))
                ev = p.op("vector", lambda e, qq=qq: e.tensor_scalar(
                    s["hqi"][:, qq * 1024:(qq + 1) * 1024], s["i16t"][:, :],
                    -128.0, None, ALU.add), "eV")
            p.wait("sync", "eV", ev)
            ds = p.op("sync", lambda e, b=b, t0=t0, tt=tt: e.dma_start(
                out=hq_sc[b, t0 + tt * 128:t0 + (tt + 1) * 128, :], in_=s["hqi"][:, :]), "dS")
            p.wait("vector", "dS", ds)
            p.wait("scalar", "dS", ds)
        d_hq = p.cnt["dS"]

        for tp in range(4):
            p.wait("scalar", "dS", d_hq)
            p.wait("scalar", "eT", p.cnt["eT"])
            da = None
            for cb in range(32):
                da = p.op("scalar", lambda e, b=b, t0=t0, tp=tp, cb=cb: e.dma_start_transpose(
                    out=s["hqT"][:, cb, :],
                    in_=hq_sc[b, t0 + tp * 256:t0 + (tp + 1) * 256, cb * 128:(cb + 1) * 128]), "dA")
            p.wait("tensor", "dA", da)
            et = None
            for two in range(2):
                for nhf in range(2):
                    for cb in range(32):
                        et = p.op("tensor", mm(
                            ps[two * 2 + nhf][:, :],
                            s["hqT"][:, cb, two * 128:(two + 1) * 128],
                            s["t2T"][:, cb, nhf * 512:(nhf + 1) * 512],
                            cb == 0, cb == 31), "eT" if cb == 31 else None)
            p.wait("vector", "eT", et)
            for two in range(2):
                tt = tp * 2 + two
                rbase = t0 + tt * 128
                p.op("vector", lambda e, two=two, tt=tt: e.tensor_scalar(
                    s["y2"][:, 0:512], ps[two * 2][:, :], s["p2l"][:, 16 + tt:17 + tt], None, ALU.mult))
                ev = p.op("vector", lambda e, two=two, tt=tt: e.tensor_scalar(
                    s["y2"][:, 512:1024], ps[two * 2 + 1][:, :], s["p2l"][:, 16 + tt:17 + tt], None, ALU.mult), "eV")
                ds = p.op("sync", lambda e, b=b, rbase=rbase: e.dma_start(
                    out=s["p2x"][:, 0, :], in_=x2_sc[b, rbase:rbase + 128, :]), "dS")
                p.wait("scalar", "eV", ev)
                p.wait("vector", "eV", ev)
                layernorm_tile(s["y2"][:, :], 128, None, s["p2l"], scratch=s["p2n"])
                p.wait("vector", "dS", ds)
                p.op("vector", lambda e: e.tensor_scalar(
                    s["y2"][:, :], s["y2"][:, :], s["p2l"][:, 3:4], s["p2l"][:, 7:8],
                    ALU.subtract, ALU.mult))
                ev = p.op("vector", lambda e: e.tensor_add(s["y2"][:, :], s["y2"][:, :], s["p2x"][:, 0, :]), "eV")
                p.wait("sync", "eV", ev)
                ds = p.op("sync", lambda e, b=b, rbase=rbase: e.dma_start(
                    out=out_d[b, rbase:rbase + 128, :], in_=s["y2"][:, :]), "dS")
                p.wait("vector", "dS", ds)
                p.wait("scalar", "dS", ds)
                p.wait("tensor", "dS", ds)

    p.wait("sync", "dS", p.cnt["dS"])
    p.wait("gpsimd", "dG", p.cnt["dG"])
    p.wait("scalar", "dA", p.cnt["dA"])

    with nc.Block() as block:
        p.replay(block)
    return nc


_NC_CACHE = [None]


def kernel(**inputs):
    if _NC_CACHE[0] is None:
        _NC_CACHE[0] = build_program()
    nc = _NC_CACHE[0]
    x = np.ascontiguousarray(np.asarray(inputs["L0_attn_x"], dtype=np.float32))
    wmap = {
        "W_in": "L0_attn_W_in", "conv_x_w": "L0_attn_conv_x_w",
        "conv_x_b": "L0_attn_conv_x_b", "conv_z_w": "L0_attn_conv_z_w",
        "conv_z_b": "L0_attn_conv_z_b", "W_q": "L0_attn_W_q",
        "W_k": "L0_attn_W_k", "W_v": "L0_attn_W_v", "W_g": "L0_attn_W_g",
        "W_gk": "L0_attn_W_gk", "W_b": "L0_attn_W_b", "b_b": "L0_attn_b_b",
        "A_log": "L0_attn_A_log", "dt_bias": "L0_attn_dt_bias",
        "g_norm_w": "L0_attn_g_norm_w", "W_so": "L0_attn_W_so",
        "W_out": "L0_attn_W_out", "w1": "L0_ffn_w1", "conv_w": "L0_ffn_conv_w",
        "conv_b": "L0_ffn_conv_b", "w2": "L0_ffn_w2",
    }
    weights = {k: np.ascontiguousarray(np.asarray(inputs[v], dtype=np.float32))
               for k, v in wmap.items()}
    in_maps = []
    for i in range(8):
        m = {"x": x[2 * i:2 * i + 2]}
        m.update(weights)
        in_maps.append(m)
    res = run_bass_kernel_spmd(nc, in_maps, core_ids=list(range(8)))
    out = np.concatenate([r["out"] for r in res.results], axis=0)
    return out.reshape(16, L, D).astype(np.float32)


if __name__ == "__main__":
    nc = build_program()
    print("built:", nc.n_instructions(), "instructions")


# version 04
# speedup vs baseline: 1.1109x; 1.0782x over previous
import numpy as np
import ml_dtypes
import concourse.bass as bass
import concourse.mybir as mybir
from concourse.bass_utils import run_bass_kernel_spmd

F32 = mybir.dt.float32
BF16 = mybir.dt.bfloat16
I16 = mybir.dt.int16
AF = mybir.ActivationFunctionType
ALU = mybir.AluOpType
AX = mybir.AxisListType
NPBF = ml_dtypes.bfloat16

BC, L, D = 2, 2048, 1024
H, DKH, DVH = 4, 192, 384
KD, VD, DI, DH = 768, 1536, 2048, 4096
C = 128
G1 = 1024
NG = BC * L // G1
PK = 2 * KD + 2 * VD + 8
BIG = 1.0e30

SC_KN, SC_QN, SC_QG, SC_RK, SC_KR, SC_RV, SC_GT = 0, KD, 2 * KD, 3 * KD, 4 * KD, 5 * KD, 5 * KD + VD
SCW = 5 * KD + 2 * VD

CL_D, CL_QB, CL_WQB, CL_SW1, CL_ISW1, CL_SW2, CL_ISW2, CL_ZERO, CL_T0, CL_T1 = range(10)


def pieces(c0, c1, width=512):
    out = []
    c = c0
    while c < c1:
        b = c // width
        hi = min(c1, (b + 1) * width)
        out.append((b, c - b * width, hi - c, c))
        c = hi
    return out


def head_slices(h, dk=DKH):
    out = []
    a, b = dk * h, dk * (h + 1)
    while a < b:
        t = a // 128
        hi = min(b, (t + 1) * 128)
        out.append((t, a - t * 128, hi - t * 128))
        a = hi
    return out


class P:
    ENGINES = ("sync", "gpsimd", "tensor", "vector", "scalar")

    def __init__(self, nc):
        self.nc = nc
        self.ev = {e: [] for e in self.ENGINES}
        self.cnt = {}
        self.lw = {e: {} for e in self.ENGINES}
        self.sems = {}
        for s in ("dS", "dG", "dA", "eT", "eV", "eS"):
            self.sems[s] = nc.alloc_semaphore(name=s)
            self.cnt[s] = 0

    AUTO_RAW = {"vector": "eV", "scalar": "eS"}

    def op(self, eng, fn, inc=None, amt=None):
        own = self.AUTO_RAW.get(eng)
        if inc in ("dS", "dG", "dA"):
            own = None
        if own is not None and eng != "gpsimd":
            prev = self.cnt[own]
            self.wait(eng, own, prev)
            if inc is None or inc == own:
                inc, amt = own, 1
            else:
                self.cnt[own] += 1
                ownv = self.cnt[own]
                sem_o = self.sems[own]
                self.cnt[inc] += amt if amt else (16 if inc in ("dS", "dG", "dA") else 1)
                sem_i = self.sems[inc]
                amt_i = amt if amt else (16 if inc in ("dS", "dG", "dA") else 1)

                def wrapped2(e, fn=fn):
                    ins = fn(e)
                    ins.then_inc(sem_i, amt_i)
                    ins.then_inc(sem_o, 1)
                self.ev[eng].append(wrapped2)
                return self.cnt[inc]
        if inc is None:
            self.ev[eng].append(fn)
            return None
        if amt is None:
            amt = 16 if inc in ("dS", "dG", "dA") else 1
        self.cnt[inc] += amt
        sem = self.sems[inc]

        def wrapped(e, fn=fn, sem=sem, amt=amt):
            fn(e).then_inc(sem, amt)
        self.ev[eng].append(wrapped)
        return self.cnt[inc]

    def wait(self, eng, semname, val):
        if not val:
            return
        if self.lw[eng].get(semname, 0) >= val:
            return
        self.lw[eng][semname] = val
        sem = self.sems[semname]
        self.ev[eng].append(lambda e, sem=sem, val=val: e.wait_ge(sem, val))

    def join_all(self):
        m = dict(self.cnt)
        for eng in self.ENGINES:
            for s, v in m.items():
                self.wait(eng, s, v)

    def replay(self, block):
        for name in self.ENGINES:
            evs = self.ev[name]

            def body(e, evs=evs):
                for fn in evs:
                    fn(e)
            getattr(block, name)(body)


def build_program(p0_only=False, p1_groups=NG, p2_groups=NG, p1_stage='e', d_chunks=8, dbg_out=False):
    nc = bass.Bass("TRN2", target_bir_lowering=False,
                   detect_race_conditions=False)

    ein = {}
    for name, shape in [
            ("x", (BC, L, D)), ("W_in", (D, DI)),
            ("conv_x_w", (D, 1, 4)), ("conv_x_b", (D,)),
            ("conv_z_w", (D, 1, 4)), ("conv_z_b", (D,)),
            ("W_q", (D, KD)), ("W_k", (D, KD)), ("W_v", (D, VD)),
            ("W_g", (D, VD)), ("W_gk", (D, H)), ("W_b", (D, H)),
            ("b_b", (H,)), ("A_log", (H,)), ("dt_bias", (H,)),
            ("g_norm_w", (DVH,)), ("W_so", (VD, D)), ("W_out", (DI, D)),
            ("w1", (DH, D)), ("conv_w", (DH, 1, 3)), ("conv_b", (DH,)),
            ("w2", (D, DH))]:
        ein[name] = nc.dram_tensor(name, list(shape), F32, kind="ExternalInput").ap()
    out_d = nc.dram_tensor("out", [BC, L, D], F32, kind="ExternalOutput").ap()

    def scr(name, shape, dt):
        kind = "ExternalOutput" if dbg_out else "Internal"
        return nc.dram_tensor(name, list(shape), dt, kind=kind).ap()

    win_sc = scr("win_sc", (D, DI), BF16)
    wpk_sc = scr("wpk_sc", (D, PK), BF16)
    wso_sc = scr("wso_sc", (VD, D), BF16)
    wout_sc = scr("wout_sc", (DI, D), BF16)
    t1_sc = scr("t1_sc", (DH, D), BF16)
    t2_sc = scr("t2_sc", (D, DH), BF16)
    t1T_sc = scr("t1T_sc", (D, DH), BF16)
    t2T_sc = scr("t2T_sc", (DH, D), BF16)
    dgf_sc = scr("dgf_sc", (96 * 128, 128), BF16)
    xn1_sc = scr("xn1_sc", (BC, L + 256, D), BF16)
    scC = scr("scC", (NG, G1, SCW), BF16)
    zs_sc = scr("zs_sc", (NG, 8, 128, G1), BF16)
    og_sc = scr("og_sc", (NG, G1, VD), BF16)
    x2_sc = scr("x2_sc", (BC, L, D), F32)
    xq1_sc = scr("xq1_sc", (BC, L + 128, D), BF16)
    hs_sc = scr("hs_sc", (BC, DH, L), BF16)
    hq_sc = scr("hq_sc", (BC, L, DH), BF16)

    eye = np.eye(128, dtype=np.float32)
    dm32 = np.zeros((128, 128), np.float32)
    for i in range(4):
        dm32[i * 32:(i + 1) * 32, i * 32:(i + 1) * 32] = 1.0
    consts_np = {
        "c_idbf": eye.astype(NPBF),
        "c_idf": eye,
        "c_triu": np.triu(np.ones((128, 128), np.float32)),
        "c_trils": np.tril(np.ones((128, 128), np.float32), -1),
        "c_ones": np.ones((128, 128), np.float32),
        "c_ltm": np.where(np.tril(np.ones((128, 128), np.float32), -1) > 0, 0.0, -BIG).astype(np.float32),
        "c_utm": np.where(np.triu(np.ones((128, 128), np.float32), 0) > 0, 0.0, -BIG).astype(np.float32),
        "c_dm": dm32,
        "c_nom": (dm32 - 1.0).astype(np.float32),
    }
    sel4 = np.zeros((4, 512), np.float32)
    for i in range(4):
        sel4[i, i * 128:(i + 1) * 128] = 1.0
    consts_np["c_sel"] = sel4
    const_t = {k: nc.inline_tensor(v, name=k) for k, v in consts_np.items()}

    p = P(nc)
    s = {}

    def alloc(name, shape, dt):
        s[name] = nc.alloc_sbuf_tensor(name, list(shape), dt)
        return s[name]

    for nm, sh, dt in [
            ("idbf", (128, 128), BF16), ("idf", (128, 128), F32),
            ("triu", (128, 128), F32), ("trils", (128, 128), F32),
            ("ones", (128, 128), F32), ("ltm", (128, 128), F32),
            ("utm", (128, 128), F32), ("dm", (128, 128), F32),
            ("nom", (128, 128), F32),
            ("cols", (128, 16), F32), ("i16c", (128, 2), I16),
            ("negA", (128, H), F32), ("dtb", (128, H), F32),
            ("bbt", (128, H), F32), ("gnw", (128, DVH), F32),
            ("cxb", (128, 16), F32), ("fcb", (128, 32), F32),
            ("cw", (128, 64), F32), ("fcw", (128, 96), F32),
            ("sel", (4, 512), F32),
            ("dga", (128, 64 * 128), BF16)]:
        alloc(nm, sh, dt)

    resident_end = nc.sbuf_base
    arena0 = (resident_end + 31) & ~31
    cur = [arena0]

    def place(name, shape, dt):
        nbytes = int(np.prod(shape[1:])) * mybir.dt.size(dt)
        off = (cur[0] + 31) & ~31
        cur[0] = off + nbytes
        s[name] = nc.alloc_sbuf_tensor_at(name, list(shape), dt, offset=off)
        return s[name]

    place("xw", (128, 2, D), F32)
    place("xn", (128, 2, D), BF16)
    place("xnT", (128, 8, 1152), BF16)
    place("xz", (128, 2, 1027), BF16)
    place("xs", (128, 8, G1), BF16)
    place("wp", (128, 16384), BF16)
    place("kqT", (128, 18, 128), BF16)
    place("kqtok", (128, 5 * KD), BF16)
    place("vt", (128, VD), BF16)
    place("gate", (128, 2, VD), BF16)
    place("rhsv", (128, H, DVH), BF16)
    place("rhsk", (128, H, DKH), BF16)
    place("kr", (128, H, DKH), BF16)
    place("q2", (128, 512), F32)
    place("qkc", (128, 48), F32)
    place("gcols", (128, 8, 28), F32)
    place("cT", (4, 8, 128), F32)
    place("R", (128, 512), F32)
    place("f1", (128, 512), F32)
    for i in range(11):
        place(f"db{i}", (128, 512), BF16)
    place("W1", (128, H, DVH), BF16)
    place("Kt", (128, H, DKH), BF16)
    place("P2T", (128, 2, DKH), BF16)
    place("QeT", (128, 2, 128), BF16)
    place("S1", (128, H, DVH), F32)
    place("S2", (64, H, DVH), F32)
    place("S1b", (128, H, DVH), BF16)
    place("S2b", (64, H, DVH), BF16)
    place("og", (128, VD), BF16)
    place("att", (128, D), F32)
    place("lnc", (128, 16), F32)
    p1_end = cur[0]
    cur[0] = nc.lookup_mloc(s["xnT"]).addr
    place("ogT", (128, 12, 512), BF16)
    place("ybf", (128, 8, 512), BF16)
    place("zsE", (128, 8, 512), BF16)
    assert cur[0] <= nc.lookup_mloc(s["wp"]).addr, "stage-E overlay overflow"

    cur[0] = arena0
    place("xqT", (128, 8, 1152), BF16)
    place("hh", (128, 2, 1027), BF16)
    place("hst", (128, 1, DH), BF16)
    place("hqi", (128, DH), BF16)
    place("i16t", (128, 1024), I16)
    place("qsc", (128, 2048), BF16)
    place("hqT", (128, 32, 256), BF16)
    place("dgf", (128, 6, 128), BF16)
    place("t1T", (128, 2, 8, 1024), BF16)
    place("t2T", (128, 32, 1024), BF16)
    place("p2x", (128, 2, D), F32)
    place("p2n", (128, D), F32)
    place("p2q", (128, D), BF16)
    place("i16q", (128, D), I16)
    place("y2", (128, D), F32)
    place("p2l", (128, 56), F32)
    place("p2m", (128, D), F32)
    place("fq", (128, 512), BF16)
    place("lni", (128, 10), F32)
    p2_end = cur[0]

    cur[0] = p1_end
    place("qprep", (128, PK), BF16)
    q_end = cur[0]
    cur[0] = p1_end
    place("wf32", (128, 1024), F32)
    place("wi16", (128, 1024), I16)
    place("wbf", (128, 1024), BF16)
    place("wcol", (128, 4), F32)
    cur[0] = max(cur[0], q_end)
    place("qtp", (128, 1024), BF16)
    place("kqT2", (128, 18, 128), BF16)
    place("rhsk2", (128, H, DKH), BF16)
    place("kr2", (128, H, DKH), BF16)
    assert cur[0] < nc.sbuf_top - 32, f"Q/D2 overflow {cur[0]} vs {nc.sbuf_top}"

    assert max(p1_end, p2_end) < nc.sbuf_top - 64, \
        f"SBUF overflow {max(p1_end, p2_end)} vs {nc.sbuf_top}"

    ps = [nc.alloc_psum_tensor(f"pb{i}", [128, 512], F32) for i in range(6)]
    pS = nc.alloc_psum_tensor("pS", [128, 512], F32)
    pT = nc.alloc_psum_tensor("pT", [128, 1024], BF16)

    def mm(out, lhsT, rhs, start, stop):
        return lambda e: e.matmul(out, lhsT, rhs, start=start, stop=stop)

    def b3(tile128, nrep=4, w=128):
        return tile128[:, :].unsqueeze(1).to_broadcast([128, nrep, w])

    def colb(col_ap, w=128):
        n = col_ap.shape[1]
        return col_ap.unsqueeze(2).to_broadcast([128, n, w])

    def v3(tile, n=4, w=128):
        return tile.rearrange("p (a b) -> p a b", a=n)

    for dst, key in [("idbf", "c_idbf"), ("idf", "c_idf"), ("triu", "c_triu"),
                     ("trils", "c_trils"), ("ones", "c_ones"), ("ltm", "c_ltm"),
                     ("utm", "c_utm"), ("dm", "c_dm"), ("nom", "c_nom"),
                     ("sel", "c_sel")]:
        p.op("sync", lambda e, d=s[dst], t=const_t[key]: e.dma_start(out=d[:, :], in_=t.ap()[:, :]), "dS")
    p.op("sync", lambda e: e.dma_start(out=s["negA"][:, :], in_=ein["A_log"].partition_broadcast(128)), "dS")
    p.op("sync", lambda e: e.dma_start(out=s["dtb"][:, :], in_=ein["dt_bias"].partition_broadcast(128)), "dS")
    p.op("sync", lambda e: e.dma_start(out=s["bbt"][:, :], in_=ein["b_b"].partition_broadcast(128)), "dS")
    p.op("sync", lambda e: e.dma_start(out=s["gnw"][:, :], in_=ein["g_norm_w"].partition_broadcast(128)), "dS")
    p.op("sync", lambda e: e.dma_start(out=s["cxb"][:, 0:8], in_=ein["conv_x_b"].rearrange("(t p) -> p t", p=128), allow_slow_non_contiguous=True), "dS")
    p.op("sync", lambda e: e.dma_start(out=s["cxb"][:, 8:16], in_=ein["conv_z_b"].rearrange("(t p) -> p t", p=128), allow_slow_non_contiguous=True), "dS")
    p.op("sync", lambda e: e.dma_start(out=s["fcb"][:, :], in_=ein["conv_b"].rearrange("(t p) -> p t", p=128), allow_slow_non_contiguous=True), "dS")
    p.op("sync", lambda e: e.dma_start(out=s["cw"][:, 0:32].rearrange("p (t k) -> p t k", k=4), in_=ein["conv_x_w"].rearrange("(t p) o k -> p t (o k)", p=128), allow_slow_non_contiguous=True), "dS")
    p.op("sync", lambda e: e.dma_start(out=s["cw"][:, 32:64].rearrange("p (t k) -> p t k", k=4), in_=ein["conv_z_w"].rearrange("(t p) o k -> p t (o k)", p=128), allow_slow_non_contiguous=True), "dS")
    p.op("sync", lambda e: e.dma_start(out=s["fcw"][:, :].rearrange("p (t k) -> p t k", k=3), in_=ein["conv_w"].rearrange("(t p) o k -> p t (o k)", p=128), allow_slow_non_contiguous=True), "dS")
    d0 = p.cnt["dS"]
    p.wait("vector", "dS", d0)
    p.wait("scalar", "dS", d0)

    p.op("vector", lambda e: e.memset(s["cols"][:, CL_T0:CL_T0 + 1], 0.7))
    p.op("vector", lambda e: e.tensor_copy(s["i16c"][:, 0:1], s["cols"][:, CL_T0:CL_T0 + 1]))
    p.op("vector", lambda e: e.tensor_copy(s["cols"][:, CL_D:CL_D + 1], s["i16c"][:, 0:1]))
    p.op("vector", lambda e: e.tensor_scalar(s["cols"][:, CL_QB:CL_QB + 1], s["cols"][:, CL_D:CL_D + 1], -0.5, 128.5, ALU.mult, ALU.add))
    p.op("vector", lambda e: e.tensor_scalar(s["cols"][:, CL_WQB:CL_WQB + 1], s["cols"][:, CL_D:CL_D + 1], -0.5, 4.5, ALU.mult, ALU.add))
    p.op("vector", lambda e: e.memset(s["cols"][:, CL_ZERO:CL_ZERO + 1], 0.0))
    p.op("scalar", lambda e: e.activation(s["negA"][:, :], s["negA"][:, :], AF.Exp), "eS")
    p.wait("vector", "eS", p.cnt["eS"])
    p.op("vector", lambda e: e.tensor_scalar_mul(s["negA"][:, :], s["negA"][:, :], -1.0))

    for idx in range(64):
        p.op("vector", lambda e, idx=idx: e.tensor_tensor(
            s["dga"][:, idx * 128:(idx + 1) * 128], s["idf"][:, :],
            s["cw"][:, idx:idx + 1].to_broadcast([128, 128]), ALU.mult))
    ev_dga = p.cnt["eV"]

    ev = p.op("vector", lambda e: e.memset(s["xn"][:, :, :], 0.0), "eV")
    p.wait("sync", "eV", ev)
    for b in range(BC):
        for r0 in range(0, L + 256, 128):
            p.op("sync", lambda e, b=b, r0=r0: e.dma_start(out=xn1_sc[b, r0:r0 + 128, :], in_=s["xn"][:, 0, :]), "dS")
        for r0 in range(0, L + 128, 128):
            p.op("sync", lambda e, b=b, r0=r0: e.dma_start(out=xq1_sc[b, r0:r0 + 128, :], in_=s["xn"][:, 0, :]), "dS")

    d_zero = p.cnt["dS"]

    def cast_rows(dst_sc, src_ap, rows, cols):
        for r0 in range(0, rows, 128):
            dg = p.op("gpsimd", lambda e, r0=r0, cols=cols, src_ap=src_ap: e.dma_start(
                out=s["qprep"][:, 0:cols], in_=src_ap[r0:r0 + 128, 0:cols]), "dG")
            p.wait("gpsimd", "dG", dg)
            p.op("gpsimd", lambda e, r0=r0, cols=cols, dst_sc=dst_sc: e.dma_start(
                out=dst_sc[r0:r0 + 128, 0:cols], in_=s["qprep"][:, 0:cols]), "dG")
    cast_rows(win_sc, ein["W_in"], D, DI)
    c_win = p.cnt["dG"]
    segs = [("W_q", 0, KD), ("W_k", KD, KD), ("W_v", 2 * KD, VD),
            ("W_g", 2 * KD + VD, VD), ("W_gk", 2 * KD + 2 * VD, H),
            ("W_b", 2 * KD + 2 * VD + H, H)]
    for r0 in range(0, D, 128):
        dg = None
        for nm, c0, cn in segs:
            dg = p.op("gpsimd", lambda e, r0=r0, nm=nm, c0=c0, cn=cn: e.dma_start(
                out=s["qprep"][:, c0:c0 + cn], in_=ein[nm][r0:r0 + 128, 0:cn]), "dG")
        p.wait("gpsimd", "dG", dg)
        p.op("gpsimd", lambda e, r0=r0: e.dma_start(
            out=wpk_sc[r0:r0 + 128, :], in_=s["qprep"][:, :]), "dG")
    c_wpk = p.cnt["dG"]
    cast_rows(wso_sc, ein["W_so"], VD, D)
    c_wso = p.cnt["dG"]
    cast_rows(wout_sc, ein["W_out"], DI, D)
    c_cast = c_wout = p.cnt["dG"]

    q_items = []
    qst = {"ev": 0, "es": 0, "ds": 0, "dsl": 0}

    def _qi(fn):
        q_items.append(fn)

    def q_build():
        def dgf_item(idx):
            def it():
                p.wait("vector", "dG", c_cast)
                p.wait("vector", "dS", qst["ds"])
                ev = p.op("vector", lambda e: e.tensor_tensor(
                    s["wbf"][:, 0:128], s["idf"][:, :],
                    s["fcw"][:, idx:idx + 1].to_broadcast([128, 128]), ALU.mult), "eV")
                p.wait("sync", "eV", ev)
                qst["ds"] = p.op("sync", lambda e: e.dma_start(
                    out=dgf_sc[idx * 128:(idx + 1) * 128, :], in_=s["wbf"][:, 0:128]), "dS")
            return it
        for idx in range(96):
            _qi(dgf_item(idx))

        def quant_items(wname, tdst, rows, cols, CLS, CLI):
            src = ein[wname]
            tiles = [(r0, c0) for r0 in range(0, rows, 128) for c0 in range(0, cols, 1024)]

            def init_it():
                p.wait("sync", "dG", c_cast)
                p.wait("vector", "dG", c_cast)
                p.wait("scalar", "dG", c_cast)
                qst["ev"] = p.op("vector", lambda e: e.memset(s["wcol"][:, 0:1], 0.0), "eV")
            _qi(init_it)

            def p1_item(r0, c0):
                def it():
                    p.wait("sync", "eV", qst["ev"])
                    ds = p.op("sync", lambda e: e.dma_start(
                        out=s["wf32"][:, :], in_=src[r0:r0 + 128, c0:c0 + 1024]), "dS")
                    p.wait("vector", "dS", ds)
                    p.op("vector", lambda e: e.reduce_sum(
                        s["wcol"][:, 1:2], s["wf32"][:, :], axis=AX.X, apply_absolute_value=True))
                    qst["ev"] = p.op("vector", lambda e: e.tensor_add(
                        s["wcol"][:, 0:1], s["wcol"][:, 0:1], s["wcol"][:, 1:2]), "eV")
                return it
            for (r0, c0) in tiles:
                _qi(p1_item(r0, c0))

            def mid_it():
                p.wait("tensor", "eV", qst["ev"])
                et = p.op("tensor", mm(ps[0][:, 0:1], s["ones"][:, :], s["wcol"][:, 0:1], True, True), "eT")
                p.wait("vector", "eT", et)
                n = rows * cols
                p.op("vector", lambda e: e.tensor_scalar(
                    s["cols"][:, CL_T1:CL_T1 + 1], ps[0][:, 0:1], 1.0 / n, 1e-5, ALU.mult, ALU.max))
                p.op("vector", lambda e: e.reciprocal(s["cols"][:, CLS:CLS + 1], s["cols"][:, CL_T1:CL_T1 + 1]))
                ev = p.op("vector", lambda e: e.tensor_copy(
                    s["cols"][:, CLI:CLI + 1], s["cols"][:, CL_T1:CL_T1 + 1]), "eV")
                p.wait("scalar", "eV", ev)
                p.wait("tensor", "eV", ev)
                qst["ev"] = ev
            _qi(mid_it)

            def p2_item(r0, c0):
                def it():
                    p.wait("sync", "eV", qst["ev"])
                    p.wait("vector", "dS", qst.get("ds2", 0))
                    ds = p.op("sync", lambda e: e.dma_start(
                        out=s["wf32"][:, :], in_=src[r0:r0 + 128, c0:c0 + 1024]), "dS")
                    p.wait("scalar", "dS", ds)
                    es = p.op("scalar", lambda e: e.activation(
                        s["wf32"][:, :], s["wf32"][:, :], AF.Identity,
                        bias=s["cols"][:, CL_WQB:CL_WQB + 1], scale=s["cols"][:, CLS:CLS + 1]), "eS")
                    p.wait("vector", "eS", es)
                    p.op("vector", lambda e: e.tensor_copy(s["wi16"][:, :], s["wf32"][:, :]))
                    p.op("vector", lambda e: e.tensor_scalar(
                        s["wf32"][:, :], s["wi16"][:, :], -4.0, -1.0, ALU.add, ALU.max))
                    ev = p.op("vector", lambda e: e.tensor_scalar(
                        s["wbf"][:, :], s["wf32"][:, :], 1.0, None, ALU.min), "eV")
                    p.wait("sync", "eV", ev)
                    p.op("sync", lambda e: e.dma_start(
                        out=tdst[r0:r0 + 128, c0:c0 + 1024], in_=s["wbf"][:, :]), "dS")
                    qst["ev"] = ev
                    qst["ds2"] = p.cnt["dS"]
                return it
            for (r0, c0) in tiles:
                _qi(p2_item(r0, c0))

        def t1T_item(dt_i, oct, first):
            def it():
                if first:
                    p.wait("sync", "dS", qst["ds2"])
                p.wait("sync", "dS", qst["dsl"])
                ds = p.op("sync", lambda e: e.dma_start_transpose(
                    out=s["qtp"][:, :], in_=t1_sc[oct * 1024:(oct + 1) * 1024,
                                                  dt_i * 128:(dt_i + 1) * 128]), "dS")
                p.wait("sync", "dS", ds)
                qst["dsl"] = p.op("sync", lambda e: e.dma_start(
                    out=t1T_sc[dt_i * 128:(dt_i + 1) * 128, oct * 1024:(oct + 1) * 1024],
                    in_=s["qtp"][:, :]), "dS")
            return it

        def t2T_item(ct, first):
            def it():
                if first:
                    p.wait("sync", "dS", qst["ds2"])
                p.wait("sync", "dS", qst["dsl"])
                ds = p.op("sync", lambda e: e.dma_start_transpose(
                    out=s["qtp"][:, :], in_=t2_sc[:, ct * 128:(ct + 1) * 128]), "dS")
                p.wait("sync", "dS", ds)
                qst["dsl"] = p.op("sync", lambda e: e.dma_start(
                    out=t2T_sc[ct * 128:(ct + 1) * 128, :], in_=s["qtp"][:, :]), "dS")
            return it

        quant_items("w1", t1_sc, DH, D, CL_SW1, CL_ISW1)
        for dt_i in range(8):
            for oct in range(4):
                _qi(t1T_item(dt_i, oct, dt_i == 0 and oct == 0))
        quant_items("w2", t2_sc, D, DH, CL_SW2, CL_ISW2)
        for ct in range(32):
            _qi(t2T_item(ct, ct == 0))
    q_build()
    q_pos = [0]

    def q_step(n):
        while n > 0 and q_pos[0] < len(q_items):
            q_items[q_pos[0]]()
            q_pos[0] += 1
            n -= 1

    def q_flush():
        q_step(len(q_items))

    if p0_only:
        p1_groups = 0
        p2_groups = 0
    def layernorm_tile(xap, rr, outap, lnc, extra_minmax=False, scratch=None):
        sq = scratch if scratch is not None else s["q2"]
        es = p.op("scalar", lambda e: e.activation(
            sq[0:rr, 0:512], xap[0:rr, 0:512], AF.Square, accum_out=lnc[0:rr, 0:1]), "eS")
        es = p.op("scalar", lambda e: e.activation(
            sq[0:rr, 0:512], xap[0:rr, 512:1024], AF.Square, accum_out=lnc[0:rr, 1:2]), "eS")
        p.wait("vector", "eS", es)
        p.op("vector", lambda e: e.reduce_sum(lnc[0:rr, 2:3], xap[0:rr, :], axis=AX.X))
        if extra_minmax:
            p.op("vector", lambda e: e.reduce_max(lnc[0:rr, 8:9], xap[0:rr, :], axis=AX.X))
            p.op("vector", lambda e: e.tensor_reduce(lnc[0:rr, 9:10], xap[0:rr, :], axis=AX.X, op=ALU.min))
        p.op("vector", lambda e: e.tensor_scalar_mul(lnc[0:rr, 3:4], lnc[0:rr, 2:3], 1.0 / D))
        p.op("vector", lambda e: e.tensor_add(lnc[0:rr, 4:5], lnc[0:rr, 0:1], lnc[0:rr, 1:2]))
        p.op("vector", lambda e: e.tensor_scalar_mul(lnc[0:rr, 4:5], lnc[0:rr, 4:5], 1.0 / D))
        p.op("vector", lambda e: e.tensor_mul(lnc[0:rr, 5:6], lnc[0:rr, 3:4], lnc[0:rr, 3:4]))
        p.op("vector", lambda e: e.tensor_sub(lnc[0:rr, 4:5], lnc[0:rr, 4:5], lnc[0:rr, 5:6]))
        ev = p.op("vector", lambda e: e.tensor_scalar(lnc[0:rr, 4:5], lnc[0:rr, 4:5], 1e-5, None, ALU.add), "eV")
        p.wait("scalar", "eV", ev)
        es = p.op("scalar", lambda e: e.activation(lnc[0:rr, 6:7], lnc[0:rr, 4:5], AF.Sqrt), "eS")
        p.wait("vector", "eS", es)
        p.op("vector", lambda e: e.reciprocal(lnc[0:rr, 7:8], lnc[0:rr, 6:7]))
        if outap is not None:
            ev = p.op("vector", lambda e: e.tensor_scalar(
                outap[0:rr, :], xap[0:rr, :], lnc[0:rr, 3:4], lnc[0:rr, 7:8],
                ALU.subtract, ALU.mult), "eV")
            return ev
        return p.cnt["eV"]

    d_lastuse = {}
    rv_used = [None]
    for grp in range(p1_groups):
        p.wait("sync", "eV", p.cnt["eV"])
        p.wait("sync", "eT", p.cnt["eT"])
        p.wait("gpsimd", "eT", p.cnt["eT"])
        p.wait("tensor", "eS", p.cnt["eS"])
        p.wait("vector", "eT", p.cnt["eT"])
        b, gi = divmod(grp, 2)
        t0 = gi * G1
        nh = 0 if gi == 1 else 2
        ntok = G1 + nh

        last_w = None
        for tt in range((ntok + 127) // 128):
            r0 = t0 + tt * 128
            rr = min(128, ntok - tt * 128)
            buf = tt % 2
            p.wait("sync", "eV", p.lw["sync"].get("eV", 0))
            ds = p.op("sync", lambda e, b=b, r0=r0, rr=rr, buf=buf: e.dma_start(
                out=s["xw"][0:rr, buf, :], in_=ein["x"][b, r0:r0 + rr, :]), "dS")
            p.wait("scalar", "dS", ds)
            p.wait("vector", "dS", ds)
            ev = layernorm_tile(s["xw"][:, buf, :], rr, s["xn"][:, buf, :], s["lnc"])
            p.wait("sync", "eV", ev)
            last_w = p.op("sync", lambda e, b=b, r0=r0, rr=rr, buf=buf: e.dma_start(
                out=xn1_sc[b, 128 + r0:128 + r0 + rr, :], in_=s["xn"][0:rr, buf, :]), "dS")
            p.wait("vector", "dS", last_w)
        p.wait("sync", "dS", last_w)
        for dt_i in range(8):
            p.op("sync", lambda e, b=b, t0=t0, dt_i=dt_i: e.dma_start_transpose(
                out=s["xnT"][:, dt_i, :],
                in_=xn1_sc[b, 64 + t0:64 + t0 + 1152, dt_i * 128:(dt_i + 1) * 128]), "dS")
        d_xnT = p.cnt["dS"]

        if p1_stage < 'b':
            continue
        ev_zsw = []
        et_hist = {}
        if grp == 0:
            p.wait("gpsimd", "dG", c_win)
        for ct in range(16):
            par = ct % 2
            if ct >= 2:
                p.wait("gpsimd", "eT", et_hist[ct - 2])
            dgm = None
            for kt in range(8):
                dgm = p.op("gpsimd", lambda e, ct=ct, kt=kt, par=par: e.dma_start(
                    out=s["wp"][:, par * 1024 + kt * 128:par * 1024 + (kt + 1) * 128],
                    in_=win_sc[kt * 128:(kt + 1) * 128, ct * 128:(ct + 1) * 128]), "dG")
            p.wait("tensor", "dG", dgm)
            p.wait("tensor", "dS", d_xnT)
            if grp == 0 and ct == 0:
                p.wait("tensor", "dS", d0)
                p.wait("tensor", "eV", ev_dga)
            for kt in range(8):
                p.op("tensor", mm(ps[0][:, :], s["wp"][:, par * 1024 + kt * 128:par * 1024 + (kt + 1) * 128],
                                  s["xnT"][:, kt, 63:575], kt == 0, kt == 7))
            for kt in range(8):
                p.op("tensor", mm(ps[1][:, :], s["wp"][:, par * 1024 + kt * 128:par * 1024 + (kt + 1) * 128],
                                  s["xnT"][:, kt, 575:1087], kt == 0, kt == 7))
            et = None
            for kt in range(8):
                et = p.op("tensor", mm(ps[2][:, 0:3], s["wp"][:, par * 1024 + kt * 128:par * 1024 + (kt + 1) * 128],
                                       s["xnT"][:, kt, 1087:1090], kt == 0, kt == 7),
                          "eT" if kt == 7 else None)
            p.wait("vector", "eT", et)
            p.op("vector", lambda e, par=par: e.tensor_copy(s["xz"][:, par, 0:512], ps[0][:, :]))
            p.op("vector", lambda e, par=par: e.tensor_copy(s["xz"][:, par, 512:1024], ps[1][:, :]))
            ev = p.op("vector", lambda e, par=par: e.tensor_copy(s["xz"][:, par, 1024:1027], ps[2][:, 0:3]), "eV")
            p.wait("tensor", "eV", ev)
            if p1_stage < 'b2':
                continue
            for half in range(2):
                et = None
                for j in range(4):
                    et = p.op("tensor", mm(
                        ps[3][:, :], s["dga"][:, (ct * 4 + j) * 128:(ct * 4 + j + 1) * 128],
                        s["xz"][:, par, half * 512 + j:half * 512 + j + 512], j == 0, j == 3),
                        "eT" if j == 3 else None)
                p.wait("scalar", "eT", et)
                if p1_stage < 'b3':
                    continue
                if ct < 8:
                    es = p.op("scalar", lambda e, ct=ct, half=half: e.activation(
                        s["xs"][:, ct, half * 512:(half + 1) * 512], ps[3][:, :], AF.Silu,
                        bias=s["cxb"][:, ct:ct + 1], scale=1.0), "eS")
                    p.wait("tensor", "eS", es)
                else:
                    es = p.op("scalar", lambda e, ct=ct, half=half: e.activation(
                        s["vt"][:, half * 512:(half + 1) * 512] if half == 0 else s["vt"][:, 512:1024],
                        ps[3][:, :], AF.Silu,
                        bias=s["cxb"][:, ct:ct + 1], scale=1.0), "eS")
                    p.wait("sync", "eS", es)
                    p.wait("tensor", "eS", es)
                    dw = p.op("sync", lambda e, grp=grp, ct=ct, half=half: e.dma_start(
                        out=zs_sc[grp, ct - 8, :, half * 512:(half + 1) * 512],
                        in_=s["vt"][:, half * 512:(half + 1) * 512]), "dS")
                    ev_zsw.append(dw)
                    p.wait("scalar", "dS", dw)
                    p.wait("vector", "dS", dw)
            et_hist[ct] = p.cnt["eT"]
            q_step(3)
        ev_xs = p.cnt["eS"]

        if p1_stage < 'c':
            continue
        if grp == 0:
            p.wait("gpsimd", "dG", c_wpk)
        dgm = None
        for kt in range(8):
            dgm = p.op("gpsimd", lambda e, kt=kt: e.dma_start(
                out=s["wp"][:, 2048 + kt * 8:2048 + (kt + 1) * 8],
                in_=wpk_sc[kt * 128:(kt + 1) * 128, 4608:4616]), "dG")
        p.wait("tensor", "dG", dgm)
        p.wait("tensor", "eS", ev_xs)
        for cc in range(8):
            et = None
            for kt in range(8):
                et = p.op("tensor", mm(ps[3][:, 0:8], s["xs"][:, kt, cc * 128:(cc + 1) * 128],
                                       s["wp"][:, 2048 + kt * 8:2048 + (kt + 1) * 8],
                                       kt == 0, kt == 7), "eT" if kt == 7 else None)
            p.wait("vector", "eT", et)
            if p1_stage == 'cA':
                continue
            gc = s["gcols"]
            qc = s["qkc"]
            p.op("vector", lambda e, cc=cc: e.tensor_add(qc[:, 0:4], ps[3][:, 0:4], s["dtb"][:, :]))
            ev = p.op("vector", lambda e, cc=cc: e.tensor_add(qc[:, 4:8], ps[3][:, 4:8], s["bbt"][:, :]), "eV")
            p.wait("scalar", "eV", ev)
            es = p.op("scalar", lambda e: e.activation(qc[:, 8:12], qc[:, 0:4], AF.Abs), "eS")
            es = p.op("scalar", lambda e: e.activation(qc[:, 8:12], qc[:, 8:12], AF.Exp, scale=-1.0), "eS")
            es = p.op("scalar", lambda e: e.activation(qc[:, 8:12], qc[:, 8:12], AF.Ln, bias=1.0), "eS")
            es = p.op("scalar", lambda e: e.activation(qc[:, 12:16], qc[:, 0:4], AF.Relu), "eS")
            es = p.op("scalar", lambda e: e.activation(qc[:, 16:20], qc[:, 4:8], AF.Sigmoid), "eS")
            p.wait("vector", "eS", es)
            p.op("vector", lambda e: e.tensor_add(qc[:, 8:12], qc[:, 8:12], qc[:, 12:16]))
            ev = p.op("vector", lambda e: e.tensor_mul(qc[:, 20:24], qc[:, 8:12], s["negA"][:, :]), "eV")
            p.op("vector", lambda e, cc=cc: e.tensor_copy(gc[:, cc, 0:4], qc[:, 16:20]))
            ev = p.op("vector", lambda e, cc=cc: e.tensor_scalar_mul(gc[:, cc, 24:28], qc[:, 16:20], -1.0), "eV")
            p.wait("tensor", "eV", ev)
            if p1_stage == 'cB':
                continue
            et = p.op("tensor", mm(ps[4][:, 0:4], s["triu"][:, :], qc[:, 20:24], True, True), "eT")
            et = p.op("tensor", mm(ps[4][:, 4:8], s["trils"][:, :], qc[:, 20:24], True, True), "eT")
            et = p.op("tensor", mm(ps[4][:, 8:12], s["ones"][:, :], qc[:, 20:24], True, True), "eT")
            p.wait("vector", "eT", et)
            ev = p.op("vector", lambda e, cc=cc: e.tensor_copy(gc[:, cc, 12:16], ps[4][:, 0:4]), "eV")
            p.wait("scalar", "eV", ev)
            es = p.op("scalar", lambda e, cc=cc: e.activation(gc[:, cc, 4:8], ps[4][:, 0:4], AF.Exp), "eS")
            es = p.op("scalar", lambda e, cc=cc: e.activation(gc[:, cc, 8:12], ps[4][:, 4:8], AF.Exp), "eS")
            es = p.op("scalar", lambda e, cc=cc: e.activation(gc[:, cc, 16:20], ps[4][:, 8:12], AF.Exp), "eS")
            p.wait("vector", "eS", es)
            ev = p.op("vector", lambda e, cc=cc: e.tensor_mul(gc[:, cc, 20:24], gc[:, cc, 0:4], gc[:, cc, 4:8]), "eV")
            p.wait("tensor", "eV", ev)
            if p1_stage == 'cC':
                continue
            et = p.op("tensor", lambda e, cc=cc: e.transpose(pS[0:4, 0:128], gc[:, cc, 12:16], s["idf"][:, :]), "eT")
            p.wait("vector", "eT", et)
            ev = p.op("vector", lambda e, cc=cc: e.tensor_copy(s["cT"][0:4, cc, :], pS[0:4, 0:128]), "eV")
            p.wait("tensor", "eV", ev)
            q_step(2)

        if p1_stage < 'c1':
            continue
        p.wait("gpsimd", "eT", p.cnt["eT"])
        dgm = None
        for kt in range(8):
            dgm = p.op("gpsimd", lambda e, kt=kt: e.dma_start(
                out=s["wp"][:, kt * 1536:(kt + 1) * 1536],
                in_=wpk_sc[kt * 128:(kt + 1) * 128, 0:1536]), "dG")
        p.wait("tensor", "dG", dgm)
        evp = {0: p.cnt["eV"], 1: p.cnt["eV"]}
        for cc in range(8):
            off = 3 * (cc % 2)
            p.wait("tensor", "eV", evp[cc % 2])
            et = None
            for nb in range(3):
                for kt in range(8):
                    et = p.op("tensor", mm(ps[nb + off][:, :], s["xs"][:, kt, cc * 128:(cc + 1) * 128],
                                           s["wp"][:, kt * 1536 + nb * 512:kt * 1536 + (nb + 1) * 512],
                                           kt == 0, kt == 7), "eT" if kt == 7 else None)
            p.wait("scalar", "eT", et)
            qb = (cc % 2) * 24
            es = None
            split_heads = []
            for hh in range(8):
                pl = pieces(hh * DKH, (hh + 1) * DKH)
                for i, (bk, lo, ln, ab) in enumerate(pl):
                    tgt = qb + (hh if i == 0 else 8 + hh)
                    if i > 0:
                        split_heads.append(hh)
                    es = p.op("scalar", lambda e, bk=bk + off, lo=lo, ln=ln, tgt=tgt: e.activation(
                        s["q2"][:, 0:ln], ps[bk][:, lo:lo + ln], AF.Square,
                        accum_out=s["qkc"][:, tgt:tgt + 1]), "eS")
            p.wait("vector", "eS", es)
            for hh in split_heads:
                p.op("vector", lambda e, hh=hh + qb: e.tensor_add(
                    s["qkc"][:, hh:hh + 1], s["qkc"][:, hh:hh + 1], s["qkc"][:, 8 + hh:9 + hh]))
            ev = p.op("vector", lambda e, qb=qb: e.tensor_scalar(
                s["qkc"][:, qb:qb + 8], s["qkc"][:, qb:qb + 8], 1e-6, None, ALU.add), "eV")
            p.wait("scalar", "eV", ev)
            es = p.op("scalar", lambda e, qb=qb: e.activation(s["qkc"][:, qb + 16:qb + 24], s["qkc"][:, qb:qb + 8], AF.Sqrt), "eS")
            p.wait("vector", "eS", es)
            ev = p.op("vector", lambda e, qb=qb: e.reciprocal(s["qkc"][:, qb + 8:qb + 16], s["qkc"][:, qb + 16:qb + 24]), "eV")
            p.op("vector", lambda e, cc=cc, qb=qb: e.tensor_mul(s["qkc"][:, qb + 16:qb + 20], s["qkc"][:, qb + 8:qb + 12], s["gcols"][:, cc, 4:8]))
            ev = p.op("vector", lambda e, cc=cc, qb=qb: e.tensor_copy(s["qkc"][:, qb + 20:qb + 24], s["qkc"][:, qb + 12:qb + 16]), "eV")
            for hh in range(8):
                dst0 = (SC_KN + (hh - 4) * DKH) if hh >= 4 else (SC_QN + hh * DKH)
                rs_col = qb + (12 + (hh - 4) if hh >= 4 else 8 + hh)
                for (bk, lo, ln, ab) in pieces(hh * DKH, (hh + 1) * DKH):
                    offc = ab - hh * DKH
                    ev = p.op("vector", lambda e, bk=bk + off, lo=lo, ln=ln, dst0=dst0, offc=offc, rs_col=rs_col: e.tensor_scalar(
                        s["kqtok"][:, dst0 + offc:dst0 + offc + ln], ps[bk][:, lo:lo + ln],
                        s["qkc"][:, rs_col:rs_col + 1], None, ALU.mult), "eV")
                    if hh < 4:
                        ev = p.op("vector", lambda e, bk=bk + off, lo=lo, ln=ln, hh=hh, offc=offc, qb=qb: e.tensor_scalar(
                            s["kqtok"][:, SC_QG + hh * DKH + offc:SC_QG + hh * DKH + offc + ln],
                            ps[bk][:, lo:lo + ln], s["qkc"][:, qb + 16 + hh:qb + 17 + hh], None, ALU.mult), "eV")
            for hh in range(4):
                ev = p.op("vector", lambda e, hh=hh, cc=cc: e.tensor_scalar(
                    s["kqtok"][:, SC_RK + hh * DKH:SC_RK + (hh + 1) * DKH],
                    s["kqtok"][:, SC_KN + hh * DKH:SC_KN + (hh + 1) * DKH],
                    s["gcols"][:, cc, 20 + hh:21 + hh], None, ALU.mult), "eV")
                ev = p.op("vector", lambda e, hh=hh, cc=cc: e.tensor_scalar(
                    s["kqtok"][:, SC_KR + hh * DKH:SC_KR + (hh + 1) * DKH],
                    s["kqtok"][:, SC_KN + hh * DKH:SC_KN + (hh + 1) * DKH],
                    s["gcols"][:, cc, 8 + hh:9 + hh], None, ALU.mult), "eV")
            p.wait("sync", "eV", ev)
            ds = p.op("sync", lambda e, grp=grp, cc=cc: e.dma_start(
                out=scC[grp, cc * 128:(cc + 1) * 128, 0:5 * KD], in_=s["kqtok"][:, :]), "dS")
            p.wait("vector", "dS", ds)
            evp[cc % 2] = ev
            q_step(2)

        if p1_stage < 'c2':
            continue
        p.wait("gpsimd", "eT", p.cnt["eT"])
        dgm = None
        for kt in range(8):
            dgm = p.op("gpsimd", lambda e, kt=kt: e.dma_start(
                out=s["wp"][:, kt * 1536:(kt + 1) * 1536],
                in_=wpk_sc[kt * 128:(kt + 1) * 128, 1536:3072]), "dG")
        p.wait("tensor", "dG", dgm)
        evp = {0: p.cnt["eV"], 1: p.cnt["eV"]}
        for cc in range(8):
            off = 3 * (cc % 2)
            p.wait("tensor", "eV", evp[cc % 2])
            et = None
            for nb in range(3):
                for kt in range(8):
                    et = p.op("tensor", mm(ps[nb + off][:, :], s["xs"][:, kt, cc * 128:(cc + 1) * 128],
                                           s["wp"][:, kt * 1536 + nb * 512:kt * 1536 + (nb + 1) * 512],
                                           kt == 0, kt == 7), "eT" if kt == 7 else None)
            p.wait("vector", "eT", et)
            ev = None
            for hh in range(4):
                for (bk, lo, ln, ab) in pieces(hh * DVH, (hh + 1) * DVH):
                    offc = ab
                    ev = p.op("vector", lambda e, bk=bk + off, lo=lo, ln=ln, offc=offc, hh=hh, cc=cc: e.tensor_scalar(
                        s["vt"][:, offc:offc + ln], ps[bk][:, lo:lo + ln],
                        s["gcols"][:, cc, 0 + hh:1 + hh], None, ALU.mult), "eV")
            p.wait("sync", "eV", ev)
            ds = p.op("sync", lambda e, grp=grp, cc=cc: e.dma_start(
                out=scC[grp, cc * 128:(cc + 1) * 128, SC_RV:SC_RV + VD], in_=s["vt"][:, :]), "dS")
            p.wait("vector", "dS", ds)
            p.wait("scalar", "dS", ds)
            evp[cc % 2] = ev
            q_step(2)

        if p1_stage < 'c3':
            continue
        p.wait("gpsimd", "eT", p.cnt["eT"])
        dgm = None
        for kt in range(8):
            dgm = p.op("gpsimd", lambda e, kt=kt: e.dma_start(
                out=s["wp"][:, kt * 1536:(kt + 1) * 1536],
                in_=wpk_sc[kt * 128:(kt + 1) * 128, 3072:4608]), "dG")
        p.wait("tensor", "dG", dgm)
        esp = {0: p.cnt["eS"], 1: p.cnt["eS"]}
        for cc in range(8):
            off = 3 * (cc % 2)
            p.wait("tensor", "eS", esp[cc % 2])
            et = None
            for nb in range(3):
                for kt in range(8):
                    et = p.op("tensor", mm(ps[nb + off][:, :], s["xs"][:, kt, cc * 128:(cc + 1) * 128],
                                           s["wp"][:, kt * 1536 + nb * 512:kt * 1536 + (nb + 1) * 512],
                                           kt == 0, kt == 7), "eT" if kt == 7 else None)
            p.wait("scalar", "eT", et)
            gslot = s["gate"][:, cc % 2, :]
            es = None
            for nb in range(3):
                es = p.op("scalar", lambda e, nb=nb + off, off=off, gslot=gslot: e.activation(
                    gslot[:, (nb - off) * 512:(nb - off + 1) * 512], ps[nb][:, :], AF.Silu), "eS")
            p.wait("vector", "eS", es)
            ev = p.op("vector", lambda e, gslot=gslot: e.tensor_tensor(
                v3(gslot[:, :], 4, DVH), v3(gslot[:, :], 4, DVH),
                s["gnw"][:, :].unsqueeze(1).to_broadcast([128, 4, DVH]), ALU.mult), "eV")
            p.wait("sync", "eV", ev)
            ds = p.op("sync", lambda e, grp=grp, cc=cc, gslot=gslot: e.dma_start(
                out=scC[grp, cc * 128:(cc + 1) * 128, SC_GT:SC_GT + VD], in_=gslot[:, :]), "dS")
            p.wait("vector", "dS", ds)
            p.wait("scalar", "dS", ds)
            esp[cc % 2] = es
            q_step(2)
        d_stageC = p.cnt["dS"]

        if p1_stage < 'd':
            continue

        d_zsw = ev_zsw[-1] if ev_zsw else 0
        e_state = {}

        def e_pre(half, dwait, grp=grp):
            p.wait("sync", "dG", dwait)
            p.wait("sync", "dS", d_zsw)
            p.wait("sync", "eT", p.cnt["eT"])
            for f in range(12):
                p.op("sync", lambda e, grp=grp, half=half, f=f: e.dma_start_transpose(
                    out=s["ogT"][:, f, :],
                    in_=og_sc[grp, half * 512:(half + 1) * 512, f * 128:(f + 1) * 128]), "dS")
            e_state["ogT"] = p.cnt["dS"]
            for ct in range(8):
                p.op("sync", lambda e, grp=grp, ct=ct, half=half: e.dma_start(
                    out=s["zsE"][:, ct, :], in_=zs_sc[grp, ct, :, half * 512:(half + 1) * 512]), "dS")
            e_state["zsE"] = p.cnt["dS"]

        def e_wso(half, dts):
            d_ogT = e_state["ogT"]
            if dts[0] == 0:
                p.wait("gpsimd", "dG", c_wso)
                p.wait("gpsimd", "eT", p.cnt["eT"])
            for dt_i in dts:
                dgm = None
                for f in range(12):
                    dgm = p.op("gpsimd", lambda e, dt_i=dt_i, f=f: e.dma_start(
                        out=s["wp"][:, f * 128:(f + 1) * 128],
                        in_=wso_sc[f * 128:(f + 1) * 128, dt_i * 128:(dt_i + 1) * 128]), "dG")
                p.wait("tensor", "dG", dgm)
                p.wait("tensor", "dS", d_ogT)
                et = None
                for f in range(12):
                    et = p.op("tensor", mm(ps[0][:, :], s["wp"][:, f * 128:(f + 1) * 128],
                                           s["ogT"][:, f, :], f == 0, f == 11), "eT" if f == 11 else None)
                p.wait("vector", "eT", et)
                ev = p.op("vector", lambda e, dt_i=dt_i: e.tensor_copy(s["ybf"][:, dt_i, :], ps[0][:, :]), "eV")
                p.wait("gpsimd", "eV", ev)
                p.wait("tensor", "eV", ev)

        def e_wout(half, b=b, t0=t0):
            d_zsE = e_state["zsE"]
            p.wait("gpsimd", "dG", c_wout)
            p.wait("gpsimd", "eT", p.cnt["eT"])
            dgm = None
            for kt in range(16):
                dgm = p.op("gpsimd", lambda e, kt=kt: e.dma_start(
                    out=s["wp"][:, kt * 1024:(kt + 1) * 1024],
                    in_=wout_sc[kt * 128:(kt + 1) * 128, :]), "dG")
            p.wait("tensor", "dG", dgm)
            p.wait("tensor", "dS", d_zsE)
            for tc in range(4):
                et = None
                for nhf in range(2):
                    for kt in range(16):
                        lhs = s["ybf"][:, kt, tc * 128:(tc + 1) * 128] if kt < 8 \
                            else s["zsE"][:, kt - 8, tc * 128:(tc + 1) * 128]
                        et = p.op("tensor", mm(
                            ps[2 + nhf][:, :], lhs,
                            s["wp"][:, kt * 1024 + nhf * 512:kt * 1024 + (nhf + 1) * 512],
                            kt == 0, kt == 15), "eT" if kt == 15 else None)
                p.wait("scalar", "eT", et)
                es = p.op("scalar", lambda e: e.copy(s["att"][:, 0:512], ps[2][:, :]), "eS")
                es = p.op("scalar", lambda e: e.copy(s["att"][:, 512:1024], ps[3][:, :]), "eS")
                rbase = t0 + half * 512 + tc * 128
                ds = p.op("sync", lambda e, b=b, rbase=rbase: e.dma_start(
                    out=s["xw"][:, 0, :], in_=ein["x"][b, rbase:rbase + 128, :]), "dS")
                p.wait("vector", "eS", es)
                p.wait("scalar", "eS", es)
                layernorm_tile(s["att"][:, :], 128, None, s["lnc"])
                p.wait("vector", "dS", ds)
                p.op("vector", lambda e: e.tensor_scalar(
                    s["att"][:, :], s["att"][:, :],
                    s["lnc"][:, 3:4], s["lnc"][:, 7:8], ALU.subtract, ALU.mult))
                ev = p.op("vector", lambda e: e.tensor_add(
                    s["att"][:, :], s["att"][:, :], s["xw"][:, 0, :]), "eV")
                p.wait("sync", "eV", ev)
                ds = p.op("sync", lambda e, b=b, rbase=rbase: e.dma_start(
                    out=x2_sc[b, rbase:rbase + 128, :], in_=s["att"][:, :]), "dS")
                p.wait("scalar", "dS", ds)
                p.wait("tensor", "dS", ds)

        if gi == 0:
            ev = p.op("vector", lambda e: e.memset(s["S1"][:, :, :], 0.0), "eV")
            ev = p.op("vector", lambda e: e.memset(s["S2"][:, :, :], 0.0), "eV")
            ev = p.op("vector", lambda e: e.memset(s["S1b"][:, :, :], 0.0), "eV")
            ev = p.op("vector", lambda e: e.memset(s["S2b"][:, :, :], 0.0), "eV")
            p.wait("tensor", "eV", ev)

        for cc in range(d_chunks):
            par = cc % 2
            bkq = s["kqT"] if par == 0 else s["kqT2"]
            brk = s["rhsk"] if par == 0 else s["rhsk2"]
            bkr = s["kr"] if par == 0 else s["kr2"]
            p.wait("sync", "dS", d_stageC)
            lu = d_lastuse.get(par)
            if lu is not None:
                p.wait("sync", "eT", lu[0])
                p.wait("sync", "eV", lu[1])
            for f in range(18):
                p.op("sync", lambda e, grp=grp, cc=cc, f=f, bkq=bkq: e.dma_start_transpose(
                    out=bkq[:, f, :],
                    in_=scC[grp, cc * 128:(cc + 1) * 128, f * 128:(f + 1) * 128]), "dS")
            p.op("sync", lambda e, grp=grp, cc=cc, brk=brk: e.dma_start(
                out=brk[:, :, :], in_=scC[grp, cc * 128:(cc + 1) * 128, SC_RK:SC_RK + KD]), "dS")
            p.op("sync", lambda e, grp=grp, cc=cc, bkr=bkr: e.dma_start(
                out=bkr[:, :, :], in_=scC[grp, cc * 128:(cc + 1) * 128, SC_KR:SC_KR + KD]), "dS")
            if rv_used[0] is not None:
                p.wait("sync", "eT", rv_used[0])
            p.op("sync", lambda e, grp=grp, cc=cc: e.dma_start(
                out=s["rhsv"][:, :, :], in_=scC[grp, cc * 128:(cc + 1) * 128, SC_RV:SC_RV + VD]), "dS")
            dl = p.op("sync", lambda e, grp=grp, cc=cc: e.dma_start(
                out=s["gate"][:, cc % 2, :], in_=scC[grp, cc * 128:(cc + 1) * 128, SC_GT:SC_GT + VD]), "dS")
            p.wait("tensor", "dS", dl)

            for hh in range(4):
                sls = head_slices(hh)
                first = True
                for (tl, p0, p1) in sls:
                    p.op("tensor", mm(ps[0][:, hh * 128:(hh + 1) * 128],
                                      bkq[p0:p1, tl, :], bkq[p0:p1, tl, :],
                                      first, (tl, p0, p1) == sls[-1]))
                    first = False
                first = True
                for (tl, p0, p1) in sls:
                    p.op("tensor", mm(ps[1][:, hh * 128:(hh + 1) * 128],
                                      bkq[p0:p1, tl, :], bkq[p0:p1, 6 + tl, :],
                                      first, (tl, p0, p1) == sls[-1]))
                    first = False
            et = None
            for hh in range(4):
                et = p.op("tensor", mm(ps[2][:, hh * 128:(hh + 1) * 128],
                                       s["sel"][0:4, hh * 128:(hh + 1) * 128],
                                       s["cT"][0:4, cc, :], True, True),
                          "eT")
            p.wait("vector", "eT", et)
            ev = p.op("vector", lambda e, cc=cc: e.tensor_tensor(
                v3(s["R"][:, :]), v3(ps[2][:, :]),
                s["gcols"][:, cc, 12:16].unsqueeze(2).to_broadcast([128, 4, 128]),
                ALU.subtract), "eV")
            p.wait("scalar", "eV", ev)
            es = p.op("scalar", lambda e: e.activation(s["f1"][:, :], s["R"][:, :], AF.Identity, scale=-1.0), "eS")
            p.wait("vector", "eS", es)
            ev = p.op("vector", lambda e: e.tensor_tensor(
                v3(s["f1"][:, :]), v3(s["f1"][:, :]), b3(s["ltm"]), ALU.add), "eV")
            p.wait("scalar", "eV", ev)
            es = p.op("scalar", lambda e: e.activation(s["f1"][:, :], s["f1"][:, :], AF.Exp), "eS")
            p.wait("vector", "eS", es)
            p.op("vector", lambda e: e.tensor_mul(s["f1"][:, :], s["f1"][:, :], ps[0][:, :]))
            ev = p.op("vector", lambda e, cc=cc: e.tensor_tensor(
                v3(s["db0"][:, :]), v3(s["f1"][:, :]),
                s["gcols"][:, cc, 24:28].unsqueeze(2).to_broadcast([128, 4, 128]),
                ALU.mult), "eV")
            p.op("vector", lambda e: e.tensor_tensor(
                v3(s["f1"][:, :]), v3(s["R"][:, :]), b3(s["utm"]), ALU.add))
            ev2 = p.op("vector", lambda e: e.engine_nop(), "eV")
            p.wait("scalar", "eV", ev2)
            es = p.op("scalar", lambda e: e.activation(s["f1"][:, :], s["f1"][:, :], AF.Exp), "eS")
            p.wait("vector", "eS", es)
            ev = p.op("vector", lambda e: e.tensor_mul(s["f1"][:, :], s["f1"][:, :], ps[1][:, :]), "eV")
            ev = p.op("vector", lambda e: e.tensor_copy(s["db1"][:, :], s["f1"][:, :]), "eV")
            p.op("vector", lambda e: e.tensor_tensor(v3(s["db2"][:, :]), v3(s["db0"][:, :]), b3(s["dm"]), ALU.mult))
            ev = p.op("vector", lambda e: e.tensor_tensor(v3(s["db3"][:, :]), v3(s["db0"][:, :]), b3(s["nom"]), ALU.mult), "eV")
            p.wait("tensor", "eV", ev)
            et = None
            for hh in range(4):
                et = p.op("tensor", lambda e, hh=hh: e.transpose(
                    pT[:, hh * 128:(hh + 1) * 128], s["db2"][:, hh * 128:(hh + 1) * 128], s["idbf"][:, :]), "eT")
            p.wait("vector", "eT", et)
            ev = p.op("vector", lambda e: e.tensor_copy(s["db4"][:, :], pT[:, 0:512]), "eV")
            ev = p.op("vector", lambda e: e.tensor_tensor(v3(s["db5"][:, :]), v3(s["db4"][:, :]), b3(s["idbf"]), ALU.add), "eV")
            p.wait("tensor", "eV", ev)
            Ga, Ha, Pa, Gb, Hb, Pb = "db2", "db4", "db5", "db6", "db7", "db8"
            for lev in range(1, 5):
                et = None
                for hh in range(4):
                    et = p.op("tensor", mm(ps[3][:, hh * 128:(hh + 1) * 128],
                                           s[Ha][:, hh * 128:(hh + 1) * 128],
                                           s[Ga][:, hh * 128:(hh + 1) * 128], True, True), "eT")
                if lev < 4:
                    for hh in range(4):
                        et = p.op("tensor", mm(ps[4][:, hh * 128:(hh + 1) * 128],
                                               s[Ga][:, hh * 128:(hh + 1) * 128],
                                               s[Ha][:, hh * 128:(hh + 1) * 128], True, True), "eT")
                p.wait("vector", "eT", et)
                p.op("vector", lambda e, Gb=Gb: e.tensor_copy(s[Gb][:, :], ps[3][:, :]))
                if lev < 4:
                    p.op("vector", lambda e, Hb=Hb: e.tensor_copy(s[Hb][:, :], ps[4][:, :]))
                ev = p.op("vector", lambda e: e.engine_nop(), "eV")
                p.wait("tensor", "eV", ev)
                et = None
                for hh in range(4):
                    p.op("tensor", mm(ps[5][:, hh * 128:(hh + 1) * 128],
                                      s["idbf"][:, :], s[Pa][:, hh * 128:(hh + 1) * 128], True, False))
                    et = p.op("tensor", mm(ps[5][:, hh * 128:(hh + 1) * 128],
                                           s[Gb][:, hh * 128:(hh + 1) * 128],
                                           s[Pa][:, hh * 128:(hh + 1) * 128], False, True), "eT")
                p.wait("vector", "eT", et)
                ev = p.op("vector", lambda e, Pb=Pb: e.tensor_copy(s[Pb][:, :], ps[5][:, :]), "eV")
                p.wait("tensor", "eV", ev)
                Ga, Gb = Gb, Ga
                Ha, Hb = Hb, Ha
                Pa, Pb = Pb, Pa
            TdT = Pa
            et = None
            for hh in range(4):
                et = p.op("tensor", lambda e, hh=hh, TdT=TdT: e.transpose(
                    pT[:, hh * 128:(hh + 1) * 128], s[TdT][:, hh * 128:(hh + 1) * 128], s["idbf"][:, :]), "eT")
            p.wait("vector", "eT", et)
            ev = p.op("vector", lambda e: e.tensor_copy(s["db9"][:, :], pT[:, 0:512]), "eV")
            p.wait("tensor", "eV", ev)
            et = None
            for hh in range(4):
                et = p.op("tensor", mm(ps[3][:, hh * 128:(hh + 1) * 128],
                                       s["db3"][:, hh * 128:(hh + 1) * 128],
                                       s[TdT][:, hh * 128:(hh + 1) * 128], True, True), "eT")
            p.wait("vector", "eT", et)
            ev = p.op("vector", lambda e: e.tensor_copy(s["db10"][:, :], ps[3][:, :]), "eV")
            p.wait("tensor", "eV", ev)
            et = None
            for hh in range(4):
                et = p.op("tensor", lambda e, hh=hh: e.transpose(
                    pT[:, hh * 128:(hh + 1) * 128], s["db10"][:, hh * 128:(hh + 1) * 128], s["idbf"][:, :]), "eT")
            p.wait("vector", "eT", et)
            ev = p.op("vector", lambda e, Gb=Gb: e.tensor_copy(s[Gb][:, :], pT[:, 0:512]), "eV")
            p.wait("tensor", "eV", ev)
            et = None
            for hh in range(4):
                et = p.op("tensor", mm(ps[3][:, hh * 128:(hh + 1) * 128],
                                       s[Gb][:, hh * 128:(hh + 1) * 128],
                                       s["db10"][:, hh * 128:(hh + 1) * 128], True, True), "eT")
            p.wait("vector", "eT", et)
            ev = p.op("vector", lambda e, Hb=Hb: e.tensor_copy(s[Hb][:, :], ps[3][:, :]), "eV")
            p.wait("tensor", "eV", ev)
            et = None
            for hh in range(4):
                et = p.op("tensor", mm(ps[4][:, hh * 128:(hh + 1) * 128],
                                       s[Gb][:, hh * 128:(hh + 1) * 128],
                                       s[Hb][:, hh * 128:(hh + 1) * 128], True, True), "eT")
            p.wait("vector", "eT", et)
            p.op("vector", lambda e: e.tensor_tensor(v3(s["f1"][:, :]), b3(s["idbf"]), v3(s["db10"][:, :]), ALU.subtract))
            p.op("vector", lambda e, Hb=Hb: e.tensor_add(s["f1"][:, :], s["f1"][:, :], s[Hb][:, :]))
            ev = p.op("vector", lambda e, Pb=Pb: e.tensor_sub(s[Pb][:, :], s["f1"][:, :], ps[4][:, :]), "eV")
            p.wait("tensor", "eV", ev)
            et = None
            for hh in range(4):
                et = p.op("tensor", mm(ps[3][:, hh * 128:(hh + 1) * 128],
                                       s["db9"][:, hh * 128:(hh + 1) * 128],
                                       s[Pb][:, hh * 128:(hh + 1) * 128], True, True), "eT")
            p.wait("vector", "eT", et)
            ev = p.op("vector", lambda e: e.tensor_copy(s["db2"][:, :], ps[3][:, :]), "eV")
            p.wait("tensor", "eV", ev)
            ev_par = {}
            for hh in range(4):
                pa, pb = (0, 1) if hh % 2 == 0 else (4, 5)
                if hh >= 2:
                    p.wait("tensor", "eV", ev_par[hh - 2])
                et = p.op("tensor", mm(ps[pa][:, 0:DVH], s["db2"][:, hh * 128:(hh + 1) * 128],
                                       s["rhsv"][:, hh, :], True, True), "eT")
                et = p.op("tensor", mm(ps[pb][:, 0:DKH], s["db2"][:, hh * 128:(hh + 1) * 128],
                                       brk[:, hh, :], True, True), "eT")
                p.wait("vector", "eT", et)
                p.op("vector", lambda e, hh=hh, pa=pa: e.tensor_copy(s["W1"][:, hh, :], ps[pa][:, 0:DVH]))
                ev = p.op("vector", lambda e, hh=hh, pb=pb: e.tensor_copy(s["Kt"][:, hh, :], ps[pb][:, 0:DKH]), "eV")
                ev_par[hh] = ev
            p.wait("tensor", "eV", ev)
            rv_used[0] = p.cnt["eT"]

            for hh in range(4):
                et = p.op("tensor", mm(ps[0][:, 0:128], s["Kt"][:, hh, 0:128],
                                       s["db1"][:, hh * 128:(hh + 1) * 128], True, True), "eT")
                et = p.op("tensor", mm(ps[1][0:64, 0:128], s["Kt"][:, hh, 128:192],
                                       s["db1"][:, hh * 128:(hh + 1) * 128], True, True), "eT")
                p.wait("vector", "eT", et)
                sls = head_slices(hh)
                (tl0, a0, a1) = sls[0]
                (tl1, b0, b1) = sls[1]
                n0 = a1 - a0
                p.op("vector", lambda e, tl0=tl0, a0=a0, n0=n0, bkq=bkq: e.tensor_sub(
                    s["QeT"][0:n0, 0, :], bkq[a0:a0 + n0, 12 + tl0, :], ps[0][0:n0, 0:128]))
                if n0 < 128:
                    p.op("vector", lambda e, tl1=tl1, b0=b0, n0=n0, bkq=bkq: e.tensor_sub(
                        s["QeT"][n0:128, 0, :], bkq[b0:b0 + 128 - n0, 12 + tl1, :], ps[0][n0:128, 0:128]))
                    ev = p.op("vector", lambda e, tl1=tl1, n0=n0, bkq=bkq: e.tensor_sub(
                        s["QeT"][0:64, 1, :], bkq[128 - n0:64 + 128 - n0, 12 + tl1, :], ps[1][0:64, 0:128]), "eV")
                else:
                    ev = p.op("vector", lambda e, tl1=tl1, b0=b0, bkq=bkq: e.tensor_sub(
                        s["QeT"][0:64, 1, :], bkq[b0:b0 + 64, 12 + tl1, :], ps[1][0:64, 0:128]), "eV")
                p.wait("tensor", "eV", ev)
                p.op("tensor", mm(ps[2][:, 0:DVH], s["QeT"][:, 0, :], s["S1b"][:, hh, :], True, False))
                p.op("tensor", mm(ps[2][:, 0:DVH], s["QeT"][0:64, 1, :], s["S2b"][0:64, hh, :], False, False))
                et = p.op("tensor", mm(ps[2][:, 0:DVH], s["db1"][:, hh * 128:(hh + 1) * 128],
                                       s["W1"][:, hh, :], False, True), "eT")
                p.wait("scalar", "eT", et)
                es = p.op("scalar", lambda e: e.activation(
                    s["q2"][:, 0:DVH], ps[2][:, 0:DVH], AF.Square, accum_out=s["lnc"][:, 10:11]), "eS")
                p.wait("vector", "eS", es)
                ev = p.op("vector", lambda e: e.tensor_scalar(
                    s["lnc"][:, 10:11], s["lnc"][:, 10:11], 1.0 / DVH, 1e-5, ALU.mult, ALU.add), "eV")
                p.wait("scalar", "eV", ev)
                es = p.op("scalar", lambda e: e.activation(
                    s["lnc"][:, 11:12], s["lnc"][:, 10:11], AF.Sqrt), "eS")
                p.wait("vector", "eS", es)
                p.op("vector", lambda e: e.reciprocal(s["lnc"][:, 12:13], s["lnc"][:, 11:12]))
                ev = p.op("vector", lambda e, hh=hh, cc=cc: e.scalar_tensor_tensor(
                    s["og"][:, hh * DVH:(hh + 1) * DVH], ps[2][:, 0:DVH],
                    s["lnc"][:, 12:13], s["gate"][:, cc % 2, hh * DVH:(hh + 1) * DVH],
                    ALU.mult, ALU.mult), "eV")
                p.wait("tensor", "eV", ev)
                et = p.op("tensor", mm(ps[0][:, 0:DKH], s["Kt"][:, hh, 0:128],
                                       bkr[:, hh, :], True, True), "eT")
                et = p.op("tensor", mm(ps[1][0:64, 0:DKH], s["Kt"][:, hh, 128:192],
                                       bkr[:, hh, :], True, True), "eT")
                p.wait("vector", "eT", et)
                p.op("vector", lambda e: e.tensor_scalar_mul(s["P2T"][:, 0, :], ps[0][:, 0:DKH], -1.0))
                ev = p.op("vector", lambda e: e.tensor_scalar_mul(s["P2T"][0:64, 1, :], ps[1][0:64, 0:DKH], -1.0), "eV")
                p.wait("tensor", "eV", ev)
                p.op("tensor", mm(pS[:, 0:DVH], bkr[:, hh, 0:128], s["W1"][:, hh, :], True, False))
                p.op("tensor", mm(pS[:, 0:DVH], s["P2T"][:, 0, 0:128], s["S1b"][:, hh, :], False, False))
                et = p.op("tensor", mm(pS[:, 0:DVH], s["P2T"][0:64, 1, 0:128], s["S2b"][0:64, hh, :], False, True), "eT")
                p.op("tensor", mm(ps[3][0:64, 0:DVH], bkr[:, hh, 128:192], s["W1"][:, hh, :], True, False))
                p.op("tensor", mm(ps[3][0:64, 0:DVH], s["P2T"][:, 0, 128:192], s["S1b"][:, hh, :], False, False))
                et = p.op("tensor", mm(ps[3][0:64, 0:DVH], s["P2T"][0:64, 1, 128:192], s["S2b"][0:64, hh, :], False, True), "eT")
                p.wait("vector", "eT", et)
                ev = p.op("vector", lambda e, hh=hh, cc=cc: e.scalar_tensor_tensor(
                    s["S1"][:, hh, :], s["S1"][:, hh, :], s["gcols"][:, cc, 16 + hh:17 + hh],
                    pS[:, 0:DVH], ALU.mult, ALU.add), "eV")
                ev = p.op("vector", lambda e, hh=hh, cc=cc: e.scalar_tensor_tensor(
                    s["S2"][0:64, hh, :], s["S2"][0:64, hh, :], s["gcols"][0:64, cc, 16 + hh:17 + hh],
                    ps[3][0:64, 0:DVH], ALU.mult, ALU.add), "eV")
                p.op("vector", lambda e, hh=hh: e.tensor_copy(s["S1b"][:, hh, :], s["S1"][:, hh, :]))
                ev = p.op("vector", lambda e, hh=hh: e.tensor_copy(s["S2b"][0:64, hh, :], s["S2"][0:64, hh, :]), "eV")
                p.wait("tensor", "eV", ev)
            p.wait("gpsimd", "eV", ev)
            dgo = p.op("gpsimd", lambda e, grp=grp, cc=cc: e.dma_start(
                out=og_sc[grp, cc * 128:(cc + 1) * 128, :], in_=s["og"][:, :]), "dG")
            p.wait("vector", "dG", dgo)
            d_lastuse[par] = (p.cnt["eT"], p.cnt["eV"])
            if cc == 3:
                d_og0 = p.cnt["dG"]
            if p1_stage >= 'e' and d_chunks == 8:
                if cc == 3:
                    e_pre(0, d_og0)
                elif cc == 4:
                    e_wso(0, [0, 1])
                elif cc == 5:
                    e_wso(0, [2, 3, 4])
                elif cc == 6:
                    e_wso(0, [5, 6, 7])
        d_og = p.cnt["dG"]

        if p1_stage < 'e':
            continue
        e_wout(0)
        e_pre(1, d_og)
        e_wso(1, list(range(8)))
        e_wout(1)

        if grp == p1_groups - 1:
            q_flush()


    p.join_all()

    if p2_groups > 0:
        dg = None
        for cb in range(32):
            dg = p.op("gpsimd", lambda e, cb=cb: e.dma_start(
                out=s["t2T"][:, cb, :], in_=t2T_sc[cb * 128:(cb + 1) * 128, :]), "dG")
        p.wait("tensor", "dG", dg)

    p2st = {"d_xqT": {}, "d_hs": {}, "d_hq": {}, "et_g_end": {}, "f_ev": {},
            "f_es": {}, "h_ev": 0, "h_es": 0, "et_i": 0, "ev_i": 0, "last_q": {},
            "et_conv": {}, "et_oct": {}, "dg_oct": 0}

    def p2_coord(grp):
        b, gi = divmod(grp, 2)
        return b, gi * G1

    def f_tiles(grp):
        b, t0 = p2_coord(grp)
        nh = 0 if grp % 2 == 1 else 2
        ntok = G1 + nh
        buf = grp % 2
        lc = s["p2l"]
        units = []

        def tile(tt):
            r0 = t0 + tt * 128
            rr = min(128, ntok - tt * 128)
            p.wait("sync", "eV", p2st["f_ev"].get(buf, 0))
            p.wait("sync", "eS", p2st["f_es"].get(buf, 0))
            ds = p.op("sync", lambda e: e.dma_start(
                out=s["p2x"][0:rr, buf, :], in_=x2_sc[b, r0:r0 + rr, :]), "dS")
            p.wait("scalar", "dS", ds)
            p.wait("vector", "dS", ds)
            layernorm_tile(s["p2x"][:, buf, :], rr, None, lc, extra_minmax=True, scratch=s["p2m"])
            p.op("vector", lambda e: e.tensor_sub(lc[0:rr, 12:13], lc[0:rr, 8:9], lc[0:rr, 3:4]))
            p.op("vector", lambda e: e.tensor_sub(lc[0:rr, 13:14], lc[0:rr, 3:4], lc[0:rr, 9:10]))
            p.op("vector", lambda e: e.tensor_max(lc[0:rr, 12:13], lc[0:rr, 12:13], lc[0:rr, 13:14]))
            p.op("vector", lambda e: e.tensor_mul(lc[0:rr, 12:13], lc[0:rr, 12:13], lc[0:rr, 7:8]))
            p.op("vector", lambda e: e.tensor_scalar(lc[0:rr, 12:13], lc[0:rr, 12:13], 1e-5, None, ALU.max))
            p.op("vector", lambda e: e.reciprocal(lc[0:rr, 14:15], lc[0:rr, 12:13]))
            p.op("vector", lambda e: e.tensor_scalar_mul(lc[0:rr, 14:15], lc[0:rr, 14:15], 127.0))
            p.op("vector", lambda e: e.tensor_scalar_mul(lc[0:rr, 15:16], lc[0:rr, 12:13], 1.0 / 127.0))
            p.op("vector", lambda e: e.tensor_mul(lc[0:rr, 12:13], lc[0:rr, 7:8], lc[0:rr, 14:15]))
            p.op("vector", lambda e: e.tensor_mul(lc[0:rr, 13:14], lc[0:rr, 3:4], lc[0:rr, 12:13]))
            ev = p.op("vector", lambda e: e.tensor_tensor(
                lc[0:rr, 13:14], s["cols"][0:rr, CL_QB:CL_QB + 1], lc[0:rr, 13:14], ALU.subtract), "eV")
            p.wait("scalar", "eV", ev)
            es = p.op("scalar", lambda e: e.activation(
                s["p2m"][0:rr, :], s["p2x"][0:rr, buf, :], AF.Identity,
                bias=lc[0:rr, 13:14], scale=lc[0:rr, 12:13]), "eS")
            p2st["f_es"][buf] = es
            p.wait("vector", "eS", es)
            p.op("vector", lambda e: e.tensor_copy(s["i16q"][0:rr, :], s["p2m"][0:rr, :]))
            for half in range(2):
                ev = p.op("vector", lambda e, half=half: e.tensor_scalar(
                    s["fq"][0:rr, :], s["i16q"][0:rr, half * 512:(half + 1) * 512],
                    -128.0, lc[0:rr, 15:16], ALU.add, ALU.mult), "eV")
                p.wait("sync", "eV", ev)
                ds = p.op("sync", lambda e, half=half: e.dma_start(
                    out=xq1_sc[b, 64 + r0:64 + r0 + rr, half * 512:(half + 1) * 512],
                    in_=s["fq"][0:rr, :]), "dS")
                p.wait("vector", "dS", ds)
            p2st["f_ev"][buf] = ev
            p2st["last_q"][grp] = p.cnt["dS"]
        for tt in range((ntok + 127) // 128):
            units.append(lambda tt=tt: tile(tt))
        return units

    def f_xqT(grp):
        b, t0 = p2_coord(grp)
        p.wait("sync", "dS", p2st["last_q"][grp])
        p.wait("sync", "eT", p2st["et_g_end"].get(grp - 1, 0))
        for dt_i in range(8):
            p.op("sync", lambda e, dt_i=dt_i: e.dma_start_transpose(
                out=s["xqT"][:, dt_i, :],
                in_=xq1_sc[b, t0:t0 + 1152, dt_i * 128:(dt_i + 1) * 128]), "dS")
        p2st["d_xqT"][grp] = p.cnt["dS"]

    def g_units(grp):
        b, t0 = p2_coord(grp)
        units = []
        gst = p2st

        def oct_load(oct):
            op_ = oct % 2
            p.wait("gpsimd", "eT", gst["et_oct"].get(op_, 0))
            for dt_i in range(8):
                p.op("gpsimd", lambda e, oct=oct, dt_i=dt_i, op_=op_: e.dma_start(
                    out=s["t1T"][:, op_, dt_i, :],
                    in_=t1T_sc[dt_i * 128:(dt_i + 1) * 128, oct * 1024:(oct + 1) * 1024]), "dG")
            gst["dg_oct"] = p.cnt["dG"]

        def ct_unit(oct, ctl):
            op_ = oct % 2
            ct = oct * 8 + ctl
            cp = ct % 2
            if ctl == 0:
                p.wait("tensor", "dG", gst["dg_oct"])
                p.wait("tensor", "dS", p2st["d_xqT"][grp])
            p.wait("sync", "eT", gst["et_conv"].get(cp, 0))
            dd = None
            for j in range(3):
                dd = p.op("sync", lambda e, ct=ct, j=j, cp=cp: e.dma_start(
                    out=s["dgf"][:, cp * 3 + j, :],
                    in_=dgf_sc[(ct * 3 + j) * 128:(ct * 3 + j + 1) * 128, :]), "dS")
            for dt_i in range(8):
                p.op("tensor", mm(ps[0][:, :], s["t1T"][:, op_, dt_i, ctl * 128:ctl * 128 + 128],
                                  s["xqT"][:, dt_i, 63:575], dt_i == 0, dt_i == 7))
            for dt_i in range(8):
                p.op("tensor", mm(ps[1][:, :], s["t1T"][:, op_, dt_i, ctl * 128:ctl * 128 + 128],
                                  s["xqT"][:, dt_i, 575:1087], dt_i == 0, dt_i == 7))
            et = None
            for dt_i in range(8):
                et = p.op("tensor", mm(ps[2][:, 0:3], s["t1T"][:, op_, dt_i, ctl * 128:ctl * 128 + 128],
                                       s["xqT"][:, dt_i, 1087:1090], dt_i == 0, dt_i == 7),
                          "eT" if dt_i == 7 else None)
            p.wait("vector", "eT", et)
            p.op("vector", lambda e, cp=cp: e.tensor_copy(s["hh"][:, cp, 0:512], ps[0][:, :]))
            p.op("vector", lambda e, cp=cp: e.tensor_copy(s["hh"][:, cp, 512:1024], ps[1][:, :]))
            ev = p.op("vector", lambda e, cp=cp: e.tensor_copy(s["hh"][:, cp, 1024:1027], ps[2][:, 0:3]), "eV")
            p.wait("tensor", "eV", ev)
            p.wait("tensor", "dS", dd)
            for half in range(2):
                et = None
                for j in range(3):
                    et = p.op("tensor", mm(
                        ps[3][:, :], s["dgf"][:, cp * 3 + j, :],
                        s["hh"][:, cp, half * 512 + j:half * 512 + j + 512], j == 0, j == 2),
                        "eT" if j == 2 else None)
                p.wait("scalar", "eT", et)
                es = p.op("scalar", lambda e, ct=ct: e.activation(
                    s["p2q"][:, 0:512], ps[3][:, :], AF.Silu,
                    bias=s["fcb"][:, ct:ct + 1], scale=s["cols"][:, CL_ISW1:CL_ISW1 + 1]), "eS")
                p.wait("sync", "eS", es)
                p.wait("tensor", "eS", es)
                ds = p.op("sync", lambda e, b=b, ct=ct, t0=t0, half=half: e.dma_start(
                    out=hs_sc[b, ct * 128:(ct + 1) * 128, t0 + half * 512:t0 + (half + 1) * 512],
                    in_=s["p2q"][:, 0:512]), "dS")
                p.wait("scalar", "dS", ds)
            gst["et_conv"][cp] = p.cnt["eT"]
            if ctl == 7:
                gst["et_oct"][op_] = p.cnt["eT"]
            if oct == 3 and ctl == 7:
                p2st["d_hs"][grp] = p.cnt["dS"]
                p2st["et_g_end"][grp] = p.cnt["eT"]
        for oct in range(4):
            units.append(lambda oct=oct: oct_load(oct))
            for ctl in range(8):
                units.append(lambda oct=oct, ctl=ctl: ct_unit(oct, ctl))
        return units

    def h_units(grp):
        b, t0 = p2_coord(grp)
        lc = s["p2l"]
        c2b = 16 + 8 * (grp % 2)
        units = []

        def tile(tt):
            p.wait("sync", "dS", p2st["d_hs"][grp])
            p.wait("sync", "eV", p2st["h_ev"])
            p.wait("sync", "eS", p2st["h_es"])
            ds = p.op("sync", lambda e: e.dma_start_transpose(
                out=s["hst"][:, 0, :], in_=hs_sc[b, :, t0 + tt * 128:t0 + (tt + 1) * 128]), "dS")
            p.wait("vector", "dS", ds)
            p.op("vector", lambda e: e.scalar_tensor_tensor(
                s["qsc"][:, 0:2048], s["hst"][:, 0, 0:2048], -1.0,
                s["hst"][:, 0, 0:2048], ALU.mult, ALU.max))
            p.op("vector", lambda e: e.scalar_tensor_tensor(
                s["hqi"][:, 0:2048], s["hst"][:, 0, 2048:4096], -1.0,
                s["hst"][:, 0, 2048:4096], ALU.mult, ALU.max))
            p.op("vector", lambda e: e.tensor_max(
                s["qsc"][:, 0:2048], s["qsc"][:, 0:2048], s["hqi"][:, 0:2048]))
            p.op("vector", lambda e: e.tensor_max(
                s["qsc"][:, 0:1024], s["qsc"][:, 0:1024], s["qsc"][:, 1024:2048]))
            p.op("vector", lambda e: e.tensor_max(
                s["qsc"][:, 0:512], s["qsc"][:, 0:512], s["qsc"][:, 512:1024]))
            p.op("vector", lambda e: e.reduce_max(lc[:, 32:33], s["qsc"][:, 0:512], axis=AX.X, apply_absolute_value=True))
            p.op("vector", lambda e: e.tensor_scalar(lc[:, 32:33], lc[:, 32:33], 1e-5, None, ALU.max))
            p.op("vector", lambda e: e.reciprocal(lc[:, 33:34], lc[:, 32:33]))
            p.op("vector", lambda e: e.tensor_scalar_mul(lc[:, 34:35], lc[:, 33:34], 127.0))
            p.op("vector", lambda e: e.tensor_scalar_mul(lc[:, 35:36], lc[:, 32:33], 1.0 / 127.0))
            ev = p.op("vector", lambda e: e.tensor_tensor(
                lc[:, c2b + tt:c2b + tt + 1], lc[:, 35:36], s["cols"][:, CL_ISW2:CL_ISW2 + 1], ALU.mult), "eV")
            p.wait("scalar", "eV", ev)
            for qq in range(4):
                if qq > 0:
                    p.wait("scalar", "eV", ev)
                es = p.op("scalar", lambda e, qq=qq: e.activation(
                    s["p2n"][:, :], s["hst"][:, 0, qq * 1024:(qq + 1) * 1024], AF.Identity,
                    bias=s["cols"][:, CL_QB:CL_QB + 1], scale=lc[:, 34:35]), "eS")
                p2st["h_es"] = es
                p.wait("vector", "eS", es)
                p.op("vector", lambda e: e.tensor_copy(s["i16t"][:, :], s["p2n"][:, :]))
                ev = p.op("vector", lambda e, qq=qq: e.tensor_scalar(
                    s["hqi"][:, qq * 1024:(qq + 1) * 1024], s["i16t"][:, :],
                    -128.0, None, ALU.add), "eV")
            p.wait("sync", "eV", ev)
            ds = p.op("sync", lambda e: e.dma_start(
                out=hq_sc[b, t0 + tt * 128:t0 + (tt + 1) * 128, :], in_=s["hqi"][:, :]), "dS")
            p.wait("vector", "dS", ds)
            p.wait("scalar", "dS", ds)
            p2st["h_ev"] = ev
            if tt == 7:
                p2st["d_hq"][grp] = p.cnt["dS"]
        for tt in range(8):
            units.append(lambda tt=tt: tile(tt))
        return units

    def i_units(grp):
        b, t0 = p2_coord(grp)
        lci = s["lni"]
        c2b = 16 + 8 * (grp % 2)
        xbuf = (grp + 1) % 2
        units = []

        def tp_unit(tp):
            if tp == 0:
                p.wait("scalar", "eV", p2st["h_ev"])
                p.wait("sync", "eV", p2st["f_ev"].get(xbuf, 0))
                p.wait("sync", "eS", p2st["f_es"].get(xbuf, 0))
            p.wait("scalar", "dS", p2st["d_hq"][grp])
            p.wait("scalar", "eT", p2st["et_i"])
            da = None
            for cb in range(32):
                da = p.op("scalar", lambda e, tp=tp, cb=cb: e.dma_start_transpose(
                    out=s["hqT"][:, cb, :],
                    in_=hq_sc[b, t0 + tp * 256:t0 + (tp + 1) * 256, cb * 128:(cb + 1) * 128]), "dA")
            p.wait("tensor", "dA", da)
            for two in range(2):
                tt = tp * 2 + two
                rbase = t0 + tt * 128
                p.wait("tensor", "eV", p2st["ev_i"])
                et = None
                for nhf in range(2):
                    for cb in range(32):
                        et = p.op("tensor", mm(
                            ps[4 + nhf][:, :],
                            s["hqT"][:, cb, two * 128:(two + 1) * 128],
                            s["t2T"][:, cb, nhf * 512:(nhf + 1) * 512],
                            cb == 0, cb == 31), "eT" if cb == 31 else None)
                p2st["et_i"] = et
                p.wait("vector", "eT", et)
                p.op("vector", lambda e, tt=tt: e.tensor_scalar(
                    s["y2"][:, 0:512], ps[4][:, :], s["p2l"][:, c2b + tt:c2b + tt + 1], None, ALU.mult))
                ev = p.op("vector", lambda e, tt=tt: e.tensor_scalar(
                    s["y2"][:, 512:1024], ps[5][:, :], s["p2l"][:, c2b + tt:c2b + tt + 1], None, ALU.mult), "eV")
                p2st["ev_i"] = ev
                ds = p.op("sync", lambda e, rbase=rbase: e.dma_start(
                    out=s["p2x"][:, xbuf, :], in_=x2_sc[b, rbase:rbase + 128, :]), "dS")
                p.wait("scalar", "eV", ev)
                p.wait("vector", "eV", ev)
                layernorm_tile(s["y2"][:, :], 128, None, lci, scratch=s["p2n"])
                p.wait("vector", "dS", ds)
                p.op("vector", lambda e: e.tensor_scalar(
                    s["y2"][:, :], s["y2"][:, :], lci[:, 3:4], lci[:, 7:8],
                    ALU.subtract, ALU.mult))
                ev = p.op("vector", lambda e: e.tensor_add(s["y2"][:, :], s["y2"][:, :], s["p2x"][:, xbuf, :]), "eV")
                p.wait("sync", "eV", ev)
                ds = p.op("sync", lambda e, rbase=rbase: e.dma_start(
                    out=out_d[b, rbase:rbase + 128, :], in_=s["y2"][:, :]), "dS")
                p.wait("vector", "dS", ds)
                p.wait("scalar", "dS", ds)
                p.wait("tensor", "dS", ds)
            p2st["f_ev"][xbuf] = p.cnt["eV"]
        for tp in range(4):
            units.append(lambda tp=tp: tp_unit(tp))
        return units

    def emit_interleaved(main, partner, ratio=3):
        mi = pi = 0
        while mi < len(main) or pi < len(partner):
            for _ in range(ratio):
                if mi < len(main):
                    main[mi]()
                    mi += 1
            if pi < len(partner):
                partner[pi]()
                pi += 1

    if p2_groups > 0:
        for u in f_tiles(0):
            u()
        f_xqT(0)
        nxt = f_tiles(1) if p2_groups > 1 else []
        emit_interleaved(g_units(0), nxt, ratio=4)
    for grp in range(p2_groups):
        partner = h_units(grp) + i_units(grp)
        if grp + 1 < p2_groups:
            f_xqT(grp + 1)
            main = g_units(grp + 1)
            fnext = f_tiles(grp + 2) if grp + 2 < p2_groups else []
            merged = []
            fi = 0
            for k, u in enumerate(main):
                merged.append(u)
                if k % 5 == 4 and fi < len(fnext):
                    merged.append(fnext[fi])
                    fi += 1
            merged.extend(fnext[fi:])
            emit_interleaved(merged, partner, ratio=3)
        else:
            for u in partner:
                u()

    p.wait("sync", "dS", p.cnt["dS"])
    p.wait("gpsimd", "dG", p.cnt["dG"])
    p.wait("scalar", "dA", p.cnt["dA"])

    with nc.Block() as block:
        p.replay(block)
    return nc


_NC_CACHE = [None]


def kernel(**inputs):
    if _NC_CACHE[0] is None:
        _NC_CACHE[0] = build_program()
    nc = _NC_CACHE[0]
    x = np.ascontiguousarray(np.asarray(inputs["L0_attn_x"], dtype=np.float32))
    wmap = {
        "W_in": "L0_attn_W_in", "conv_x_w": "L0_attn_conv_x_w",
        "conv_x_b": "L0_attn_conv_x_b", "conv_z_w": "L0_attn_conv_z_w",
        "conv_z_b": "L0_attn_conv_z_b", "W_q": "L0_attn_W_q",
        "W_k": "L0_attn_W_k", "W_v": "L0_attn_W_v", "W_g": "L0_attn_W_g",
        "W_gk": "L0_attn_W_gk", "W_b": "L0_attn_W_b", "b_b": "L0_attn_b_b",
        "A_log": "L0_attn_A_log", "dt_bias": "L0_attn_dt_bias",
        "g_norm_w": "L0_attn_g_norm_w", "W_so": "L0_attn_W_so",
        "W_out": "L0_attn_W_out", "w1": "L0_ffn_w1", "conv_w": "L0_ffn_conv_w",
        "conv_b": "L0_ffn_conv_b", "w2": "L0_ffn_w2",
    }
    weights = {k: np.ascontiguousarray(np.asarray(inputs[v], dtype=np.float32))
               for k, v in wmap.items()}
    in_maps = []
    for i in range(8):
        m = {"x": x[2 * i:2 * i + 2]}
        m.update(weights)
        in_maps.append(m)
    res = run_bass_kernel_spmd(nc, in_maps, core_ids=list(range(8)))
    out = np.concatenate([r["out"] for r in res.results], axis=0)
    return out.reshape(16, L, D).astype(np.float32)


if __name__ == "__main__":
    nc = build_program()
    print("built:", nc.n_instructions(), "instructions")
